# version 24
# baseline (speedup 1.0000x reference)
import math
import numpy as np
import ml_dtypes
from contextlib import ExitStack

import concourse.bass as bass
import concourse.mybir as mybir
from concourse.bass_utils import run_bass_kernel_spmd

F32 = mybir.dt.float32
BF16 = mybir.dt.bfloat16
AF = mybir.ActivationFunctionType
ALU = mybir.AluOpType
AX = mybir.AxisListType

B, T, H = 4, 4096, 2048
NH, HS = 16, 128
OWN = T // 2
NTB = T // 128
NCB = H // 128
FFH = 5120
NFB = FFH // 128
SCALE = 1.0 / math.sqrt(HS)
W = 256
NCH = OWN // W


def _bf(x):
    return np.ascontiguousarray(np.asarray(x, np.float32).astype(ml_dtypes.bfloat16))


def _host_layout(inputs):
    x = np.ascontiguousarray(np.asarray(inputs["L0_attn_x"], np.float32))
    Wq = np.asarray(inputs["L0_attn_Wq"], np.float32)
    bq = np.asarray(inputs["L0_attn_bq"], np.float32)
    Wk = np.asarray(inputs["L0_attn_Wk"], np.float32)
    bk = np.asarray(inputs["L0_attn_bk"], np.float32)
    Wv = np.ascontiguousarray(np.asarray(inputs["L0_attn_Wv"], np.float32))
    bv = np.asarray(inputs["L0_attn_bv"], np.float32)
    Wo = np.ascontiguousarray(np.asarray(inputs["L0_attn_Wo"], np.float32))
    bo = np.asarray(inputs["L0_attn_bo"], np.float32)
    wfc = np.asarray(inputs["L0_ffn_w_fc"], np.float32)
    wpr = np.asarray(inputs["L0_ffn_w_proj"], np.float32)

    perm = np.concatenate([np.concatenate([np.arange(h*HS, (h+1)*HS, 2),
                                           np.arange(h*HS+1, (h+1)*HS, 2)])
                           for h in range(NH)])
    Wq_p = np.ascontiguousarray(Wq[:, perm])
    Wk_p = np.ascontiguousarray(Wk[:, perm])
    bqc = np.ascontiguousarray(bq[perm].reshape(NH, 128).T)
    bkc = np.ascontiguousarray(bk[perm].reshape(NH, 128).T)
    bvr = np.ascontiguousarray(bv.reshape(1, H))
    bor = np.ascontiguousarray(bo.reshape(1, H))
    wfcT = np.ascontiguousarray(wfc.T)
    wprT = np.ascontiguousarray(wpr.T)
    wvr = np.ascontiguousarray(Wv.reshape(16, 128, 8, 256).transpose(2, 1, 0, 3)
                               .reshape(8, 128, 4096))
    wfc3 = np.ascontiguousarray(wfc.reshape(40, 128, 16, 128).transpose(0, 3, 2, 1)
                                .reshape(40, 128, 2048))

    t = np.arange(T, dtype=np.float32)
    inv = 1.0 / 10000.0 ** (np.arange(0, HS, 2, dtype=np.float32) / HS)
    ang = np.outer(inv, t)
    cosg, sing = np.cos(ang), np.sin(ang)

    kl = np.arange(128)[:, None]
    ql = np.arange(128)[None, :]
    mA = (kl <= ql).astype(np.float32)
    mB_even = (kl < ql).astype(np.float32)

    in_maps, gidx = [], []
    for c in range(8):
        b, p = c // 2, c % 2
        own = 2 * np.arange(OWN) + p
        other = 2 * np.arange(OWN) + (1 - p)
        g = np.concatenate([own, other])
        gidx.append(own)
        cosp, sinp = cosg[:, g], sing[:, g]
        in_maps.append({
            "xp": np.ascontiguousarray(x[b][g]),
            "wq": Wq_p, "wk": Wk_p, "wvr": wvr, "wo": Wo,
            "bqc": bqc, "bkc": bkc, "bvr": bvr, "bor": bor,
            "cos2": _bf(np.concatenate([cosp, cosp], 0)),
            "sin2s": _bf(np.concatenate([-sinp, sinp], 0)),
            "mka": _bf(mA),
            "mkb": _bf(mB_even if p == 0 else mA),
            "wfcT": wfcT, "wprT": wprT, "wfc3": wfc3,
        })
    return in_maps, gidx


def build_nc():
    nc = bass.Bass("TRN2", target_bir_lowering=False)
    d = {}
    for nm, shp, dt in [("xp", [T, H], F32), ("wq", [H, H], F32), ("wk", [H, H], F32),
                        ("wvr", [8, 128, 4096], F32), ("wo", [H, H], F32),
                        ("bqc", [128, NH], F32), ("bkc", [128, NH], F32),
                        ("bvr", [1, H], F32), ("bor", [1, H], F32),
                        ("cos2", [128, T], BF16), ("sin2s", [128, T], BF16),
                        ("mka", [128, 128], BF16), ("mkb", [128, 128], BF16),
                        ("wfcT", [H, FFH], F32), ("wprT", [FFH, H], F32),
                        ("wfc3", [NFB, 128, NCB * 128], F32)]:
        d[nm] = nc.dram_tensor(nm, shp, dt, kind="ExternalInput").ap()
    d["out"] = nc.dram_tensor("out", [OWN, H], F32, kind="ExternalOutput").ap()
    d["xn_scr"] = nc.dram_tensor("xn_scr", [T, H], BF16, kind="Internal").ap()
    d["v_scr"] = nc.dram_tensor("v_scr", [T, H], BF16, kind="Internal").ap()
    d["o_scr"] = nc.dram_tensor("o_scr", [H, OWN], BF16, kind="Internal").ap()
    d["hT_scr"] = nc.dram_tensor("hT_scr", [NCH, 128, NFB, 256], BF16, kind="Internal").ap()
    d["res_scr"] = nc.dram_tensor("res_scr", [OWN, H], F32, kind="Internal").ap()
    d["qpr_scr"] = nc.dram_tensor("qpr_scr", [4, 128, NFB, 512], BF16, kind="Internal").ap()

    streams = {"sync": [], "gpsimd": [], "vector": [], "scalar": [], "tensor": []}

    def em(name, fn):
        streams[name].append(fn)

    class Ctr:
        def __init__(self, sem, step=1):
            self.sem, self.step, self.n = sem, step, 0
        def inc(self):
            self.n += self.step
            return self.n

    with ExitStack() as S0:
        E0 = S0.enter_context
        psum = E0(nc.psum_tensor("psum", [128, 4096], F32))
        ones_cb = E0(nc.sbuf_tensor("ones_cb", [128, 128], BF16))
        ones_cf = E0(nc.sbuf_tensor("ones_cf", [128, 1], F32))
        ones_rf = E0(nc.sbuf_tensor("ones_rf", [1, 128], F32))
        ones_rb = E0(nc.sbuf_tensor("ones_rb", [1, 128], BF16))
        bqc_s = E0(nc.sbuf_tensor("bqc_s", [128, NH], F32))
        bkc_s = E0(nc.sbuf_tensor("bkc_s", [128, NH], F32))
        tcols = E0(nc.sbuf_tensor("tcols", [128, 6], F32))
        xnT_stack = ExitStack()
        xnT = xnT_stack.enter_context(nc.sbuf_tensor("xnT", [128, NCB * T], BF16, side="right", is_transpose_dest=True))
        mka_s = E0(nc.sbuf_tensor("mka_s", [128, 128], BF16))
        mkb_s = E0(nc.sbuf_tensor("mkb_s", [128, 128], BF16))
        am11 = E0(nc.sbuf_tensor("am11", [1, 8], F32))
        wqk_stack = ExitStack()
        wqk_q = wqk_stack.enter_context(nc.sbuf_tensor("wqk_q", [128, H], BF16, side="right"))
        wqk_k = wqk_stack.enter_context(nc.sbuf_tensor("wqk_k", [128, H], BF16, side="right"))
        eps_c = E0(nc.sbuf_tensor("eps_c", [128, 1], F32))
        eps2_c = E0(nc.sbuf_tensor("eps2_c", [128, 1], F32))
        lnh_c = E0(nc.sbuf_tensor("lnh_c", [128, 1], F32))

        sem_v = E0(nc.semaphore("sv"))
        sem_s = E0(nc.semaphore("ss"))
        sem_t = E0(nc.semaphore("st"))
        sv, ss, st = Ctr(sem_v), Ctr(sem_s), Ctr(sem_t)

        ALL_DSEMS = []

        def dsem(name):
            c = Ctr(E0(nc.semaphore(name)), 16)
            ALL_DSEMS.append(c)
            return c

        sx = [dsem(f"sx{i}") for i in range(3)]
        sxn = [dsem("sxn0"), dsem("sxn1")]
        scst = dsem("scst")
        sbv = dsem("sbv")
        sxnt = [dsem(f"sxnt{i}") for i in range(4)]
        swv = [dsem(f"swv{i}") for i in range(3)]
        svst = [dsem(f"svst{i}") for i in range(4)]
        stab = dsem("stab")
        swq = dsem("swq")
        swk = dsem("swk")
        sstat = [dsem("sstat0"), dsem("sstat1"), dsem("sstat2"), dsem("sstat3")]
        svh = dsem("svh")
        ssw = dsem("ssw")
        sost = dsem("sost")
        swfc = [dsem("swfc0"), dsem("swfc1")]
        shT = [dsem("shT0"), dsem("shT1")]
        swpr = [dsem("swpr0"), dsem("swpr1")]
        shtc = [dsem("shtc0"), dsem("shtc1")]
        sxe = [dsem("sxe0"), dsem("sxe1")]
        srst = [dsem("srst0"), dsem("srst1")]
        swo = [dsem(f"swo{i}") for i in range(4)]
        sbo = dsem("sbo")
        sot = [dsem("sot0"), dsem("sot1")]
        srf = [dsem("srf0"), dsem("srf1")]
        sdo = [dsem("sdo0"), dsem("sdo1")]

        def barrier():
            vals = tuple((c.sem, c.n) for c in (ALL_DSEMS + [sv, ss, st]) if c.n)
            for e in ("sync", "gpsimd", "vector", "scalar", "tensor"):
                def f(eng, vals=vals):
                    for s, v in vals:
                        eng.wait_ge(s, v)
                em(e, f)

        def f(v):
            v.memset(ones_cb[:], 1.0)
            v.memset(ones_cf[:], 1.0)
            v.memset(ones_rf[:], 1.0)
            v.memset(ones_rb[:], 1.0)
            v.memset(eps_c[:], 1e-5)
            v.memset(eps2_c[:], 41.94304)
            v.memset(lnh_c[:], 7.6246189861593985).then_inc(sem_v, 1)
        em("vector", f)
        const_v = sv.inc()
        def f(sy):
            sy.dma_start(out=bqc_s[:], in_=d["bqc"]).then_inc(scst.sem, 16)
            sy.dma_start(out=bkc_s[:], in_=d["bkc"]).then_inc(scst.sem, 16)
            sy.dma_start(out=mka_s[:], in_=d["mka"]).then_inc(scst.sem, 16)
            sy.dma_start(out=mkb_s[:], in_=d["mkb"]).then_inc(scst.sem, 16)
        em("sync", f)
        for _ in range(4):
            scst.inc()

        with ExitStack() as S1:
            E1 = S1.enter_context
            x_t = [E1(nc.sbuf_tensor(f"x_t{i}", [128, H], F32)) for i in range(3)]
            xn_t = [E1(nc.sbuf_tensor(f"xn_t{i}", [128, H], BF16)) for i in range(2)]
            sums = E1(nc.sbuf_tensor("sums", [128, NTB], F32))
            sumsq = E1(nc.sbuf_tensor("sumsq", [128, NTB], F32))
            lnv = E1(nc.sbuf_tensor("lnv", [128, NTB], F32))
            rstd = E1(nc.sbuf_tensor("rstd", [128, NTB], F32))
            negmr = E1(nc.sbuf_tensor("negmr", [128, NTB], F32))
            bvb = E1(nc.sbuf_tensor("bvb", [128, H], BF16))
            bvr_s = E1(nc.sbuf_tensor("bvr_s", [1, H], BF16))
            wv_sl = [E1(nc.sbuf_tensor(f"wv_sl{i}", [128, NCB * 256], BF16)) for i in range(3)]
            v_ev = [E1(nc.sbuf_tensor(f"v_ev{i}", [128, 256], BF16)) for i in range(4)]

            def f(g):
                g.dma_start(out=bvr_s[:], in_=d["bvr"]).then_inc(sbv.sem, 16)
            em("gpsimd", f)
            bvl = sbv.inc()
            def f(t, bvl=bvl, const_v=const_v):
                t.wait_ge(sbv.sem, bvl)
                t.wait_ge(sem_v, const_v)
                for j in range(4):
                    ins = t.matmul(psum[:, 2048+j*512:2048+(j+1)*512], ones_rb[0:1, :],
                                   bvr_s[0:1, j*512:(j+1)*512], start=True, stop=True)
                ins.then_inc(sem_t, 1)
            em("tensor", f); bvt = st.inc()
            def f(v, bvt=bvt):
                v.wait_ge(sem_t, bvt)
                v.tensor_copy(bvb[:], psum[:, 2048:4096]).then_inc(sem_v, 1)
            em("vector", f); bvb_ready = sv.inc()

            wv_load, wv_last_mm, vproj_mm, vproj_ev, v_store = [], [], [], [], []
            def plan_wv_load(li):
                dch = li % 8
                buf = wv_sl[li % 3]
                war = wv_last_mm[li - 3] if li >= 3 else None
                swvc = swv[li % 3]
                def f(g, dch=dch, buf=buf, war=war, swvc=swvc):
                    if war is not None:
                        g.wait_ge(sem_t, war)
                    g.dma_start(out=buf[:], in_=d["wvr"][dch]).then_inc(swvc.sem, 16)
                em("gpsimd", f)
                wv_load.append((swvc.sem, swvc.inc()))
            plan_wv_load(0)
            plan_wv_load(1)
            plan_wv_load(2)

            def plan_v_mm(idx):
                half, rem = idx // 128, idx % 128
                dch, tb = rem // 16, 16 * (idx // 128) + rem % 16
                li = idx // 16
                bank = (idx % 4) * 512
                lw = wv_load[li]
                qrd = xnq_ready[tb // 8]
                war = vproj_ev[idx - 4] if idx >= 4 else None
                def f(t, li=li, tb=tb, bank=bank, lw=lw, qrd=qrd, war=war):
                    t.wait_ge(*lw)
                    t.wait_ge(*qrd)
                    if war is not None:
                        t.wait_ge(sem_v, war)
                    for cb in range(NCB):
                        ins = t.matmul(psum[:, bank:bank+256],
                                       xnT[:, cb*T + tb*128: cb*T + (tb+1)*128],
                                       wv_sl[li % 3][:, cb*256:(cb+1)*256],
                                       start=(cb == 0), stop=(cb == NCB-1))
                    ins.then_inc(sem_t, 1)
                em("tensor", f)
                vproj_mm.append(st.inc())
                if idx % 16 == 15:
                    wv_last_mm.append(vproj_mm[-1])

            def plan_v_ev(idx):
                dch = (idx % 128) // 16
                bank = (idx % 4) * 512
                mmw = vproj_mm[idx]
                war = v_store[idx - 4] if idx >= 4 else None
                svc = svst[idx % 4]
                def f(v, dch=dch, bank=bank, idx=idx, mmw=mmw, war=war, svc=svc):
                    v.wait_ge(sem_t, mmw)
                    v.wait_ge(sem_v, bvb_ready)
                    if war is not None:
                        v.wait_ge(svc.sem, war)
                    v.tensor_tensor(v_ev[idx % 4][:], psum[:, bank:bank+256],
                                    bvb[:, dch*256:(dch+1)*256], ALU.add).then_inc(sem_v, 1)
                em("vector", f)
                vproj_ev.append(sv.inc())

            def plan_v_st(idx):
                dch = (idx % 128) // 16
                tb = 16 * (idx // 128) + idx % 16
                evw = vproj_ev[idx]
                svc = svst[idx % 4]
                def f(g, dch=dch, tb=tb, idx=idx, evw=evw, svc=svc):
                    g.wait_ge(sem_v, evw)
                    g.dma_start(out=d["v_scr"][tb*128:(tb+1)*128, dch*256:(dch+1)*256],
                                in_=v_ev[idx % 4][:]).then_inc(svc.sem, 16)
                em("gpsimd", f)
                v_store.append(svc.inc())

            vp = {"ld": 3, "mm": 0, "ev": 0, "st": 0}
            NV = 256
            def plan_v(tb):
                qdone = min((tb + 1) // 8, 4)
                tmm = max(0, int((tb - 9) * 6.3))
                tev = max(0, int((tb - 10) * 6.3))
                tld = min(16, 3 + max(0, int((tb - 8) * 0.6)))
                while True:
                    did = False
                    if vp["ld"] < tld and len(wv_last_mm) >= vp["ld"] - 2:
                        plan_wv_load(vp["ld"]); vp["ld"] += 1; did = True
                    if vp["mm"] < tmm:
                        idx = vp["mm"]
                        if (idx // 16 < len(wv_load)
                                and (16 * (idx // 128) + idx % 16) // 8 < len(xnq_ready)
                                and (idx < 4 or idx - 4 < len(vproj_ev))):
                            plan_v_mm(idx); vp["mm"] += 1; did = True
                    if (vp["ev"] < min(tev, len(vproj_mm))
                            and (vp["ev"] < 4 or vp["ev"] - 4 < len(v_store))):
                        plan_v_ev(vp["ev"]); vp["ev"] += 1; did = True
                    if vp["st"] < len(vproj_ev):
                        plan_v_st(vp["st"]); vp["st"] += 1; did = True
                    if not did:
                        break

            sl = d["wq"][:, 0:128].rearrange("(cb p) n -> p cb n", p=128)
            def f(g, sl=sl):
                g.dma_start(out=wqk_q[:].rearrange("p (cb n) -> p cb n", n=128),
                            in_=sl).then_inc(swq.sem, 16)
            em("gpsimd", f)
            wq0_ld = swq.inc()
            sl = d["wk"][:, 0:128].rearrange("(cb p) n -> p cb n", p=128)
            def f(g, sl=sl):
                g.dma_start(out=wqk_k[:].rearrange("p (cb n) -> p cb n", n=128),
                            in_=sl).then_inc(swk.sem, 16)
            em("gpsimd", f)
            wk0_ld = swk.inc()

            LNH = float(H)
            xn_act, xn_store, xnq_ready = [], [], []
            for tb in range(NTB):
                buf = x_t[tb % 3]
                war = xn_act[tb-3] if tb >= 3 else None
                sxc = sx[tb % 3]
                def f(sy, tb=tb, buf=buf, war=war, sxc=sxc):
                    if war is not None:
                        sy.wait_ge(sem_s, war)
                    sy.dma_start(out=buf[:], in_=d["xp"][tb*128:(tb+1)*128, :]).then_inc(sxc.sem, 16)
                em("sync", f)
                ld = sxc.inc()
                def f(v, buf=buf, ld=ld, tb=tb, sxc=sxc):
                    v.wait_ge(sxc.sem, ld)
                    v.reduce_sum(sums[:, tb:tb+1], buf[:], axis=AX.X).then_inc(sem_v, 1)
                em("vector", f)
                vred = sv.inc()
                def f(s, buf=buf, ld=ld, tb=tb, sxc=sxc):
                    s.wait_ge(sxc.sem, ld)
                    if tb == 0:
                        s.wait_ge(sem_v, bvb_ready)
                    s.activation(psum[:, 2048:4096], buf[:], AF.Square,
                                 accum_out=sumsq[:, tb:tb+1]).then_inc(sem_s, 1)
                em("scalar", f)
                sred = ss.inc()
                def f(v, tb=tb, vred=vred):
                    v.wait_ge(sem_v, vred)
                    v.tensor_tensor(lnv[:, tb:tb+1], sums[:, tb:tb+1], sums[:, tb:tb+1],
                                    ALU.mult).then_inc(sem_v, 1)
                em("vector", f)
                vt1 = sv.inc()
                def f(v, tb=tb, vt1=vt1, sred=sred):
                    v.wait_ge(sem_v, vt1)
                    v.wait_ge(sem_s, sred)
                    v.scalar_tensor_tensor(sumsq[:, tb:tb+1], sumsq[:, tb:tb+1], LNH,
                                           lnv[:, tb:tb+1], ALU.mult,
                                           ALU.subtract).then_inc(sem_v, 1)
                em("vector", f)
                vX = sv.inc()
                def f(s, tb=tb, vX=vX):
                    s.wait_ge(sem_v, vX)
                    s.activation(lnv[:, tb:tb+1], sumsq[:, tb:tb+1], AF.Ln,
                                 bias=eps2_c[:, 0:1]).then_inc(sem_s, 1)
                em("scalar", f)
                sln = ss.inc()
                def f(s, tb=tb, sln=sln):
                    s.wait_ge(sem_s, sln)
                    s.activation(rstd[:, tb:tb+1], lnv[:, tb:tb+1], AF.Exp, scale=-0.5,
                                 bias=lnh_c[:, 0:1]).then_inc(sem_s, 1)
                em("scalar", f)
                srs = ss.inc()
                def f(v, tb=tb, srs=srs):
                    v.wait_ge(sem_s, srs)
                    v.scalar_tensor_tensor(negmr[:, tb:tb+1], sums[:, tb:tb+1], -1.0/LNH,
                                           rstd[:, tb:tb+1], ALU.mult,
                                           ALU.mult).then_inc(sem_v, 1)
                em("vector", f)
                vng = sv.inc()
                sxnc = sxn[tb % 2]
                war2 = xn_store[tb-2] if tb >= 2 else None
                def f(s, tb=tb, buf=buf, vng=vng, war2=war2, sxnc=sxnc):
                    s.wait_ge(sem_v, vng)
                    if war2 is not None:
                        s.wait_ge(sxnc.sem, war2)
                    s.activation(xn_t[tb % 2][:], buf[:], AF.Identity,
                                 bias=negmr[:, tb:tb+1], scale=rstd[:, tb:tb+1]).then_inc(sem_s, 1)
                em("scalar", f)
                xn_act.append(ss.inc())
                act = xn_act[tb]
                def f(g, tb=tb, act=act, sxnc=sxnc):
                    g.wait_ge(sem_s, act)
                    g.dma_start(out=d["xn_scr"][tb*128:(tb+1)*128, :],
                                in_=xn_t[tb % 2][:]).then_inc(sxnc.sem, 16)
                em("gpsimd", f)
                xn_store.append(sxnc.inc())
                if (tb + 1) % 8 == 0:
                    qt = tb // 8
                    snap = (sxn[0].n, sxn[1].n)
                    sxq = sxnt[qt]
                    def f(sy, qt=qt, snap=snap, sxq=sxq):
                        sy.wait_ge(sxn[0].sem, snap[0])
                        sy.wait_ge(sxn[1].sem, snap[1])
                        for cb in range(NCB):
                            sy.dma_start(out=xnT[:, cb*T + qt*1024: cb*T + (qt+1)*1024],
                                         in_=d["xn_scr"][qt*1024:(qt+1)*1024,
                                                         cb*128:(cb+1)*128],
                                         transpose=True).then_inc(sxq.sem, 16)
                    em("sync", f)
                    for _ in range(NCB):
                        sxq.inc()
                    xnq_ready.append((sxq.sem, sxq.n))
                plan_v(tb)

            xnT_ready = list(xnq_ready)

            while vp["mm"] < NV or vp["ev"] < NV or vp["st"] < NV or vp["ld"] < 16:
                plan_v(999)
            v_ready = tuple((c.sem, c.n) for c in svst)


        barrier()
        with ExitStack() as S2:
            E2 = S2.enter_context
            expT = E2(nc.sbuf_tensor("expT", [128, 32 * W], BF16))
            cos_s = E2(nc.sbuf_tensor("cos_s", [128, T], BF16))
            sin_s = E2(nc.sbuf_tensor("sin_s", [128, T], BF16))
            qrotT = E2(nc.sbuf_tensor("qrotT", [128, OWN], BF16))
            krotT = E2(nc.sbuf_tensor("krotT", [128, T], BF16))
            v_head = E2(nc.sbuf_tensor("v_head", [128, 32 * 128], BF16))

            def f(sy):
                sy.dma_start(out=cos_s[:], in_=d["cos2"]).then_inc(stab.sem, 16)
                sy.dma_start(out=sin_s[:], in_=d["sin2s"]).then_inc(stab.sem, 16)
            em("sync", f)
            stab.inc()
            tables_ready = stab.inc()
            stb = [E2(nc.sbuf_tensor(f"stb{i}", [128, 512], F32)) for i in range(3)]
            fcr = E2(nc.sbuf_tensor("fcr", [128, 160], F32))
            prr = E2(nc.sbuf_tensor("prr", [128, 160], F32))
            fsum2 = E2(nc.sbuf_tensor("fsum2", [128, 2], F32))
            dmp2 = E2(nc.sbuf_tensor("dmp2", [128, 512], BF16))
            qe_b = E2(nc.sbuf_tensor("qe_b", [128, 1024], BF16))
            qsw_b = E2(nc.sbuf_tensor("qsw_b", [128, 1024], BF16))
            rt1 = E2(nc.sbuf_tensor("rt1", [128, 1024], BF16))
            rt2 = E2(nc.sbuf_tensor("rt2", [128, 1024], BF16))
            rb_sb = E2(nc.sbuf_tensor("rb_sb", [128, W], F32))
            o_sb = [E2(nc.sbuf_tensor("o_sb0", [128, W], BF16))] * 2

            REG = [0, 1024, 3072]
            DF0, RB0, OT0 = 2048, 2304, 2560
            QKB = 3072
            GROUPS = [list(range(4*g, 4*g+4)) for g in range(8)]

            wfc_flat = d["wfcT"].rearrange("a b -> (a b)")
            wpr_flat = d["wprT"].rearrange("a b -> (a b)")
            stat_load_done = [None] * 320
            stat_red_done = [None] * 320

            def emit_stat_load(i):
                buf = stb[i % 3]
                src = wfc_flat if i < 160 else wpr_flat
                j = i if i < 160 else i - 160
                sl = src[j * 65536:(j + 1) * 65536].rearrange("(p m) -> p m", p=128)
                sstc = sstat[i % 3]
                def f(sy, sl=sl, buf=buf, i=i, sstc=sstc):
                    if i >= 3:
                        war = stat_red_done[i - 3]
                        sy.wait_ge(war[0], war[1])
                    sy.dma_start(out=buf[:], in_=sl).then_inc(sstc.sem, 16)
                em("sync", f)
                stat_load_done[i] = (sstc.sem, sstc.inc())

            def emit_stat_red(i):
                buf = stb[i % 3]
                ld = stat_load_done[i]
                if i < 160:
                    def f(v, buf=buf, ld=ld, i=i):
                        v.wait_ge(*ld)
                        v.reduce_sum(fcr[:, i:i+1], buf[:], axis=AX.X,
                                     apply_absolute_value=True).then_inc(sem_v, 1)
                    em("vector", f)
                    stat_red_done[i] = (sem_v, sv.inc())
                else:
                    pa = stat_red_done[i-1] if (i > 160 and stat_red_done[i-1] is not None
                                                and stat_red_done[i-1][0] is sem_s) else None
                    def f(s, buf=buf, ld=ld, i=i, pa=pa):
                        s.wait_ge(*ld)
                        if pa is not None:
                            s.wait_ge(*pa)
                        s.activation(dmp2[:], buf[:], AF.Abs,
                                     accum_out=prr[:, i-160:i-159]).then_inc(sem_s, 1)
                    em("scalar", f)
                    stat_red_done[i] = (sem_s, ss.inc())

            HEAD_MIL = []
            PENDING_WQK = {}
            P3 = [0]
            LASTREG = [None]
            REGION_LAST_EXP = {}
            last_c_exp = [None]

            def plan_head(h):
                mil = {}
                prev = HEAD_MIL[h-1] if h > 0 else None

                if h == 0:
                    mil["wq_ld"] = wq0_ld
                    mil["wk_ld"] = wk0_ld
                else:
                    mil["wq_ld"], mil["wk_ld"] = PENDING_WQK[h]
                for k in range(20*h, 20*h+20):
                    emit_stat_load(k)
                RQ = list(range(20*h, 20*h+20))
                def reds_until(done_groups, RQ=RQ):
                    tgt = (done_groups * 20) // 64
                    while (20 - len(RQ)) < tgt:
                        emit_stat_red(RQ.pop(0))

                src = d["v_scr"][:, h*128:(h+1)*128].rearrange("(kb p) n -> p kb n", p=128)
                def f(g, src=src, prev=prev):
                    if prev is not None:
                        g.wait_ge(sem_t, prev["t_last"])
                    else:
                        for _s, _v in v_ready:
                            g.wait_ge(_s, _v)
                    g.dma_start(out=v_head[:].rearrange("p (kb n) -> p kb n", n=128),
                                in_=src).then_inc(svh.sem, 16)
                em("gpsimd", f)
                mil["v_ld"] = svh.inc()

                qkb_war = last_c_exp[0]
                war3 = prev["t_last"] if prev else None
                qev = []
                rope_piece = []
                for j in range(4):
                    bank = QKB + (j % 2) * 512
                    wq_ld = mil["wq_ld"]
                    war = qev[j-2] if j >= 2 else (prev["kev_last"] if prev else None)
                    def f(t, j=j, bank=bank, wq_ld=wq_ld, war=war, h=h, qkb_war=qkb_war):
                        t.wait_ge(swq.sem, wq_ld)
                        if h == 0 and j == 0:
                            for _sw, _sv in xnT_ready:
                                t.wait_ge(_sw, _sv)
                        if j == 0 and qkb_war is not None:
                            t.wait_ge(sem_s, qkb_war)
                        if war is not None:
                            t.wait_ge(sem_s, war)
                        for cb in range(NCB):
                            ins = t.matmul(psum[:, bank:bank+512],
                                           wqk_q[:, cb*128:(cb+1)*128],
                                           xnT[:, cb*T + j*512: cb*T + (j+1)*512],
                                           start=(cb == 0), stop=(cb == NCB-1))
                            if cb == NCB-1:
                                ins.then_inc(sem_t, 1)
                    em("tensor", f)
                    mm = st.inc()
                    war2 = rope_piece[j//2 - 1] if j >= 2 else (prev["ropek1"] if prev else None)
                    def f(s, j=j, bank=bank, mm=mm, h=h, war2=war2):
                        s.wait_ge(sem_t, mm)
                        if j % 2 == 0 and war2 is not None:
                            s.wait_ge(sem_v, war2)
                        s.activation(qe_b[:, (j % 2)*512:(j % 2)*512+512], psum[:, bank:bank+512],
                                     AF.Identity, bias=bqc_s[:, h:h+1]).then_inc(sem_s, 1)
                    em("scalar", f)
                    qev.append(ss.inc())
                    if j % 2 == 1:
                        p = j // 2
                        ev2 = qev[-1]
                        def f(g, ev2=ev2):
                            g.wait_ge(sem_s, ev2)
                            g.dma_start(out=qsw_b[0:64, :], in_=qe_b[64:128, :]).then_inc(ssw.sem, 16)
                            g.dma_start(out=qsw_b[64:128, :], in_=qe_b[0:64, :]).then_inc(ssw.sem, 16)
                        em("gpsimd", f)
                        ssw.inc()
                        swp = ssw.inc()
                        def f(v, swp=swp, p=p, h=h, war3=war3):
                            v.wait_ge(ssw.sem, swp)
                            if p == 0 and h == 0:
                                v.wait_ge(stab.sem, tables_ready)
                            v.tensor_tensor(rt1[:], qe_b[:], cos_s[:, p*1024:(p+1)*1024], ALU.mult)
                            v.tensor_tensor(rt2[:], qsw_b[:], sin_s[:, p*1024:(p+1)*1024],
                                            ALU.mult).then_inc(sem_v, 1)
                        em("vector", f)
                        eq = sv.inc()
                        def f(v, eq=eq, war3=war3, p=p):
                            v.wait_ge(sem_v, eq)
                            if p == 0 and war3 is not None:
                                v.wait_ge(sem_t, war3)
                            v.tensor_tensor(qrotT[:, p*1024:(p+1)*1024], rt1[:], rt2[:],
                                            ALU.add).then_inc(sem_v, 1)
                        em("vector", f)
                        rope_piece.append(sv.inc())
                mil["qproj_mm"] = st.n
                mil["ropeq"] = rope_piece[-1]

                kev_prev = qev
                krope_piece = list(rope_piece)
                for kh in range(2):
                    kev = []
                    for j in range(4):
                        bank = QKB + (j % 2) * 512
                        tcol = kh * 2048 + j * 512
                        wk_ld = mil["wk_ld"]
                        war = kev[j-2] if j >= 2 else kev_prev[2+j]
                        def f(t, bank=bank, tcol=tcol, wk_ld=wk_ld, war=war):
                            t.wait_ge(swk.sem, wk_ld)
                            t.wait_ge(sem_s, war)
                            for cb in range(NCB):
                                ins = t.matmul(psum[:, bank:bank+512],
                                               wqk_k[:, cb*128:(cb+1)*128],
                                               xnT[:, cb*T + tcol: cb*T + tcol + 512],
                                               start=(cb == 0), stop=(cb == NCB-1))
                                if cb == NCB-1:
                                    ins.then_inc(sem_t, 1)
                        em("tensor", f)
                        mm = st.inc()
                        war2 = krope_piece[-1] if j % 2 == 0 else None
                        def f(s, j=j, bank=bank, mm=mm, h=h, war2=war2):
                            s.wait_ge(sem_t, mm)
                            if war2 is not None:
                                s.wait_ge(sem_v, war2)
                            s.activation(qe_b[:, (j % 2)*512:(j % 2)*512+512],
                                         psum[:, bank:bank+512],
                                         AF.Identity, bias=bkc_s[:, h:h+1]).then_inc(sem_s, 1)
                        em("scalar", f)
                        kev.append(ss.inc())
                        if j % 2 == 1:
                            p = j // 2
                            off = kh*2048 + p*1024
                            ev2 = kev[-1]
                            def f(g, ev2=ev2):
                                g.wait_ge(sem_s, ev2)
                                g.dma_start(out=qsw_b[0:64, :], in_=qe_b[64:128, :]).then_inc(ssw.sem, 16)
                                g.dma_start(out=qsw_b[64:128, :], in_=qe_b[0:64, :]).then_inc(ssw.sem, 16)
                            em("gpsimd", f)
                            ssw.inc()
                            swp = ssw.inc()
                            def f(v, swp=swp, off=off):
                                v.wait_ge(ssw.sem, swp)
                                v.tensor_tensor(rt1[:], qe_b[:], cos_s[:, off:off+1024], ALU.mult)
                                v.tensor_tensor(rt2[:], qsw_b[:], sin_s[:, off:off+1024],
                                                ALU.mult).then_inc(sem_v, 1)
                            em("vector", f)
                            ek = sv.inc()
                            war4 = prev["t_last"] if (prev and kh == 0 and p == 0) else None
                            def f(v, ek=ek, war4=war4, off=off):
                                v.wait_ge(sem_v, ek)
                                if war4 is not None:
                                    v.wait_ge(sem_t, war4)
                                v.tensor_tensor(krotT[:, off:off+1024], rt1[:], rt2[:],
                                                ALU.add).then_inc(sem_v, 1)
                            em("vector", f)
                            krope_piece.append(sv.inc())
                    kev_prev = kev
                    mil["ropek%d" % kh] = krope_piece[-1]
                mil["kev_last"] = kev_prev[3]
                mil["kproj_mm"] = st.n

                NG = 8
                exp_done = [[None]*NG for _ in range(NCH)]
                ones_done = [[None, None] for _ in range(NCH)]
                mask_done = [[None, None] for _ in range(NCH)]
                recip_done = [None] * NCH
                av_done = [[None, None] for _ in range(NCH)]
                evac_done = [None] * NCH
                o_stores = [None] * NCH

                def emit_group(c, gi):
                    lg = c * 8 + gi
                    if lg < 58:
                        reg = REG[P3[0] % 3]
                        P3[0] += 1
                        if reg == LASTREG[0]:
                            reg = REG[P3[0] % 3]
                            P3[0] += 1
                    else:
                        reg = REG[0] if LASTREG[0] != REG[0] else REG[1]
                    LASTREG[0] = reg
                    grp = GROUPS[gi]
                    war = REGION_LAST_EXP.get(reg)
                    rk = mil["ropek0"] if gi < 4 else mil["ropek1"]
                    kevw = mil["kev_last"] if reg == QKB else None
                    def f(t, c=c, reg=reg, grp=tuple(grp), war=war, rk=rk, kevw=kevw):
                        if war is not None:
                            t.wait_ge(sem_s, war)
                        if kevw is not None:
                            t.wait_ge(sem_s, kevw)
                        t.wait_ge(sem_v, rk)
                        for kb in grp:
                            pos = kb - grp[0]
                            ins = t.matmul(psum[:, reg+pos*256: reg+pos*256+256],
                                           krotT[:, kb*128:(kb+1)*128],
                                           qrotT[:, c*W:(c+1)*W],
                                           start=(pos % 2 == 0), stop=(pos % 2 == 1))
                            if kb == grp[-1]:
                                ins.then_inc(sem_t, 1)
                    em("tensor", f)
                    gmm = st.inc()
                    g0 = GROUPS[gi][0]
                    half = 0 if gi < NG // 2 else 1
                    if c >= 1:
                        free_war = ("avh", c - 1, half)
                    elif prev is not None:
                        free_war = ("pt", prev["t_last"], None)
                    else:
                        free_war = None
                    def f(s, reg=reg, g0=g0, gmm=gmm, free_war=free_war):
                        s.wait_ge(sem_t, gmm)
                        if free_war is not None:
                            if free_war[0] == "avh":
                                s.wait_ge(sem_t, av_done[free_war[1]][free_war[2]])
                            else:
                                s.wait_ge(sem_t, free_war[1])
                        s.activation(expT[:, g0*W:(g0+4)*W], psum[:, reg:reg+1024],
                                     AF.Exp, scale=SCALE).then_inc(sem_s, 1)
                    em("scalar", f)
                    ev = ss.inc()
                    exp_done[c][gi] = ev
                    REGION_LAST_EXP[reg] = ev
                    if reg == REG[2]:
                        last_c_exp[0] = ev

                def emit_ones(c, half):
                    kbs = range(16) if half == 0 else range(16, 32)
                    war = (recip_done[c-1] if c >= 1 else
                           (prev["recip_last"] if prev else None)) if half == 0 else None
                    lastexp = exp_done[c][3 if half == 0 else 7]
                    def f(t, kbs=tuple(kbs), war=war, half=half, lastexp=lastexp):
                        t.wait_ge(sem_s, lastexp)
                        if war is not None:
                            t.wait_ge(sem_v, war)
                        for kb in kbs:
                            ins = t.matmul(psum[:, DF0:DF0+W], ones_cb[:],
                                           expT[:, kb*W:(kb+1)*W],
                                           start=(kb == 0), stop=(kb == 31))
                            if kb == kbs[-1]:
                                ins.then_inc(sem_t, 1)
                    em("tensor", f)
                    ones_done[c][half] = st.inc()

                def emit_masks(c, half):
                    j0, j1 = 2*c, 2*c+1
                    if half == 0:
                        slices = [(j0*W, j0*W+128, mka_s), (j1*W+128, j1*W+256, mka_s)]
                    else:
                        slices = [((16+j0)*W, (16+j0)*W+128, mkb_s),
                                  ((16+j1)*W+128, (16+j1)*W+256, mkb_s)]
                    def f(v, c=c, half=half, slices=tuple(slices)):
                        v.wait_ge(sem_t, ones_done[c][half])
                        for a, b, mk in slices:
                            ins = v.tensor_tensor(expT[:, a:b], expT[:, a:b], mk[:], ALU.mult)
                        ins.then_inc(sem_v, 1)
                    em("vector", f)
                    mask_done[c][half] = sv.inc()

                def emit_av(c, half):
                    valid = [kb for kb in (range(16) if half == 0 else range(16, 32))
                             if (kb % 16) <= 2*c+1]
                    vld = mil["v_ld"]
                    war = (evac_done[c-1] if c >= 1 else
                           (prev["evac_last"] if prev else None)) if half == 0 else None
                    def f(t, c=c, half=half, valid=tuple(valid), vld=vld, war=war):
                        t.wait_ge(sem_v, mask_done[c][half])
                        t.wait_ge(svh.sem, vld)
                        if war is not None:
                            t.wait_ge(sem_v, war)
                        for i, kb in enumerate(valid):
                            kj = kb % 16
                            cs = 128 if kj == 2*c+1 else 0
                            ins = t.matmul(psum[:, OT0+cs:OT0+256],
                                           v_head[:, kb*128:(kb+1)*128],
                                           expT[:, kb*W+cs:(kb+1)*W],
                                           start=(half == 0 and i == 0),
                                           stop=(half == 1 and i == len(valid)-1))
                            if i == len(valid)-1:
                                ins.then_inc(sem_t, 1)
                    em("tensor", f)
                    av_done[c][half] = st.inc()

                def emit_recip(c):
                    war = evac_done[c-1] if c >= 1 else (prev["evac_last"] if prev else None)
                    def f(v, c=c, war=war):
                        v.wait_ge(sem_t, ones_done[c][1])
                        if war is not None:
                            v.wait_ge(sem_v, war)
                        v.reciprocal(rb_sb[:], psum[:, DF0:DF0+W]).then_inc(sem_v, 1)
                    em("vector", f)
                    recip_done[c] = sv.inc()

                def emit_bcast(c):
                    war = o_stores[c-1] if c >= 1 else (prev["o_last"] if prev else None)
                    def f(v, c=c, war=war):
                        v.wait_ge(sem_v, recip_done[c])
                        v.wait_ge(sem_t, av_done[c][1])
                        if war is not None:
                            v.wait_ge(sost.sem, war)
                        v.tensor_tensor(o_sb[c % 2][:], psum[:, OT0:OT0+W], rb_sb[:],
                                        ALU.mult).then_inc(sem_v, 1)
                    em("vector", f)
                    evac_done[c] = sv.inc()
                    def f(g, c=c, h=h):
                        g.wait_ge(sem_v, evac_done[c])
                        g.dma_start(out=d["o_scr"][h*128:(h+1)*128, c*W:(c+1)*W],
                                    in_=o_sb[c % 2][:]).then_inc(sost.sem, 16)
                    em("gpsimd", f)
                    o_stores[c] = sost.inc()

                def emit_p2(c):
                    emit_ones(c, 0)
                    emit_masks(c, 0)
                    emit_av(c, 0)
                    emit_ones(c, 1)
                    emit_masks(c, 1)
                    emit_recip(c)
                    emit_av(c, 1)
                    emit_bcast(c)

                for c in range(NCH):
                    emit_group(c, 0)
                    emit_group(c, 1)
                    if c >= 1:
                        emit_ones(c-1, 0)
                        emit_masks(c-1, 0)
                        emit_av(c-1, 0)
                    emit_group(c, 2)
                    reds_until(c*8 + 3)
                    emit_group(c, 3)
                    if c >= 1:
                        emit_ones(c-1, 1)
                        emit_masks(c-1, 1)
                        emit_recip(c-1)
                    emit_group(c, 4)
                    reds_until(c*8 + 5)
                    emit_group(c, 5)
                    if c >= 1:
                        emit_av(c-1, 1)
                        emit_bcast(c-1)
                    emit_group(c, 6)
                    reds_until(c*8 + 7)
                    emit_group(c, 7)
                    reds_until(c*8 + 8)
                    if c == 1 and h + 1 < NH:
                        sl = d["wq"][:, (h+1)*128:(h+2)*128].rearrange(
                            "(cb p) n -> p cb n", p=128)
                        qpw = mil["qproj_mm"]
                        def f(g, sl=sl, qpw=qpw):
                            g.wait_ge(sem_t, qpw)
                            g.dma_start(out=wqk_q[:].rearrange("p (cb n) -> p cb n", n=128),
                                        in_=sl).then_inc(swq.sem, 16)
                        em("gpsimd", f)
                        wqv = swq.inc()
                        sl = d["wk"][:, (h+1)*128:(h+2)*128].rearrange(
                            "(cb p) n -> p cb n", p=128)
                        kpw = mil["kproj_mm"]
                        def f(g, sl=sl, kpw=kpw):
                            g.wait_ge(sem_t, kpw)
                            g.dma_start(out=wqk_k[:].rearrange("p (cb n) -> p cb n", n=128),
                                        in_=sl).then_inc(swk.sem, 16)
                        em("gpsimd", f)
                        PENDING_WQK[h+1] = (wqv, swk.inc())
                emit_p2(NCH-1)
                while RQ:
                    emit_stat_red(RQ.pop(0))

                mil["t_last"] = st.n
                mil["exp_last"] = exp_done[NCH-1][NG-1]
                mil["evac_last"] = evac_done[NCH-1]
                mil["recip_last"] = recip_done[NCH-1]
                mil["o_last"] = o_stores[NCH-1]
                return mil

            for h in range(NH):
                HEAD_MIL.append(plan_head(h))
            o_ready = HEAD_MIL[-1]["o_last"]

            vW2, sW2, tW2 = sv.n, ss.n, st.n
            def f(v, vW2=vW2, sW2=sW2):
                v.wait_ge(sem_v, vW2)
                v.wait_ge(sem_s, sW2)
                v.reduce_sum(fsum2[:, 0:1], fcr[:], axis=AX.X)
                v.reduce_sum(fsum2[:, 1:2], prr[:], axis=AX.X).then_inc(sem_v, 1)
            em("vector", f); fin1 = sv.inc()
            def f(t, fin1=fin1, tW2=tW2):
                t.wait_ge(sem_v, fin1)
                t.wait_ge(sem_t, tW2)
                t.matmul(psum[0:1, 0:1], fsum2[:, 0:1], ones_cf[:], start=True, stop=False)
                t.matmul(psum[0:1, 1:2], fsum2[:, 1:2], ones_cf[:], start=False,
                         stop=True).then_inc(sem_t, 1)
            em("tensor", f); fin2 = st.inc()
            NWF = float(H * FFH)
            def f(v, fin2=fin2):
                v.wait_ge(sem_t, fin2)
                v.tensor_scalar(am11[0:1, 0:2], psum[0:1, 0:2], 1.0/NWF, 1e-5,
                                ALU.mult, ALU.max).then_inc(sem_v, 1)
            em("vector", f); fin3 = sv.inc()
            def f(v, fin3=fin3):
                v.wait_ge(sem_v, fin3)
                v.tensor_scalar(am11[0:1, 2:3], am11[0:1, 0:1], 0.7, None, ALU.mult)
                v.tensor_scalar(am11[0:1, 3:4], am11[0:1, 0:1], -0.7, None, ALU.mult)
                v.tensor_scalar(am11[0:1, 4:5], am11[0:1, 0:1], -1.0, None, ALU.mult)
                v.tensor_scalar(am11[0:1, 5:6], am11[0:1, 1:2], 0.7, None, ALU.mult)
                v.tensor_scalar(am11[0:1, 6:7], am11[0:1, 1:2], -0.7, None, ALU.mult)
                v.tensor_scalar(am11[0:1, 7:8], am11[0:1, 1:2], -1.0, None,
                                ALU.mult).then_inc(sem_v, 1)
            em("vector", f); fin4 = sv.inc()
            def f(t, fin4=fin4):
                t.wait_ge(sem_v, fin4)
                for j in range(6):
                    ins = t.matmul(psum[:, 8+j:9+j], ones_rf[0:1, :], am11[0:1, 2+j:3+j],
                                   start=(j == 0), stop=(j == 5))
                ins.then_inc(sem_t, 1)
            em("tensor", f); fin5 = st.inc()
            def f(v, fin5=fin5):
                v.wait_ge(sem_t, fin5)
                v.tensor_copy(tcols[:], psum[:, 8:14]).then_inc(sem_v, 1)
            em("vector", f); tern_ready = sv.inc()

        wqk_stack.close()

        barrier()
        with ExitStack() as S3:
            E3 = S3.enter_context
            wfc_b = [E3(nc.sbuf_tensor(f"wfc_b{i}", [128, H], F32)) for i in range(2)]
            wpr_b = [E3(nc.sbuf_tensor(f"wpr_b{i}", [128, H], F32)) for i in range(2)]
            qprs = [E3(nc.sbuf_tensor("qprs0", [128, H], BF16))] * 2
            qfc = [E3(nc.sbuf_tensor(f"qfc{i}", [128, H], BF16)) for i in range(2)]
            gtb = E3(nc.sbuf_tensor("gtb", [128, H], BF16))
            relu_t = E3(nc.sbuf_tensor("relu_t", [128, 1024], F32))
            hT_t = [E3(nc.sbuf_tensor("hT_t0", [128, OWN], BF16))] * 2

            pr_tern_d, pr_store_d = [], []
            def emit_wpr_strip(fb):
                sl = d["wprT"][fb*128:(fb+1)*128, :]
                war = pr_store_d[fb-2] if fb >= 2 else None
                spc = swpr[fb % 2]
                def f(sy, sl=sl, war=war, fb=fb, spc=spc):
                    if war is not None:
                        sy.wait_ge(*war)
                    sy.dma_start(out=wpr_b[fb % 2][:], in_=sl).then_inc(spc.sem, 16)
                em("sync", f)
                ld = spc.inc()
                gwar = fc_tern[-1] if fc_tern else None
                gwar2 = pr_tern_d[-1] if pr_tern_d else None
                def f(v, fb=fb, ld=ld, spc=spc, gwar=gwar, gwar2=gwar2):
                    v.wait_ge(spc.sem, ld)
                    v.wait_ge(sem_v, tern_ready)
                    if gwar is not None:
                        v.wait_ge(sem_v, gwar)
                    if gwar2 is not None:
                        v.wait_ge(sem_v, gwar2)
                    v.tensor_scalar(gtb[:], wpr_b[fb % 2][:], tcols[:, 3:4], None,
                                    ALU.is_gt).then_inc(sem_v, 1)
                em("vector", f)
                ed = sv.inc()
                swar = pr_store_d[-1] if pr_store_d else None
                def f(v, fb=fb, ed=ed, swar=swar):
                    v.wait_ge(sem_v, ed)
                    if swar is not None:
                        v.wait_ge(*swar)
                    v.scalar_tensor_tensor(qprs[fb % 2][:], wpr_b[fb % 2][:],
                                           tcols[:, 4:5], gtb[:], ALU.is_lt,
                                           ALU.subtract).then_inc(sem_v, 1)
                em("vector", f)
                pr_tern_d.append(sv.inc())
                trn = pr_tern_d[-1]
                srsc = srst[fb % 2]
                def f(g, fb=fb, trn=trn, srsc=srsc):
                    g.wait_ge(sem_v, trn)
                    for q in range(4):
                        g.dma_start(out=d["qpr_scr"][q][:, fb],
                                    in_=qprs[fb % 2][:, q*512:(q+1)*512]).then_inc(srsc.sem, 16)
                em("gpsimd", f)
                for _ in range(4):
                    srsc.inc()
                pr_store_d.append((srsc.sem, srsc.n))

            fc_ld, fc_tern, fc_mm, fc_relu, fc_sq, fc_st = [], [], [], [], [], []
            for fb in range(NFB):
                sl = d["wfc3"][fb]
                war = fc_tern[fb-2] if fb >= 2 else None
                sfc = swfc[fb % 2]
                def f(sy, sl=sl, fb=fb, war=war, sfc=sfc):
                    if war is not None:
                        sy.wait_ge(sem_v, war)
                    sy.dma_start(out=wfc_b[fb % 2][:], in_=sl).then_inc(sfc.sem, 16)
                em("sync", f)
                fc_ld.append(sfc.inc())
                ld = fc_ld[fb]
                war = fc_mm[fb-2] if fb >= 2 else None
                gwar = pr_tern_d[-1] if pr_tern_d else None
                def f(v, fb=fb, ld=ld, war=war, sfc=sfc, gwar=gwar):
                    v.wait_ge(sfc.sem, ld)
                    v.wait_ge(sem_v, tern_ready)
                    if war is not None:
                        v.wait_ge(sem_t, war)
                    if gwar is not None:
                        v.wait_ge(sem_v, gwar)
                    v.tensor_scalar(gtb[:], wfc_b[fb % 2][:], tcols[:, 0:1], None,
                                    ALU.is_gt).then_inc(sem_v, 1)
                em("vector", f)
                ed = sv.inc()
                def f(v, fb=fb, ed=ed):
                    v.wait_ge(sem_v, ed)
                    v.scalar_tensor_tensor(qfc[fb % 2][:], wfc_b[fb % 2][:], tcols[:, 1:2],
                                           gtb[:], ALU.is_lt, ALU.subtract).then_inc(sem_v, 1)
                em("vector", f)
                fc_tern.append(sv.inc())
                emit_wpr_strip(fb)
                reg = (fb % 2) * 2048
                trn = fc_tern[fb]
                war = fc_relu[4*(fb-2)+3] if fb >= 2 else None
                def f(t, fb=fb, reg=reg, trn=trn, war=war):
                    t.wait_ge(sem_v, trn)
                    if war is not None:
                        t.wait_ge(sem_s, war)
                    for tc in range(4):
                        for cb in range(NCB):
                            ins = t.matmul(psum[:, reg + tc*512: reg + (tc+1)*512],
                                           qfc[fb % 2][:, cb*128:(cb+1)*128],
                                           xnT[:, cb*T + tc*512: cb*T + (tc+1)*512],
                                           start=(cb == 0), stop=(cb == NCB-1))
                        ins.then_inc(sem_t, 1)
                em("tensor", f)
                mmv = [st.inc() for _ in range(4)]
                fc_mm.append(mmv[3])
                for hh in range(4):
                    gi = 4*fb + hh
                    rwar = fc_sq[gi-2] if gi >= 2 else None
                    mmw = mmv[hh]
                    def f(s, reg=reg, hh=hh, rwar=rwar, mmw=mmw):
                        s.wait_ge(sem_t, mmw)
                        if rwar is not None:
                            s.wait_ge(sem_s, rwar)
                        s.activation(relu_t[:, (hh % 2)*512:(hh % 2)*512+512],
                                     psum[:, reg+hh*512:reg+(hh+1)*512],
                                     AF.Relu, scale=tcols[:, 2:3]).then_inc(sem_s, 1)
                    em("scalar", f)
                    fc_relu.append(ss.inc())
                    rv = fc_relu[-1]
                    war2 = (shT[(fb-1) % 2].sem, fc_st[fb-1]) if (hh == 0 and fb >= 1) else None
                    def f(s, fb=fb, hh=hh, rv=rv, war2=war2):
                        s.wait_ge(sem_s, rv)
                        if war2 is not None:
                            s.wait_ge(*war2)
                        s.activation(hT_t[fb % 2][:, hh*512:(hh+1)*512],
                                     relu_t[:, (hh % 2)*512:(hh % 2)*512+512],
                                     AF.Square).then_inc(sem_s, 1)
                    em("scalar", f)
                    fc_sq.append(ss.inc())
                shc = shT[fb % 2]
                def f(g, fb=fb, shc=shc):
                    g.wait_ge(sem_s, fc_sq[4*fb+3])
                    for tcx in range(NCH):
                        g.dma_start(out=d["hT_scr"][tcx][:, fb],
                                    in_=hT_t[fb % 2][:, tcx*256:(tcx+1)*256]).then_inc(shc.sem, 16)
                em("gpsimd", f)
                for _ in range(NCH):
                    shc.inc()
                fc_st.append(shc.n)
            hT_ready = (shT[0].n, shT[1].n)
        xnT_stack.close()

        barrier()
        with ExitStack() as S4:
            E4 = S4.enter_context
            qpr = [E4(nc.sbuf_tensor(f"qpr{i}", [128, NFB * 512], BF16)) for i in range(2)]
            hT_tc = [E4(nc.sbuf_tensor(f"hT_tc{i}", [128, NFB * 256], BF16)) for i in range(2)]
            x_e = [E4(nc.sbuf_tensor(f"x_e{i}", [128, 512], F32)) for i in range(2)]
            res_e = [E4(nc.sbuf_tensor(f"res_e{i}", [128, 512], F32)) for i in range(2)]

            res_stores, ev_all, mm_all = [], [], []
            qpr_ready_vals = (srst[0].n, srst[1].n)
            qpr_ld = []
            def emit_qpr_load(q):
                war = mm_all[-1] if (q >= 2 and mm_all) else None
                spc = swpr[q % 2]
                def f(sy, q=q, war=war, spc=spc):
                    if q < 2:
                        sy.wait_ge(srst[0].sem, qpr_ready_vals[0])
                        sy.wait_ge(srst[1].sem, qpr_ready_vals[1])
                    if war is not None:
                        sy.wait_ge(sem_t, war)
                    sy.dma_start(out=qpr[q % 2][:].rearrange("p (fb n) -> p fb n", n=512),
                                 in_=d["qpr_scr"][q]).then_inc(spc.sem, 16)
                em("sync", f)
                qpr_ld.append((spc.sem, spc.inc()))

            def emit_quarter(q):
                tc_ld = []
                for tc in range(NCH):
                    war = mm_all[-3] if len(mm_all) >= 3 else None
                    shtcc = shtc[tc % 2]
                    src = d["hT_scr"][tc]
                    def f(sy, src=src, tc=tc, q=q, war=war, shtcc=shtcc):
                        if tc < 2 and q == 0:
                            sy.wait_ge(shT[0].sem, hT_ready[0])
                            sy.wait_ge(shT[1].sem, hT_ready[1])
                        if war is not None:
                            sy.wait_ge(sem_t, war)
                        sy.dma_start(out=hT_tc[tc % 2][:].rearrange("p (fb n) -> p fb n", n=256),
                                     in_=src).then_inc(shtcc.sem, 16)
                    em("sync", f)
                    tc_ld.append((shtcc.sem, shtcc.inc()))
                    if tc == 1 and q + 1 < 4:
                        emit_qpr_load(q + 1)
                    for tb2 in range(2):
                        reg = (tc % 2) * 2048 + tb2 * 512
                        row = tc * 256 + tb2 * 128
                        tl_ = tc_ld[tc]
                        war = ev_all[-4] if len(ev_all) >= 4 else None
                        qld = qpr_ld[q]
                        def f(t, tc=tc, tb2=tb2, reg=reg, tl_=tl_, war=war, q=q, qld=qld):
                            t.wait_ge(*tl_)
                            t.wait_ge(*qld)
                            if war is not None:
                                t.wait_ge(sem_v, war)
                            for fb in range(NFB):
                                lhs = hT_tc[tc % 2][:, fb*256 + tb2*128: fb*256 + (tb2+1)*128]
                                ins = t.matmul(psum[:, reg:reg+512], lhs,
                                               qpr[q % 2][:, fb*512:(fb+1)*512],
                                               start=(fb == 0), stop=(fb == NFB-1))
                            ins.then_inc(sem_t, 1)
                        em("tensor", f)
                        mm_all.append(st.inc())
                        mmv = mm_all[-1]
                        war = ev_all[-2] if len(ev_all) >= 2 else None
                        sxec = sxe[len(ev_all) % 2]
                        srsc = srst[len(ev_all) % 2]
                        def f(sy, row=row, q=q, war=war, idx=len(ev_all), sxec=sxec):
                            if war is not None:
                                sy.wait_ge(sem_v, war)
                            sy.dma_start(out=x_e[idx % 2][:],
                                         in_=d["xp"][row:row+128, q*512:(q+1)*512]).then_inc(sxec.sem, 16)
                        em("sync", f)
                        xl = sxec.inc()
                        war2 = res_stores[-2] if len(res_stores) >= 2 else None
                        def f(v, reg=reg, xl=xl, mmv=mmv, war2=war2, idx=len(ev_all), sxec=sxec, srsc=srsc):
                            v.wait_ge(sem_t, mmv)
                            v.wait_ge(sxec.sem, xl)
                            if war2 is not None:
                                v.wait_ge(srsc.sem, war2)
                            v.scalar_tensor_tensor(res_e[idx % 2][:], psum[:, reg:reg+512],
                                                   tcols[:, 5:6], x_e[idx % 2][:],
                                                   ALU.mult, ALU.add).then_inc(sem_v, 1)
                        em("vector", f)
                        ev_all.append(sv.inc())
                        evv = ev_all[-1]
                        def f(g, row=row, q=q, evv=evv, idx=len(ev_all)-1, srsc=srsc):
                            g.wait_ge(sem_v, evv)
                            g.dma_start(out=d["res_scr"][row:row+128, q*512:(q+1)*512],
                                        in_=res_e[idx % 2][:]).then_inc(srsc.sem, 16)
                        em("gpsimd", f)
                        res_stores.append(srsc.inc())

            emit_qpr_load(0)
            for q in range(4):
                emit_quarter(q)
            res_ready = (srst[0].n, srst[1].n)
        barrier()
        with ExitStack() as S5:
            E5 = S5.enter_context
            wo_all = E5(nc.sbuf_tensor("wo_all", [128, NCB * H], BF16))
            oT_t = [E5(nc.sbuf_tensor(f"oT_t{i}", [128, NCB * 256], BF16)) for i in range(2)]
            res_f = [E5(nc.sbuf_tensor(f"res_f{i}", [128, H], F32)) for i in range(2)]
            out_f = [E5(nc.sbuf_tensor(f"out_f{i}", [128, H], F32)) for i in range(2)]
            bob = E5(nc.sbuf_tensor("bob", [1, H], BF16))

            for grp in range(4):
                sl = d["wo"][grp*512:(grp+1)*512, :].rearrange("(ob p) n -> p ob n", p=128)
                def f(g, sl=sl, grp=grp):
                    g.dma_start(out=wo_all[:, grp*4*H:(grp+1)*4*H].rearrange(
                        "p (ob n) -> p ob n", n=H), in_=sl).then_inc(swo[grp].sem, 16)
                em("gpsimd", f)
                swo[grp].inc()
            def f(g):
                g.dma_start(out=bob[:], in_=d["bor"]).then_inc(sbo.sem, 16)
            em("gpsimd", f)
            sbo.inc()

            f_mm, f_ev, out_stores = [], [], []
            for tc in range(NCH):
                src = d["o_scr"][:, tc*256:(tc+1)*256].rearrange("(ob p) n -> p ob n", p=128)
                war = f_mm[2*(tc-2)+1] if tc >= 2 else None
                sotc = sot[tc % 2]
                def f(sy, src=src, tc=tc, war=war, sotc=sotc):
                    if tc < 2:
                        sy.wait_ge(sost.sem, o_ready)
                    if war is not None:
                        sy.wait_ge(sem_t, war)
                    sy.dma_start(out=oT_t[tc % 2][:].rearrange("p (ob n) -> p ob n", n=256),
                                 in_=src).then_inc(sotc.sem, 16)
                em("sync", f)
                ol = (sotc.sem, sotc.inc())
                for tb2 in range(2):
                    reg = tb2 * 2048
                    row = tc * 256 + tb2 * 128
                    idx = 2*tc + tb2
                    war = f_ev[idx-2] if idx >= 2 else None
                    def f(t, tc=tc, tb2=tb2, reg=reg, ol=ol, war=war):
                        t.wait_ge(*ol)
                        if war is not None:
                            t.wait_ge(sem_v, war)
                        for ob in range(NCB):
                            if ob % 4 == 0:
                                t.wait_ge(swo[ob // 4].sem, 16)
                            if ob == NCB-1:
                                t.wait_ge(sbo.sem, 16)
                            lhs = oT_t[tc % 2][:, ob*256 + tb2*128: ob*256 + (tb2+1)*128]
                            for cc in range(4):
                                t.matmul(psum[:, reg + cc*512: reg + (cc+1)*512], lhs,
                                         wo_all[:, ob*H + cc*512: ob*H + (cc+1)*512],
                                         start=(ob == 0), stop=False)
                        for cc in range(4):
                            ins = t.matmul(psum[:, reg + cc*512: reg + (cc+1)*512],
                                           ones_rb[0:1, :], bob[0:1, cc*512:(cc+1)*512],
                                           start=False, stop=True)
                        ins.then_inc(sem_t, 1)
                    em("tensor", f)
                    f_mm.append(st.inc())
                    war = f_ev[idx-2] if idx >= 2 else None
                    srfc = srf[idx % 2]
                    def f(sy, row=row, war=war, srfc=srfc, idx=idx):
                        if idx < 2:
                            sy.wait_ge(srst[0].sem, res_ready[0])
                            sy.wait_ge(srst[1].sem, res_ready[1])
                        if war is not None:
                            sy.wait_ge(sem_v, war)
                        sy.dma_start(out=res_f[(row//128) % 2][:],
                                     in_=d["res_scr"][row:row+128, :]).then_inc(srfc.sem, 16)
                    em("sync", f)
                    rl = (srfc.sem, srfc.inc())
                    mmv = f_mm[-1]
                    war2 = out_stores[idx-2] if idx >= 2 else None
                    sdoc = sdo[idx % 2]
                    def f(v, reg=reg, rl=rl, mmv=mmv, war2=war2, idx=idx, sdoc=sdoc):
                        v.wait_ge(sem_t, mmv)
                        v.wait_ge(*rl)
                        if war2 is not None:
                            v.wait_ge(sdoc.sem, war2)
                        v.tensor_tensor(out_f[idx % 2][:], psum[:, reg:reg+2048],
                                        res_f[idx % 2][:], ALU.add).then_inc(sem_v, 1)
                    em("vector", f)
                    f_ev.append(sv.inc())
                    evv = f_ev[-1]
                    def f(g, row=row, evv=evv, idx=idx, sdoc=sdoc):
                        g.wait_ge(sem_v, evv)
                        g.dma_start(out=d["out"][row:row+128, :],
                                    in_=out_f[idx % 2][:]).then_inc(sdoc.sem, 16)
                    em("gpsimd", f)
                    out_stores.append(sdoc.inc())

            fin = (sdo[0].n, sdo[1].n)
            def f(g, fin=fin):
                g.wait_ge(sdo[0].sem, fin[0])
                g.wait_ge(sdo[1].sem, fin[1])
            em("gpsimd", f)

        with nc.Block() as blk:
            @blk.sync
            def _(eng):
                for fn in streams["sync"]:
                    fn(eng)
            @blk.gpsimd
            def _(eng):
                for fn in streams["gpsimd"]:
                    fn(eng)
            @blk.vector
            def _(eng):
                for fn in streams["vector"]:
                    fn(eng)
            @blk.scalar
            def _(eng):
                for fn in streams["scalar"]:
                    fn(eng)
            @blk.tensor
            def _(eng):
                for fn in streams["tensor"]:
                    fn(eng)

    return nc


def kernel(**inputs):
    in_maps, gidx = _host_layout(inputs)
    nc = build_nc()
    res = run_bass_kernel_spmd(nc, in_maps, core_ids=list(range(8)))
    out = np.zeros((B, T, H), np.float32)
    for c in range(8):
        out[c // 2, gidx[c]] = np.asarray(res.results[c]["out"], np.float32)
    return out


# version 25
# speedup vs baseline: 1.0743x; 1.0743x over previous
import math
import numpy as np
import ml_dtypes
from contextlib import ExitStack

import concourse.bass as bass
import concourse.mybir as mybir
from concourse.bass_utils import run_bass_kernel_spmd

F32 = mybir.dt.float32
BF16 = mybir.dt.bfloat16
AF = mybir.ActivationFunctionType
ALU = mybir.AluOpType
AX = mybir.AxisListType

B, T, H = 4, 4096, 2048
NH, HS = 16, 128
OWN = T // 2
NTB = T // 128
NCB = H // 128
FFH = 5120
NFB = FFH // 128
SCALE = 1.0 / math.sqrt(HS)
W = 256
NCH = OWN // W


def _bf(x):
    return np.ascontiguousarray(np.asarray(x, np.float32).astype(ml_dtypes.bfloat16))


def _host_layout(inputs):
    x = np.ascontiguousarray(np.asarray(inputs["L0_attn_x"], np.float32))
    Wq = np.asarray(inputs["L0_attn_Wq"], np.float32)
    bq = np.asarray(inputs["L0_attn_bq"], np.float32)
    Wk = np.asarray(inputs["L0_attn_Wk"], np.float32)
    bk = np.asarray(inputs["L0_attn_bk"], np.float32)
    Wv = np.ascontiguousarray(np.asarray(inputs["L0_attn_Wv"], np.float32))
    bv = np.asarray(inputs["L0_attn_bv"], np.float32)
    Wo = np.ascontiguousarray(np.asarray(inputs["L0_attn_Wo"], np.float32))
    bo = np.asarray(inputs["L0_attn_bo"], np.float32)
    wfc = np.asarray(inputs["L0_ffn_w_fc"], np.float32)
    wpr = np.asarray(inputs["L0_ffn_w_proj"], np.float32)

    perm = np.concatenate([np.concatenate([np.arange(h*HS, (h+1)*HS, 2),
                                           np.arange(h*HS+1, (h+1)*HS, 2)])
                           for h in range(NH)])
    Wq_p = np.ascontiguousarray(Wq[:, perm])
    Wk_p = np.ascontiguousarray(Wk[:, perm])
    bqc = np.ascontiguousarray(bq[perm].reshape(NH, 128).T)
    bkc = np.ascontiguousarray(bk[perm].reshape(NH, 128).T)
    bvr = np.ascontiguousarray(bv.reshape(1, H))
    bor = np.ascontiguousarray(bo.reshape(1, H))
    wfcT = np.ascontiguousarray(wfc.T)
    wprT = np.ascontiguousarray(wpr.T)
    wvr = np.ascontiguousarray(Wv.reshape(16, 128, 8, 256).transpose(2, 1, 0, 3)
                               .reshape(8, 128, 4096))
    wfc3 = np.ascontiguousarray(wfc.reshape(40, 128, 16, 128).transpose(0, 3, 2, 1)
                                .reshape(40, 128, 2048))

    t = np.arange(T, dtype=np.float32)
    inv = 1.0 / 10000.0 ** (np.arange(0, HS, 2, dtype=np.float32) / HS)
    ang = np.outer(inv, t)
    cosg, sing = np.cos(ang), np.sin(ang)

    kl = np.arange(128)[:, None]
    ql = np.arange(128)[None, :]
    mA = (kl <= ql).astype(np.float32)
    mB_even = (kl < ql).astype(np.float32)

    in_maps, gidx = [], []
    for c in range(8):
        b, p = c // 2, c % 2
        own = 2 * np.arange(OWN) + p
        other = 2 * np.arange(OWN) + (1 - p)
        g = np.concatenate([own, other])
        gidx.append(own)
        cosp, sinp = cosg[:, g], sing[:, g]
        in_maps.append({
            "xp": np.ascontiguousarray(x[b][g]),
            "wq": Wq_p, "wk": Wk_p, "wvr": wvr, "wo": Wo,
            "bqc": bqc, "bkc": bkc, "bvr": bvr, "bor": bor,
            "cos2": _bf(np.concatenate([cosp, cosp], 0)),
            "sin2s": _bf(np.concatenate([-sinp, sinp], 0)),
            "mka": _bf(mA),
            "mkb": _bf(mB_even if p == 0 else mA),
            "wfcT": wfcT, "wprT": wprT, "wfc3": wfc3,
        })
    return in_maps, gidx


def build_nc():
    nc = bass.Bass("TRN2", target_bir_lowering=False)
    d = {}
    for nm, shp, dt in [("xp", [T, H], F32), ("wq", [H, H], F32), ("wk", [H, H], F32),
                        ("wvr", [8, 128, 4096], F32), ("wo", [H, H], F32),
                        ("bqc", [128, NH], F32), ("bkc", [128, NH], F32),
                        ("bvr", [1, H], F32), ("bor", [1, H], F32),
                        ("cos2", [128, T], BF16), ("sin2s", [128, T], BF16),
                        ("mka", [128, 128], BF16), ("mkb", [128, 128], BF16),
                        ("wfcT", [H, FFH], F32), ("wprT", [FFH, H], F32),
                        ("wfc3", [NFB, 128, NCB * 128], F32)]:
        d[nm] = nc.dram_tensor(nm, shp, dt, kind="ExternalInput").ap()
    d["out"] = nc.dram_tensor("out", [OWN, H], F32, kind="ExternalOutput").ap()
    d["xn_scr"] = nc.dram_tensor("xn_scr", [T, H], BF16, kind="Internal").ap()
    d["v_scr"] = nc.dram_tensor("v_scr", [T, H], BF16, kind="Internal").ap()
    d["o_scr"] = nc.dram_tensor("o_scr", [H, OWN], BF16, kind="Internal").ap()
    d["hT_scr"] = nc.dram_tensor("hT_scr", [NCH, 128, NFB, 256], BF16, kind="Internal").ap()
    d["res_scr"] = nc.dram_tensor("res_scr", [OWN, H], F32, kind="Internal").ap()
    d["qpr_scr"] = nc.dram_tensor("qpr_scr", [NFB, 4, 128, 512], BF16, kind="Internal").ap()

    streams = {"sync": [], "gpsimd": [], "vector": [], "scalar": [], "tensor": []}

    def em(name, fn):
        streams[name].append(fn)

    class Ctr:
        def __init__(self, sem, step=1):
            self.sem, self.step, self.n = sem, step, 0
        def inc(self):
            self.n += self.step
            return self.n

    with ExitStack() as S0:
        E0 = S0.enter_context
        psum = E0(nc.psum_tensor("psum", [128, 4096], F32))
        ones_cb = E0(nc.sbuf_tensor("ones_cb", [128, 128], BF16))
        ones_cf = E0(nc.sbuf_tensor("ones_cf", [128, 1], F32))
        ones_rf = E0(nc.sbuf_tensor("ones_rf", [1, 128], F32))
        ones_rb = E0(nc.sbuf_tensor("ones_rb", [1, 128], BF16))
        bqc_s = E0(nc.sbuf_tensor("bqc_s", [128, NH], F32))
        bkc_s = E0(nc.sbuf_tensor("bkc_s", [128, NH], F32))
        tcols = E0(nc.sbuf_tensor("tcols", [128, 6], F32))
        xnT_stack = ExitStack()
        xnT = xnT_stack.enter_context(nc.sbuf_tensor("xnT", [128, NCB * T], BF16, side="right", is_transpose_dest=True))
        mka_s = E0(nc.sbuf_tensor("mka_s", [128, 128], BF16))
        mkb_s = E0(nc.sbuf_tensor("mkb_s", [128, 128], BF16))
        am11 = E0(nc.sbuf_tensor("am11", [1, 8], F32))
        wqk_stack = ExitStack()
        wqk_q = wqk_stack.enter_context(nc.sbuf_tensor("wqk_q", [128, H], BF16, side="right"))
        wqk_k = wqk_stack.enter_context(nc.sbuf_tensor("wqk_k", [128, H], BF16, side="right"))
        eps_c = E0(nc.sbuf_tensor("eps_c", [128, 1], F32))
        eps2_c = E0(nc.sbuf_tensor("eps2_c", [128, 1], F32))
        lnh_c = E0(nc.sbuf_tensor("lnh_c", [128, 1], F32))

        sem_v = E0(nc.semaphore("sv"))
        sem_s = E0(nc.semaphore("ss"))
        sem_t = E0(nc.semaphore("st"))
        sv, ss, st = Ctr(sem_v), Ctr(sem_s), Ctr(sem_t)

        ALL_DSEMS = []

        def dsem(name):
            c = Ctr(E0(nc.semaphore(name)), 16)
            ALL_DSEMS.append(c)
            return c

        sx = [dsem(f"sx{i}") for i in range(3)]
        sxn = [dsem("sxn0"), dsem("sxn1")]
        scst = dsem("scst")
        sbv = dsem("sbv")
        sxnt = [dsem(f"sxnt{i}") for i in range(4)]
        swv = [dsem(f"swv{i}") for i in range(3)]
        svst = [dsem(f"svst{i}") for i in range(4)]
        stab = dsem("stab")
        swq = dsem("swq")
        swk = dsem("swk")
        sstat = [dsem("sstat0"), dsem("sstat1"), dsem("sstat2"), dsem("sstat3")]
        svh = dsem("svh")
        ssw = dsem("ssw")
        sost = dsem("sost")
        swfc = [dsem("swfc0"), dsem("swfc1")]
        shT = [dsem("shT0"), dsem("shT1")]
        swpr = [dsem("swpr0"), dsem("swpr1")]
        shtc = [dsem("shtc0"), dsem("shtc1")]
        sxe = [dsem("sxe0"), dsem("sxe1")]
        srst = [dsem("srst0"), dsem("srst1")]
        swo = [dsem(f"swo{i}") for i in range(4)]
        sbo = dsem("sbo")
        sot = [dsem("sot0"), dsem("sot1")]
        srf = [dsem("srf0"), dsem("srf1")]
        sdo = [dsem("sdo0"), dsem("sdo1")]

        def barrier():
            vals = tuple((c.sem, c.n) for c in (ALL_DSEMS + [sv, ss, st]) if c.n)
            for e in ("sync", "gpsimd", "vector", "scalar", "tensor"):
                def f(eng, vals=vals):
                    for s, v in vals:
                        eng.wait_ge(s, v)
                em(e, f)

        def f(v):
            v.memset(ones_cb[:], 1.0)
            v.memset(ones_cf[:], 1.0)
            v.memset(ones_rf[:], 1.0)
            v.memset(ones_rb[:], 1.0)
            v.memset(eps_c[:], 1e-5)
            v.memset(eps2_c[:], 41.94304)
            v.memset(lnh_c[:], 7.6246189861593985).then_inc(sem_v, 1)
        em("vector", f)
        const_v = sv.inc()
        def f(sy):
            sy.dma_start(out=bqc_s[:], in_=d["bqc"]).then_inc(scst.sem, 16)
            sy.dma_start(out=bkc_s[:], in_=d["bkc"]).then_inc(scst.sem, 16)
            sy.dma_start(out=mka_s[:], in_=d["mka"]).then_inc(scst.sem, 16)
            sy.dma_start(out=mkb_s[:], in_=d["mkb"]).then_inc(scst.sem, 16)
        em("sync", f)
        for _ in range(4):
            scst.inc()

        with ExitStack() as S1:
            E1 = S1.enter_context
            x_t = [E1(nc.sbuf_tensor(f"x_t{i}", [128, H], F32)) for i in range(3)]
            xn_t = [E1(nc.sbuf_tensor(f"xn_t{i}", [128, H], BF16)) for i in range(2)]
            sums = E1(nc.sbuf_tensor("sums", [128, NTB], F32))
            sumsq = E1(nc.sbuf_tensor("sumsq", [128, NTB], F32))
            lnv = E1(nc.sbuf_tensor("lnv", [128, NTB], F32))
            rstd = E1(nc.sbuf_tensor("rstd", [128, NTB], F32))
            negmr = E1(nc.sbuf_tensor("negmr", [128, NTB], F32))
            bvb = E1(nc.sbuf_tensor("bvb", [128, H], BF16))
            bvr_s = E1(nc.sbuf_tensor("bvr_s", [1, H], BF16))
            wv_sl = [E1(nc.sbuf_tensor(f"wv_sl{i}", [128, NCB * 256], BF16)) for i in range(3)]
            v_ev = [E1(nc.sbuf_tensor(f"v_ev{i}", [128, 256], BF16)) for i in range(4)]

            def f(g):
                g.dma_start(out=bvr_s[:], in_=d["bvr"]).then_inc(sbv.sem, 16)
            em("gpsimd", f)
            bvl = sbv.inc()
            def f(t, bvl=bvl, const_v=const_v):
                t.wait_ge(sbv.sem, bvl)
                t.wait_ge(sem_v, const_v)
                for j in range(4):
                    ins = t.matmul(psum[:, 2048+j*512:2048+(j+1)*512], ones_rb[0:1, :],
                                   bvr_s[0:1, j*512:(j+1)*512], start=True, stop=True)
                ins.then_inc(sem_t, 1)
            em("tensor", f); bvt = st.inc()
            def f(v, bvt=bvt):
                v.wait_ge(sem_t, bvt)
                v.tensor_copy(bvb[:], psum[:, 2048:4096]).then_inc(sem_v, 1)
            em("vector", f); bvb_ready = sv.inc()

            wv_load, wv_last_mm, vproj_mm, vproj_ev, v_store = [], [], [], [], []
            def plan_wv_load(li):
                dch = li % 8
                buf = wv_sl[li % 3]
                war = wv_last_mm[li - 3] if li >= 3 else None
                swvc = swv[li % 3]
                def f(g, dch=dch, buf=buf, war=war, swvc=swvc):
                    if war is not None:
                        g.wait_ge(sem_t, war)
                    g.dma_start(out=buf[:], in_=d["wvr"][dch]).then_inc(swvc.sem, 16)
                em("gpsimd", f)
                wv_load.append((swvc.sem, swvc.inc()))
            plan_wv_load(0)
            plan_wv_load(1)
            plan_wv_load(2)

            def plan_v_mm(idx):
                half, rem = idx // 128, idx % 128
                dch, tb = rem // 16, 16 * (idx // 128) + rem % 16
                li = idx // 16
                bank = (idx % 4) * 512
                lw = wv_load[li]
                qrd = xnq_ready[tb // 8]
                war = vproj_ev[idx - 4] if idx >= 4 else None
                def f(t, li=li, tb=tb, bank=bank, lw=lw, qrd=qrd, war=war):
                    t.wait_ge(*lw)
                    t.wait_ge(*qrd)
                    if war is not None:
                        t.wait_ge(sem_v, war)
                    for cb in range(NCB):
                        ins = t.matmul(psum[:, bank:bank+256],
                                       xnT[:, cb*T + tb*128: cb*T + (tb+1)*128],
                                       wv_sl[li % 3][:, cb*256:(cb+1)*256],
                                       start=(cb == 0), stop=(cb == NCB-1))
                    ins.then_inc(sem_t, 1)
                em("tensor", f)
                vproj_mm.append(st.inc())
                if idx % 16 == 15:
                    wv_last_mm.append(vproj_mm[-1])

            def plan_v_ev(idx):
                dch = (idx % 128) // 16
                bank = (idx % 4) * 512
                mmw = vproj_mm[idx]
                war = v_store[idx - 4] if idx >= 4 else None
                svc = svst[idx % 4]
                def f(v, dch=dch, bank=bank, idx=idx, mmw=mmw, war=war, svc=svc):
                    v.wait_ge(sem_t, mmw)
                    v.wait_ge(sem_v, bvb_ready)
                    if war is not None:
                        v.wait_ge(svc.sem, war)
                    v.tensor_tensor(v_ev[idx % 4][:], psum[:, bank:bank+256],
                                    bvb[:, dch*256:(dch+1)*256], ALU.add).then_inc(sem_v, 1)
                em("vector", f)
                vproj_ev.append(sv.inc())

            def plan_v_st(idx):
                dch = (idx % 128) // 16
                tb = 16 * (idx // 128) + idx % 16
                evw = vproj_ev[idx]
                svc = svst[idx % 4]
                def f(g, dch=dch, tb=tb, idx=idx, evw=evw, svc=svc):
                    g.wait_ge(sem_v, evw)
                    g.dma_start(out=d["v_scr"][tb*128:(tb+1)*128, dch*256:(dch+1)*256],
                                in_=v_ev[idx % 4][:]).then_inc(svc.sem, 16)
                em("gpsimd", f)
                v_store.append(svc.inc())

            vp = {"ld": 3, "mm": 0, "ev": 0, "st": 0}
            NV = 256
            def plan_v(tb):
                qdone = min((tb + 1) // 8, 4)
                tmm = max(0, int((tb - 9) * 6.3))
                tev = max(0, int((tb - 10) * 6.3))
                tld = min(16, 3 + max(0, int((tb - 8) * 0.6)))
                while True:
                    did = False
                    if vp["ld"] < tld and len(wv_last_mm) >= vp["ld"] - 2:
                        plan_wv_load(vp["ld"]); vp["ld"] += 1; did = True
                    if vp["mm"] < tmm:
                        idx = vp["mm"]
                        if (idx // 16 < len(wv_load)
                                and (16 * (idx // 128) + idx % 16) // 8 < len(xnq_ready)
                                and (idx < 4 or idx - 4 < len(vproj_ev))):
                            plan_v_mm(idx); vp["mm"] += 1; did = True
                    if (vp["ev"] < min(tev, len(vproj_mm))
                            and (vp["ev"] < 4 or vp["ev"] - 4 < len(v_store))):
                        plan_v_ev(vp["ev"]); vp["ev"] += 1; did = True
                    if vp["st"] < len(vproj_ev):
                        plan_v_st(vp["st"]); vp["st"] += 1; did = True
                    if not did:
                        break

            sl = d["wq"][:, 0:128].rearrange("(cb p) n -> p cb n", p=128)
            def f(g, sl=sl):
                g.dma_start(out=wqk_q[:].rearrange("p (cb n) -> p cb n", n=128),
                            in_=sl).then_inc(swq.sem, 16)
            em("gpsimd", f)
            wq0_ld = swq.inc()
            sl = d["wk"][:, 0:128].rearrange("(cb p) n -> p cb n", p=128)
            def f(g, sl=sl):
                g.dma_start(out=wqk_k[:].rearrange("p (cb n) -> p cb n", n=128),
                            in_=sl).then_inc(swk.sem, 16)
            em("gpsimd", f)
            wk0_ld = swk.inc()

            LNH = float(H)
            xn_act, xn_store, xnq_ready = [], [], []
            for tb in range(NTB):
                buf = x_t[tb % 3]
                war = xn_act[tb-3] if tb >= 3 else None
                sxc = sx[tb % 3]
                def f(sy, tb=tb, buf=buf, war=war, sxc=sxc):
                    if war is not None:
                        sy.wait_ge(sem_s, war)
                    sy.dma_start(out=buf[:], in_=d["xp"][tb*128:(tb+1)*128, :]).then_inc(sxc.sem, 16)
                em("sync", f)
                ld = sxc.inc()
                def f(v, buf=buf, ld=ld, tb=tb, sxc=sxc):
                    v.wait_ge(sxc.sem, ld)
                    v.reduce_sum(sums[:, tb:tb+1], buf[:], axis=AX.X).then_inc(sem_v, 1)
                em("vector", f)
                vred = sv.inc()
                def f(s, buf=buf, ld=ld, tb=tb, sxc=sxc):
                    s.wait_ge(sxc.sem, ld)
                    if tb == 0:
                        s.wait_ge(sem_v, bvb_ready)
                    s.activation(psum[:, 2048:4096], buf[:], AF.Square,
                                 accum_out=sumsq[:, tb:tb+1]).then_inc(sem_s, 1)
                em("scalar", f)
                sred = ss.inc()
                def f(v, tb=tb, vred=vred):
                    v.wait_ge(sem_v, vred)
                    v.tensor_tensor(lnv[:, tb:tb+1], sums[:, tb:tb+1], sums[:, tb:tb+1],
                                    ALU.mult).then_inc(sem_v, 1)
                em("vector", f)
                vt1 = sv.inc()
                def f(v, tb=tb, vt1=vt1, sred=sred):
                    v.wait_ge(sem_v, vt1)
                    v.wait_ge(sem_s, sred)
                    v.scalar_tensor_tensor(sumsq[:, tb:tb+1], sumsq[:, tb:tb+1], LNH,
                                           lnv[:, tb:tb+1], ALU.mult,
                                           ALU.subtract).then_inc(sem_v, 1)
                em("vector", f)
                vX = sv.inc()
                def f(s, tb=tb, vX=vX):
                    s.wait_ge(sem_v, vX)
                    s.activation(lnv[:, tb:tb+1], sumsq[:, tb:tb+1], AF.Ln,
                                 bias=eps2_c[:, 0:1]).then_inc(sem_s, 1)
                em("scalar", f)
                sln = ss.inc()
                def f(s, tb=tb, sln=sln):
                    s.wait_ge(sem_s, sln)
                    s.activation(rstd[:, tb:tb+1], lnv[:, tb:tb+1], AF.Exp, scale=-0.5,
                                 bias=lnh_c[:, 0:1]).then_inc(sem_s, 1)
                em("scalar", f)
                srs = ss.inc()
                def f(v, tb=tb, srs=srs):
                    v.wait_ge(sem_s, srs)
                    v.scalar_tensor_tensor(negmr[:, tb:tb+1], sums[:, tb:tb+1], -1.0/LNH,
                                           rstd[:, tb:tb+1], ALU.mult,
                                           ALU.mult).then_inc(sem_v, 1)
                em("vector", f)
                vng = sv.inc()
                sxnc = sxn[tb % 2]
                war2 = xn_store[tb-2] if tb >= 2 else None
                def f(s, tb=tb, buf=buf, vng=vng, war2=war2, sxnc=sxnc):
                    s.wait_ge(sem_v, vng)
                    if war2 is not None:
                        s.wait_ge(sxnc.sem, war2)
                    s.activation(xn_t[tb % 2][:], buf[:], AF.Identity,
                                 bias=negmr[:, tb:tb+1], scale=rstd[:, tb:tb+1]).then_inc(sem_s, 1)
                em("scalar", f)
                xn_act.append(ss.inc())
                act = xn_act[tb]
                def f(g, tb=tb, act=act, sxnc=sxnc):
                    g.wait_ge(sem_s, act)
                    g.dma_start(out=d["xn_scr"][tb*128:(tb+1)*128, :],
                                in_=xn_t[tb % 2][:]).then_inc(sxnc.sem, 16)
                em("gpsimd", f)
                xn_store.append(sxnc.inc())
                if (tb + 1) % 8 == 0:
                    qt = tb // 8
                    snap = (sxn[0].n, sxn[1].n)
                    sxq = sxnt[qt]
                    def f(sy, qt=qt, snap=snap, sxq=sxq):
                        sy.wait_ge(sxn[0].sem, snap[0])
                        sy.wait_ge(sxn[1].sem, snap[1])
                        for cb in range(NCB):
                            sy.dma_start(out=xnT[:, cb*T + qt*1024: cb*T + (qt+1)*1024],
                                         in_=d["xn_scr"][qt*1024:(qt+1)*1024,
                                                         cb*128:(cb+1)*128],
                                         transpose=True).then_inc(sxq.sem, 16)
                    em("sync", f)
                    for _ in range(NCB):
                        sxq.inc()
                    xnq_ready.append((sxq.sem, sxq.n))
                plan_v(tb)

            xnT_ready = list(xnq_ready)

            while vp["mm"] < NV or vp["ev"] < NV or vp["st"] < NV or vp["ld"] < 16:
                plan_v(999)
            v_ready = tuple((c.sem, c.n) for c in svst)


        barrier()
        with ExitStack() as S2:
            E2 = S2.enter_context
            expT = E2(nc.sbuf_tensor("expT", [128, 32 * W], BF16))
            cos_s = E2(nc.sbuf_tensor("cos_s", [128, T], BF16))
            sin_s = E2(nc.sbuf_tensor("sin_s", [128, T], BF16))
            qrotT = E2(nc.sbuf_tensor("qrotT", [128, OWN], BF16))
            krotT = E2(nc.sbuf_tensor("krotT", [128, T], BF16))
            v_head = E2(nc.sbuf_tensor("v_head", [128, 32 * 128], BF16))

            def f(sy):
                sy.dma_start(out=cos_s[:], in_=d["cos2"]).then_inc(stab.sem, 16)
                sy.dma_start(out=sin_s[:], in_=d["sin2s"]).then_inc(stab.sem, 16)
            em("sync", f)
            stab.inc()
            tables_ready = stab.inc()
            stb = [E2(nc.sbuf_tensor(f"stb{i}", [128, 512], F32)) for i in range(3)]
            fcr = E2(nc.sbuf_tensor("fcr", [128, 160], F32))
            prr = E2(nc.sbuf_tensor("prr", [128, 160], F32))
            fsum2 = E2(nc.sbuf_tensor("fsum2", [128, 2], F32))
            dmp2 = E2(nc.sbuf_tensor("dmp2", [128, 512], BF16))
            qe_b = E2(nc.sbuf_tensor("qe_b", [128, 1024], BF16))
            qsw_b = E2(nc.sbuf_tensor("qsw_b", [128, 1024], BF16))
            rt1 = E2(nc.sbuf_tensor("rt1", [128, 1024], BF16))
            rt2 = E2(nc.sbuf_tensor("rt2", [128, 1024], BF16))
            rb_sb = E2(nc.sbuf_tensor("rb_sb", [128, W], F32))
            o_sb = [E2(nc.sbuf_tensor("o_sb0", [128, W], BF16))] * 2

            REG = [0, 1024, 3072]
            DF0, RB0, OT0 = 2048, 2304, 2560
            QKB = 3072
            GROUPS = [list(range(4*g, 4*g+4)) for g in range(8)]

            wfc_flat = d["wfcT"].rearrange("a b -> (a b)")
            wpr_flat = d["wprT"].rearrange("a b -> (a b)")
            stat_load_done = [None] * 320
            stat_red_done = [None] * 320

            def emit_stat_load(i):
                buf = stb[i % 3]
                src = wfc_flat if i < 160 else wpr_flat
                j = i if i < 160 else i - 160
                sl = src[j * 65536:(j + 1) * 65536].rearrange("(p m) -> p m", p=128)
                sstc = sstat[i % 3]
                def f(sy, sl=sl, buf=buf, i=i, sstc=sstc):
                    if i >= 3:
                        war = stat_red_done[i - 3]
                        sy.wait_ge(war[0], war[1])
                    sy.dma_start(out=buf[:], in_=sl).then_inc(sstc.sem, 16)
                em("sync", f)
                stat_load_done[i] = (sstc.sem, sstc.inc())

            def emit_stat_red(i):
                buf = stb[i % 3]
                ld = stat_load_done[i]
                if i < 160:
                    def f(v, buf=buf, ld=ld, i=i):
                        v.wait_ge(*ld)
                        v.reduce_sum(fcr[:, i:i+1], buf[:], axis=AX.X,
                                     apply_absolute_value=True).then_inc(sem_v, 1)
                    em("vector", f)
                    stat_red_done[i] = (sem_v, sv.inc())
                else:
                    pa = stat_red_done[i-1] if (i > 160 and stat_red_done[i-1] is not None
                                                and stat_red_done[i-1][0] is sem_s) else None
                    def f(s, buf=buf, ld=ld, i=i, pa=pa):
                        s.wait_ge(*ld)
                        if pa is not None:
                            s.wait_ge(*pa)
                        s.activation(dmp2[:], buf[:], AF.Abs,
                                     accum_out=prr[:, i-160:i-159]).then_inc(sem_s, 1)
                    em("scalar", f)
                    stat_red_done[i] = (sem_s, ss.inc())

            HEAD_MIL = []
            PENDING_WQK = {}
            P3 = [0]
            LASTREG = [None]
            REGION_LAST_EXP = {}
            last_c_exp = [None]

            def plan_head(h):
                mil = {}
                prev = HEAD_MIL[h-1] if h > 0 else None

                if h == 0:
                    mil["wq_ld"] = wq0_ld
                    mil["wk_ld"] = wk0_ld
                else:
                    mil["wq_ld"], mil["wk_ld"] = PENDING_WQK[h]
                for k in range(20*h, 20*h+20):
                    emit_stat_load(k)
                RQ = list(range(20*h, 20*h+20))
                def reds_until(done_groups, RQ=RQ):
                    tgt = (done_groups * 20) // 64
                    while (20 - len(RQ)) < tgt:
                        emit_stat_red(RQ.pop(0))

                src = d["v_scr"][:, h*128:(h+1)*128].rearrange("(kb p) n -> p kb n", p=128)
                def f(g, src=src, prev=prev):
                    if prev is not None:
                        g.wait_ge(sem_t, prev["t_last"])
                    else:
                        for _s, _v in v_ready:
                            g.wait_ge(_s, _v)
                    g.dma_start(out=v_head[:].rearrange("p (kb n) -> p kb n", n=128),
                                in_=src).then_inc(svh.sem, 16)
                em("gpsimd", f)
                mil["v_ld"] = svh.inc()

                qkb_war = last_c_exp[0]
                war3 = prev["t_last"] if prev else None
                qev = []
                rope_piece = []
                for j in range(4):
                    bank = QKB + (j % 2) * 512
                    wq_ld = mil["wq_ld"]
                    war = qev[j-2] if j >= 2 else (prev["kev_last"] if prev else None)
                    def f(t, j=j, bank=bank, wq_ld=wq_ld, war=war, h=h, qkb_war=qkb_war):
                        t.wait_ge(swq.sem, wq_ld)
                        if h == 0 and j == 0:
                            for _sw, _sv in xnT_ready:
                                t.wait_ge(_sw, _sv)
                        if j == 0 and qkb_war is not None:
                            t.wait_ge(sem_s, qkb_war)
                        if war is not None:
                            t.wait_ge(sem_s, war)
                        for cb in range(NCB):
                            ins = t.matmul(psum[:, bank:bank+512],
                                           wqk_q[:, cb*128:(cb+1)*128],
                                           xnT[:, cb*T + j*512: cb*T + (j+1)*512],
                                           start=(cb == 0), stop=(cb == NCB-1))
                            if cb == NCB-1:
                                ins.then_inc(sem_t, 1)
                    em("tensor", f)
                    mm = st.inc()
                    war2 = rope_piece[j//2 - 1] if j >= 2 else (prev["ropek1"] if prev else None)
                    def f(s, j=j, bank=bank, mm=mm, h=h, war2=war2):
                        s.wait_ge(sem_t, mm)
                        if j % 2 == 0 and war2 is not None:
                            s.wait_ge(sem_v, war2)
                        s.activation(qe_b[:, (j % 2)*512:(j % 2)*512+512], psum[:, bank:bank+512],
                                     AF.Identity, bias=bqc_s[:, h:h+1]).then_inc(sem_s, 1)
                    em("scalar", f)
                    qev.append(ss.inc())
                    if j % 2 == 1:
                        p = j // 2
                        ev2 = qev[-1]
                        def f(g, ev2=ev2):
                            g.wait_ge(sem_s, ev2)
                            g.dma_start(out=qsw_b[0:64, :], in_=qe_b[64:128, :]).then_inc(ssw.sem, 16)
                            g.dma_start(out=qsw_b[64:128, :], in_=qe_b[0:64, :]).then_inc(ssw.sem, 16)
                        em("gpsimd", f)
                        ssw.inc()
                        swp = ssw.inc()
                        def f(v, swp=swp, p=p, h=h, war3=war3):
                            v.wait_ge(ssw.sem, swp)
                            if p == 0 and h == 0:
                                v.wait_ge(stab.sem, tables_ready)
                            v.tensor_tensor(rt1[:], qe_b[:], cos_s[:, p*1024:(p+1)*1024], ALU.mult)
                            v.tensor_tensor(rt2[:], qsw_b[:], sin_s[:, p*1024:(p+1)*1024],
                                            ALU.mult).then_inc(sem_v, 1)
                        em("vector", f)
                        eq = sv.inc()
                        def f(v, eq=eq, war3=war3, p=p):
                            v.wait_ge(sem_v, eq)
                            if p == 0 and war3 is not None:
                                v.wait_ge(sem_t, war3)
                            v.tensor_tensor(qrotT[:, p*1024:(p+1)*1024], rt1[:], rt2[:],
                                            ALU.add).then_inc(sem_v, 1)
                        em("vector", f)
                        rope_piece.append(sv.inc())
                mil["qproj_mm"] = st.n
                mil["ropeq"] = rope_piece[-1]

                kev_prev = qev
                krope_piece = list(rope_piece)
                for kh in range(2):
                    kev = []
                    for j in range(4):
                        bank = QKB + (j % 2) * 512
                        tcol = kh * 2048 + j * 512
                        wk_ld = mil["wk_ld"]
                        war = kev[j-2] if j >= 2 else kev_prev[2+j]
                        def f(t, bank=bank, tcol=tcol, wk_ld=wk_ld, war=war):
                            t.wait_ge(swk.sem, wk_ld)
                            t.wait_ge(sem_s, war)
                            for cb in range(NCB):
                                ins = t.matmul(psum[:, bank:bank+512],
                                               wqk_k[:, cb*128:(cb+1)*128],
                                               xnT[:, cb*T + tcol: cb*T + tcol + 512],
                                               start=(cb == 0), stop=(cb == NCB-1))
                                if cb == NCB-1:
                                    ins.then_inc(sem_t, 1)
                        em("tensor", f)
                        mm = st.inc()
                        war2 = krope_piece[-1] if j % 2 == 0 else None
                        def f(s, j=j, bank=bank, mm=mm, h=h, war2=war2):
                            s.wait_ge(sem_t, mm)
                            if war2 is not None:
                                s.wait_ge(sem_v, war2)
                            s.activation(qe_b[:, (j % 2)*512:(j % 2)*512+512],
                                         psum[:, bank:bank+512],
                                         AF.Identity, bias=bkc_s[:, h:h+1]).then_inc(sem_s, 1)
                        em("scalar", f)
                        kev.append(ss.inc())
                        if j % 2 == 1:
                            p = j // 2
                            off = kh*2048 + p*1024
                            ev2 = kev[-1]
                            def f(g, ev2=ev2):
                                g.wait_ge(sem_s, ev2)
                                g.dma_start(out=qsw_b[0:64, :], in_=qe_b[64:128, :]).then_inc(ssw.sem, 16)
                                g.dma_start(out=qsw_b[64:128, :], in_=qe_b[0:64, :]).then_inc(ssw.sem, 16)
                            em("gpsimd", f)
                            ssw.inc()
                            swp = ssw.inc()
                            def f(v, swp=swp, off=off):
                                v.wait_ge(ssw.sem, swp)
                                v.tensor_tensor(rt1[:], qe_b[:], cos_s[:, off:off+1024], ALU.mult)
                                v.tensor_tensor(rt2[:], qsw_b[:], sin_s[:, off:off+1024],
                                                ALU.mult).then_inc(sem_v, 1)
                            em("vector", f)
                            ek = sv.inc()
                            war4 = prev["t_last"] if (prev and kh == 0 and p == 0) else None
                            def f(v, ek=ek, war4=war4, off=off):
                                v.wait_ge(sem_v, ek)
                                if war4 is not None:
                                    v.wait_ge(sem_t, war4)
                                v.tensor_tensor(krotT[:, off:off+1024], rt1[:], rt2[:],
                                                ALU.add).then_inc(sem_v, 1)
                            em("vector", f)
                            krope_piece.append(sv.inc())
                    kev_prev = kev
                    mil["ropek%d" % kh] = krope_piece[-1]
                mil["kev_last"] = kev_prev[3]
                mil["kproj_mm"] = st.n

                NG = 8
                exp_done = [[None]*NG for _ in range(NCH)]
                ones_done = [[None, None] for _ in range(NCH)]
                mask_done = [[None, None] for _ in range(NCH)]
                recip_done = [None] * NCH
                av_done = [[None, None] for _ in range(NCH)]
                evac_done = [None] * NCH
                o_stores = [None] * NCH

                def emit_group(c, gi):
                    lg = c * 8 + gi
                    if lg < 58:
                        reg = REG[P3[0] % 3]
                        P3[0] += 1
                        if reg == LASTREG[0]:
                            reg = REG[P3[0] % 3]
                            P3[0] += 1
                    else:
                        reg = REG[0] if LASTREG[0] != REG[0] else REG[1]
                    LASTREG[0] = reg
                    grp = GROUPS[gi]
                    war = REGION_LAST_EXP.get(reg)
                    rk = mil["ropek0"] if gi < 4 else mil["ropek1"]
                    kevw = mil["kev_last"] if reg == QKB else None
                    def f(t, c=c, reg=reg, grp=tuple(grp), war=war, rk=rk, kevw=kevw):
                        if war is not None:
                            t.wait_ge(sem_s, war)
                        if kevw is not None:
                            t.wait_ge(sem_s, kevw)
                        t.wait_ge(sem_v, rk)
                        for kb in grp:
                            pos = kb - grp[0]
                            ins = t.matmul(psum[:, reg+pos*256: reg+pos*256+256],
                                           krotT[:, kb*128:(kb+1)*128],
                                           qrotT[:, c*W:(c+1)*W],
                                           start=(pos % 2 == 0), stop=(pos % 2 == 1))
                            if kb == grp[-1]:
                                ins.then_inc(sem_t, 1)
                    em("tensor", f)
                    gmm = st.inc()
                    g0 = GROUPS[gi][0]
                    half = 0 if gi < NG // 2 else 1
                    if c >= 1:
                        free_war = ("avh", c - 1, half)
                    elif prev is not None:
                        free_war = ("pt", prev["t_last"], None)
                    else:
                        free_war = None
                    def f(s, reg=reg, g0=g0, gmm=gmm, free_war=free_war):
                        s.wait_ge(sem_t, gmm)
                        if free_war is not None:
                            if free_war[0] == "avh":
                                s.wait_ge(sem_t, av_done[free_war[1]][free_war[2]])
                            else:
                                s.wait_ge(sem_t, free_war[1])
                        s.activation(expT[:, g0*W:(g0+4)*W], psum[:, reg:reg+1024],
                                     AF.Exp, scale=SCALE).then_inc(sem_s, 1)
                    em("scalar", f)
                    ev = ss.inc()
                    exp_done[c][gi] = ev
                    REGION_LAST_EXP[reg] = ev
                    if reg == REG[2]:
                        last_c_exp[0] = ev

                def emit_ones(c, half):
                    kbs = range(16) if half == 0 else range(16, 32)
                    war = (recip_done[c-1] if c >= 1 else
                           (prev["recip_last"] if prev else None)) if half == 0 else None
                    lastexp = exp_done[c][3 if half == 0 else 7]
                    def f(t, kbs=tuple(kbs), war=war, half=half, lastexp=lastexp):
                        t.wait_ge(sem_s, lastexp)
                        if war is not None:
                            t.wait_ge(sem_v, war)
                        for kb in kbs:
                            ins = t.matmul(psum[:, DF0:DF0+W], ones_cb[:],
                                           expT[:, kb*W:(kb+1)*W],
                                           start=(kb == 0), stop=(kb == 31))
                            if kb == kbs[-1]:
                                ins.then_inc(sem_t, 1)
                    em("tensor", f)
                    ones_done[c][half] = st.inc()

                def emit_masks(c, half):
                    j0, j1 = 2*c, 2*c+1
                    if half == 0:
                        slices = [(j0*W, j0*W+128, mka_s), (j1*W+128, j1*W+256, mka_s)]
                    else:
                        slices = [((16+j0)*W, (16+j0)*W+128, mkb_s),
                                  ((16+j1)*W+128, (16+j1)*W+256, mkb_s)]
                    def f(v, c=c, half=half, slices=tuple(slices)):
                        v.wait_ge(sem_t, ones_done[c][half])
                        for a, b, mk in slices:
                            ins = v.tensor_tensor(expT[:, a:b], expT[:, a:b], mk[:], ALU.mult)
                        ins.then_inc(sem_v, 1)
                    em("vector", f)
                    mask_done[c][half] = sv.inc()

                def emit_av(c, half):
                    valid = [kb for kb in (range(16) if half == 0 else range(16, 32))
                             if (kb % 16) <= 2*c+1]
                    vld = mil["v_ld"]
                    war = (evac_done[c-1] if c >= 1 else
                           (prev["evac_last"] if prev else None)) if half == 0 else None
                    def f(t, c=c, half=half, valid=tuple(valid), vld=vld, war=war):
                        t.wait_ge(sem_v, mask_done[c][half])
                        t.wait_ge(svh.sem, vld)
                        if war is not None:
                            t.wait_ge(sem_v, war)
                        for i, kb in enumerate(valid):
                            kj = kb % 16
                            cs = 128 if kj == 2*c+1 else 0
                            ins = t.matmul(psum[:, OT0+cs:OT0+256],
                                           v_head[:, kb*128:(kb+1)*128],
                                           expT[:, kb*W+cs:(kb+1)*W],
                                           start=(half == 0 and i == 0),
                                           stop=(half == 1 and i == len(valid)-1))
                            if i == len(valid)-1:
                                ins.then_inc(sem_t, 1)
                    em("tensor", f)
                    av_done[c][half] = st.inc()

                def emit_recip(c):
                    war = evac_done[c-1] if c >= 1 else (prev["evac_last"] if prev else None)
                    def f(v, c=c, war=war):
                        v.wait_ge(sem_t, ones_done[c][1])
                        if war is not None:
                            v.wait_ge(sem_v, war)
                        v.reciprocal(rb_sb[:], psum[:, DF0:DF0+W]).then_inc(sem_v, 1)
                    em("vector", f)
                    recip_done[c] = sv.inc()

                def emit_bcast(c):
                    war = o_stores[c-1] if c >= 1 else (prev["o_last"] if prev else None)
                    def f(v, c=c, war=war):
                        v.wait_ge(sem_v, recip_done[c])
                        v.wait_ge(sem_t, av_done[c][1])
                        if war is not None:
                            v.wait_ge(sost.sem, war)
                        v.tensor_tensor(o_sb[c % 2][:], psum[:, OT0:OT0+W], rb_sb[:],
                                        ALU.mult).then_inc(sem_v, 1)
                    em("vector", f)
                    evac_done[c] = sv.inc()
                    def f(g, c=c, h=h):
                        g.wait_ge(sem_v, evac_done[c])
                        g.dma_start(out=d["o_scr"][h*128:(h+1)*128, c*W:(c+1)*W],
                                    in_=o_sb[c % 2][:]).then_inc(sost.sem, 16)
                    em("gpsimd", f)
                    o_stores[c] = sost.inc()

                def emit_p2(c):
                    emit_ones(c, 0)
                    emit_masks(c, 0)
                    emit_av(c, 0)
                    emit_ones(c, 1)
                    emit_masks(c, 1)
                    emit_recip(c)
                    emit_av(c, 1)
                    emit_bcast(c)

                for c in range(NCH):
                    if c >= 1:
                        emit_ones(c-1, 0)
                        emit_masks(c-1, 0)
                        emit_av(c-1, 0)
                    emit_group(c, 0)
                    emit_group(c, 1)
                    if c >= 1:
                        emit_ones(c-1, 1)
                        emit_masks(c-1, 1)
                        emit_recip(c-1)
                    emit_group(c, 2)
                    reds_until(c*8 + 3)
                    emit_group(c, 3)
                    if c >= 1:
                        emit_av(c-1, 1)
                        emit_bcast(c-1)
                    emit_group(c, 4)
                    reds_until(c*8 + 5)
                    emit_group(c, 5)
                    emit_group(c, 6)
                    reds_until(c*8 + 7)
                    emit_group(c, 7)
                    reds_until(c*8 + 8)
                    if c == 1 and h + 1 < NH:
                        sl = d["wq"][:, (h+1)*128:(h+2)*128].rearrange(
                            "(cb p) n -> p cb n", p=128)
                        qpw = mil["qproj_mm"]
                        def f(g, sl=sl, qpw=qpw):
                            g.wait_ge(sem_t, qpw)
                            g.dma_start(out=wqk_q[:].rearrange("p (cb n) -> p cb n", n=128),
                                        in_=sl).then_inc(swq.sem, 16)
                        em("gpsimd", f)
                        wqv = swq.inc()
                        sl = d["wk"][:, (h+1)*128:(h+2)*128].rearrange(
                            "(cb p) n -> p cb n", p=128)
                        kpw = mil["kproj_mm"]
                        def f(g, sl=sl, kpw=kpw):
                            g.wait_ge(sem_t, kpw)
                            g.dma_start(out=wqk_k[:].rearrange("p (cb n) -> p cb n", n=128),
                                        in_=sl).then_inc(swk.sem, 16)
                        em("gpsimd", f)
                        PENDING_WQK[h+1] = (wqv, swk.inc())
                emit_p2(NCH-1)
                while RQ:
                    emit_stat_red(RQ.pop(0))

                mil["t_last"] = st.n
                mil["exp_last"] = exp_done[NCH-1][NG-1]
                mil["evac_last"] = evac_done[NCH-1]
                mil["recip_last"] = recip_done[NCH-1]
                mil["o_last"] = o_stores[NCH-1]
                return mil

            for h in range(NH):
                HEAD_MIL.append(plan_head(h))
            o_ready = HEAD_MIL[-1]["o_last"]

            vW2, sW2, tW2 = sv.n, ss.n, st.n
            def f(v, vW2=vW2, sW2=sW2):
                v.wait_ge(sem_v, vW2)
                v.wait_ge(sem_s, sW2)
                v.reduce_sum(fsum2[:, 0:1], fcr[:], axis=AX.X)
                v.reduce_sum(fsum2[:, 1:2], prr[:], axis=AX.X).then_inc(sem_v, 1)
            em("vector", f); fin1 = sv.inc()
            def f(t, fin1=fin1, tW2=tW2):
                t.wait_ge(sem_v, fin1)
                t.wait_ge(sem_t, tW2)
                t.matmul(psum[0:1, 0:1], fsum2[:, 0:1], ones_cf[:], start=True, stop=False)
                t.matmul(psum[0:1, 1:2], fsum2[:, 1:2], ones_cf[:], start=False,
                         stop=True).then_inc(sem_t, 1)
            em("tensor", f); fin2 = st.inc()
            NWF = float(H * FFH)
            def f(v, fin2=fin2):
                v.wait_ge(sem_t, fin2)
                v.tensor_scalar(am11[0:1, 0:2], psum[0:1, 0:2], 1.0/NWF, 1e-5,
                                ALU.mult, ALU.max).then_inc(sem_v, 1)
            em("vector", f); fin3 = sv.inc()
            def f(v, fin3=fin3):
                v.wait_ge(sem_v, fin3)
                v.tensor_scalar(am11[0:1, 2:3], am11[0:1, 0:1], 0.7, None, ALU.mult)
                v.tensor_scalar(am11[0:1, 3:4], am11[0:1, 0:1], -0.7, None, ALU.mult)
                v.tensor_scalar(am11[0:1, 4:5], am11[0:1, 0:1], -1.0, None, ALU.mult)
                v.tensor_scalar(am11[0:1, 5:6], am11[0:1, 1:2], 0.7, None, ALU.mult)
                v.tensor_scalar(am11[0:1, 6:7], am11[0:1, 1:2], -0.7, None, ALU.mult)
                v.tensor_scalar(am11[0:1, 7:8], am11[0:1, 1:2], -1.0, None,
                                ALU.mult).then_inc(sem_v, 1)
            em("vector", f); fin4 = sv.inc()
            def f(t, fin4=fin4):
                t.wait_ge(sem_v, fin4)
                for j in range(6):
                    ins = t.matmul(psum[:, 8+j:9+j], ones_rf[0:1, :], am11[0:1, 2+j:3+j],
                                   start=(j == 0), stop=(j == 5))
                ins.then_inc(sem_t, 1)
            em("tensor", f); fin5 = st.inc()
            def f(v, fin5=fin5):
                v.wait_ge(sem_t, fin5)
                v.tensor_copy(tcols[:], psum[:, 8:14]).then_inc(sem_v, 1)
            em("vector", f); tern_ready = sv.inc()

        wqk_stack.close()

        barrier()
        with ExitStack() as S3:
            E3 = S3.enter_context
            wfc_b = [E3(nc.sbuf_tensor(f"wfc_b{i}", [128, H], F32)) for i in range(2)]
            wpr_b = [E3(nc.sbuf_tensor(f"wpr_b{i}", [128, H], F32)) for i in range(2)]
            qprs = [E3(nc.sbuf_tensor("qprs0", [128, H], BF16))] * 2
            qfc = [E3(nc.sbuf_tensor(f"qfc{i}", [128, H], BF16)) for i in range(2)]
            gtb = E3(nc.sbuf_tensor("gtb", [128, H], BF16))
            relu_t = E3(nc.sbuf_tensor("relu_t", [128, 1024], F32))
            hT_t = [E3(nc.sbuf_tensor("hT_t0", [128, OWN], BF16))] * 2

            pr_tern_d, pr_store_d = [], []
            def emit_wpr_strip(fb):
                sl = d["wprT"][fb*128:(fb+1)*128, :]
                war = pr_store_d[fb-2] if fb >= 2 else None
                spc = swpr[fb % 2]
                def f(sy, sl=sl, war=war, fb=fb, spc=spc):
                    if war is not None:
                        sy.wait_ge(*war)
                    sy.dma_start(out=wpr_b[fb % 2][:], in_=sl).then_inc(spc.sem, 16)
                em("sync", f)
                ld = spc.inc()
                gwar = fc_tern[-1] if fc_tern else None
                gwar2 = pr_tern_d[-1] if pr_tern_d else None
                def f(v, fb=fb, ld=ld, spc=spc, gwar=gwar, gwar2=gwar2):
                    v.wait_ge(spc.sem, ld)
                    v.wait_ge(sem_v, tern_ready)
                    if gwar is not None:
                        v.wait_ge(sem_v, gwar)
                    if gwar2 is not None:
                        v.wait_ge(sem_v, gwar2)
                    v.tensor_scalar(gtb[:], wpr_b[fb % 2][:], tcols[:, 3:4], None,
                                    ALU.is_gt).then_inc(sem_v, 1)
                em("vector", f)
                ed = sv.inc()
                swar = pr_store_d[-1] if pr_store_d else None
                def f(v, fb=fb, ed=ed, swar=swar):
                    v.wait_ge(sem_v, ed)
                    if swar is not None:
                        v.wait_ge(*swar)
                    v.scalar_tensor_tensor(qprs[fb % 2][:], wpr_b[fb % 2][:],
                                           tcols[:, 4:5], gtb[:], ALU.is_lt,
                                           ALU.subtract).then_inc(sem_v, 1)
                em("vector", f)
                pr_tern_d.append(sv.inc())
                trn = pr_tern_d[-1]
                srsc = srst[fb % 2]
                def f(g, fb=fb, trn=trn, srsc=srsc):
                    g.wait_ge(sem_v, trn)
                    for q in range(4):
                        g.dma_start(out=d["qpr_scr"][fb][q],
                                    in_=qprs[fb % 2][:, q*512:(q+1)*512]).then_inc(srsc.sem, 16)
                em("gpsimd", f)
                for _ in range(4):
                    srsc.inc()
                pr_store_d.append((srsc.sem, srsc.n))

            fc_ld, fc_tern, fc_mm, fc_relu, fc_sq, fc_st = [], [], [], [], [], []
            for fb in range(NFB):
                sl = d["wfc3"][fb]
                war = fc_tern[fb-2] if fb >= 2 else None
                sfc = swfc[fb % 2]
                def f(sy, sl=sl, fb=fb, war=war, sfc=sfc):
                    if war is not None:
                        sy.wait_ge(sem_v, war)
                    sy.dma_start(out=wfc_b[fb % 2][:], in_=sl).then_inc(sfc.sem, 16)
                em("sync", f)
                fc_ld.append(sfc.inc())
                ld = fc_ld[fb]
                war = fc_mm[fb-2] if fb >= 2 else None
                gwar = pr_tern_d[-1] if pr_tern_d else None
                def f(v, fb=fb, ld=ld, war=war, sfc=sfc, gwar=gwar):
                    v.wait_ge(sfc.sem, ld)
                    v.wait_ge(sem_v, tern_ready)
                    if war is not None:
                        v.wait_ge(sem_t, war)
                    if gwar is not None:
                        v.wait_ge(sem_v, gwar)
                    v.tensor_scalar(gtb[:], wfc_b[fb % 2][:], tcols[:, 0:1], None,
                                    ALU.is_gt).then_inc(sem_v, 1)
                em("vector", f)
                ed = sv.inc()
                def f(v, fb=fb, ed=ed):
                    v.wait_ge(sem_v, ed)
                    v.scalar_tensor_tensor(qfc[fb % 2][:], wfc_b[fb % 2][:], tcols[:, 1:2],
                                           gtb[:], ALU.is_lt, ALU.subtract).then_inc(sem_v, 1)
                em("vector", f)
                fc_tern.append(sv.inc())
                emit_wpr_strip(fb)
                reg = (fb % 2) * 2048
                trn = fc_tern[fb]
                war = fc_relu[4*(fb-2)+3] if fb >= 2 else None
                def f(t, fb=fb, reg=reg, trn=trn, war=war):
                    t.wait_ge(sem_v, trn)
                    if war is not None:
                        t.wait_ge(sem_s, war)
                    for tc in range(4):
                        for cb in range(NCB):
                            ins = t.matmul(psum[:, reg + tc*512: reg + (tc+1)*512],
                                           qfc[fb % 2][:, cb*128:(cb+1)*128],
                                           xnT[:, cb*T + tc*512: cb*T + (tc+1)*512],
                                           start=(cb == 0), stop=(cb == NCB-1))
                        ins.then_inc(sem_t, 1)
                em("tensor", f)
                mmv = [st.inc() for _ in range(4)]
                fc_mm.append(mmv[3])
                for hh in range(4):
                    gi = 4*fb + hh
                    rwar = fc_sq[gi-2] if gi >= 2 else None
                    mmw = mmv[hh]
                    def f(s, reg=reg, hh=hh, rwar=rwar, mmw=mmw):
                        s.wait_ge(sem_t, mmw)
                        if rwar is not None:
                            s.wait_ge(sem_s, rwar)
                        s.activation(relu_t[:, (hh % 2)*512:(hh % 2)*512+512],
                                     psum[:, reg+hh*512:reg+(hh+1)*512],
                                     AF.Relu, scale=tcols[:, 2:3]).then_inc(sem_s, 1)
                    em("scalar", f)
                    fc_relu.append(ss.inc())
                    rv = fc_relu[-1]
                    war2 = (shT[(fb-1) % 2].sem, fc_st[fb-1]) if (hh == 0 and fb >= 1) else None
                    def f(s, fb=fb, hh=hh, rv=rv, war2=war2):
                        s.wait_ge(sem_s, rv)
                        if war2 is not None:
                            s.wait_ge(*war2)
                        s.activation(hT_t[fb % 2][:, hh*512:(hh+1)*512],
                                     relu_t[:, (hh % 2)*512:(hh % 2)*512+512],
                                     AF.Square).then_inc(sem_s, 1)
                    em("scalar", f)
                    fc_sq.append(ss.inc())
                shc = shT[fb % 2]
                def f(g, fb=fb, shc=shc):
                    g.wait_ge(sem_s, fc_sq[4*fb+3])
                    for tcx in range(NCH):
                        g.dma_start(out=d["hT_scr"][tcx][:, fb],
                                    in_=hT_t[fb % 2][:, tcx*256:(tcx+1)*256]).then_inc(shc.sem, 16)
                em("gpsimd", f)
                for _ in range(NCH):
                    shc.inc()
                fc_st.append(shc.n)
            hT_ready = (shT[0].n, shT[1].n)
        xnT_stack.close()

        barrier()
        with ExitStack() as S4:
            E4 = S4.enter_context
            qpr = [E4(nc.sbuf_tensor(f"qpr{i}", [128, NFB * 512], BF16)) for i in range(2)]
            hT_tc = [E4(nc.sbuf_tensor(f"hT_tc{i}", [128, NFB * 256], BF16)) for i in range(2)]
            x_e = [E4(nc.sbuf_tensor(f"x_e{i}", [128, 512], F32)) for i in range(2)]
            res_e = [E4(nc.sbuf_tensor(f"res_e{i}", [128, 512], F32)) for i in range(2)]

            res_stores, ev_all, mm_all = [], [], []
            qpr_ready_vals = (srst[0].n, srst[1].n)
            qpr_ld = []
            def emit_qpr_load(q):
                war = mm_all[-1] if (q >= 2 and mm_all) else None
                spc = swpr[q % 2]
                def f(sy, q=q, war=war, spc=spc):
                    if q < 2:
                        sy.wait_ge(srst[0].sem, qpr_ready_vals[0])
                        sy.wait_ge(srst[1].sem, qpr_ready_vals[1])
                    if war is not None:
                        sy.wait_ge(sem_t, war)
                    for fb in range(NFB):
                        sy.dma_start(out=qpr[q % 2][:, fb*512:(fb+1)*512],
                                     in_=d["qpr_scr"][fb][q]).then_inc(spc.sem, 16)
                em("sync", f)
                for _ in range(NFB):
                    spc.inc()
                qpr_ld.append((spc.sem, spc.n))

            def emit_quarter(q):
                tc_ld = []
                for tc in range(NCH):
                    war = mm_all[-3] if len(mm_all) >= 3 else None
                    shtcc = shtc[tc % 2]
                    src = d["hT_scr"][tc]
                    def f(sy, src=src, tc=tc, q=q, war=war, shtcc=shtcc):
                        if tc < 2 and q == 0:
                            sy.wait_ge(shT[0].sem, hT_ready[0])
                            sy.wait_ge(shT[1].sem, hT_ready[1])
                        if war is not None:
                            sy.wait_ge(sem_t, war)
                        sy.dma_start(out=hT_tc[tc % 2][:].rearrange("p (fb n) -> p fb n", n=256),
                                     in_=src).then_inc(shtcc.sem, 16)
                    em("sync", f)
                    tc_ld.append((shtcc.sem, shtcc.inc()))
                    if tc == 1 and q + 1 < 4:
                        emit_qpr_load(q + 1)
                    for tb2 in range(2):
                        reg = (tc % 2) * 2048 + tb2 * 512
                        row = tc * 256 + tb2 * 128
                        tl_ = tc_ld[tc]
                        war = ev_all[-4] if len(ev_all) >= 4 else None
                        qld = qpr_ld[q]
                        def f(t, tc=tc, tb2=tb2, reg=reg, tl_=tl_, war=war, q=q, qld=qld):
                            t.wait_ge(*tl_)
                            t.wait_ge(*qld)
                            if war is not None:
                                t.wait_ge(sem_v, war)
                            for fb in range(NFB):
                                lhs = hT_tc[tc % 2][:, fb*256 + tb2*128: fb*256 + (tb2+1)*128]
                                ins = t.matmul(psum[:, reg:reg+512], lhs,
                                               qpr[q % 2][:, fb*512:(fb+1)*512],
                                               start=(fb == 0), stop=(fb == NFB-1))
                            ins.then_inc(sem_t, 1)
                        em("tensor", f)
                        mm_all.append(st.inc())
                        mmv = mm_all[-1]
                        war = ev_all[-2] if len(ev_all) >= 2 else None
                        sxec = sxe[len(ev_all) % 2]
                        srsc = srst[len(ev_all) % 2]
                        def f(sy, row=row, q=q, war=war, idx=len(ev_all), sxec=sxec):
                            if war is not None:
                                sy.wait_ge(sem_v, war)
                            sy.dma_start(out=x_e[idx % 2][:],
                                         in_=d["xp"][row:row+128, q*512:(q+1)*512]).then_inc(sxec.sem, 16)
                        em("sync", f)
                        xl = sxec.inc()
                        war2 = res_stores[-2] if len(res_stores) >= 2 else None
                        def f(v, reg=reg, xl=xl, mmv=mmv, war2=war2, idx=len(ev_all), sxec=sxec, srsc=srsc):
                            v.wait_ge(sem_t, mmv)
                            v.wait_ge(sxec.sem, xl)
                            if war2 is not None:
                                v.wait_ge(srsc.sem, war2)
                            v.scalar_tensor_tensor(res_e[idx % 2][:], psum[:, reg:reg+512],
                                                   tcols[:, 5:6], x_e[idx % 2][:],
                                                   ALU.mult, ALU.add).then_inc(sem_v, 1)
                        em("vector", f)
                        ev_all.append(sv.inc())
                        evv = ev_all[-1]
                        def f(g, row=row, q=q, evv=evv, idx=len(ev_all)-1, srsc=srsc):
                            g.wait_ge(sem_v, evv)
                            g.dma_start(out=d["res_scr"][row:row+128, q*512:(q+1)*512],
                                        in_=res_e[idx % 2][:]).then_inc(srsc.sem, 16)
                        em("gpsimd", f)
                        res_stores.append(srsc.inc())

            emit_qpr_load(0)
            for q in range(4):
                emit_quarter(q)
            res_ready = (srst[0].n, srst[1].n)
        barrier()
        with ExitStack() as S5:
            E5 = S5.enter_context
            wo_all = E5(nc.sbuf_tensor("wo_all", [128, NCB * H], BF16))
            oT_t = [E5(nc.sbuf_tensor(f"oT_t{i}", [128, NCB * 256], BF16)) for i in range(2)]
            res_f = [E5(nc.sbuf_tensor(f"res_f{i}", [128, H], F32)) for i in range(2)]
            out_f = [E5(nc.sbuf_tensor(f"out_f{i}", [128, H], F32)) for i in range(2)]
            bob = E5(nc.sbuf_tensor("bob", [1, H], BF16))

            for grp in range(4):
                sl = d["wo"][grp*512:(grp+1)*512, :].rearrange("(ob p) n -> p ob n", p=128)
                def f(g, sl=sl, grp=grp):
                    g.dma_start(out=wo_all[:, grp*4*H:(grp+1)*4*H].rearrange(
                        "p (ob n) -> p ob n", n=H), in_=sl).then_inc(swo[grp].sem, 16)
                em("gpsimd", f)
                swo[grp].inc()
            def f(g):
                g.dma_start(out=bob[:], in_=d["bor"]).then_inc(sbo.sem, 16)
            em("gpsimd", f)
            sbo.inc()

            f_mm, f_ev, out_stores = [], [], []
            for tc in range(NCH):
                src = d["o_scr"][:, tc*256:(tc+1)*256].rearrange("(ob p) n -> p ob n", p=128)
                war = f_mm[2*(tc-2)+1] if tc >= 2 else None
                sotc = sot[tc % 2]
                def f(sy, src=src, tc=tc, war=war, sotc=sotc):
                    if tc < 2:
                        sy.wait_ge(sost.sem, o_ready)
                    if war is not None:
                        sy.wait_ge(sem_t, war)
                    sy.dma_start(out=oT_t[tc % 2][:].rearrange("p (ob n) -> p ob n", n=256),
                                 in_=src).then_inc(sotc.sem, 16)
                em("sync", f)
                ol = (sotc.sem, sotc.inc())
                for tb2 in range(2):
                    reg = tb2 * 2048
                    row = tc * 256 + tb2 * 128
                    idx = 2*tc + tb2
                    war = f_ev[idx-2] if idx >= 2 else None
                    def f(t, tc=tc, tb2=tb2, reg=reg, ol=ol, war=war):
                        t.wait_ge(*ol)
                        if war is not None:
                            t.wait_ge(sem_v, war)
                        for ob in range(NCB):
                            if ob % 4 == 0:
                                t.wait_ge(swo[ob // 4].sem, 16)
                            if ob == NCB-1:
                                t.wait_ge(sbo.sem, 16)
                            lhs = oT_t[tc % 2][:, ob*256 + tb2*128: ob*256 + (tb2+1)*128]
                            for cc in range(4):
                                t.matmul(psum[:, reg + cc*512: reg + (cc+1)*512], lhs,
                                         wo_all[:, ob*H + cc*512: ob*H + (cc+1)*512],
                                         start=(ob == 0), stop=False)
                        for cc in range(4):
                            ins = t.matmul(psum[:, reg + cc*512: reg + (cc+1)*512],
                                           ones_rb[0:1, :], bob[0:1, cc*512:(cc+1)*512],
                                           start=False, stop=True)
                        ins.then_inc(sem_t, 1)
                    em("tensor", f)
                    f_mm.append(st.inc())
                    war = f_ev[idx-2] if idx >= 2 else None
                    srfc = srf[idx % 2]
                    def f(sy, row=row, war=war, srfc=srfc, idx=idx):
                        if idx < 2:
                            sy.wait_ge(srst[0].sem, res_ready[0])
                            sy.wait_ge(srst[1].sem, res_ready[1])
                        if war is not None:
                            sy.wait_ge(sem_v, war)
                        sy.dma_start(out=res_f[(row//128) % 2][:],
                                     in_=d["res_scr"][row:row+128, :]).then_inc(srfc.sem, 16)
                    em("sync", f)
                    rl = (srfc.sem, srfc.inc())
                    mmv = f_mm[-1]
                    war2 = out_stores[idx-2] if idx >= 2 else None
                    sdoc = sdo[idx % 2]
                    def f(v, reg=reg, rl=rl, mmv=mmv, war2=war2, idx=idx, sdoc=sdoc):
                        v.wait_ge(sem_t, mmv)
                        v.wait_ge(*rl)
                        if war2 is not None:
                            v.wait_ge(sdoc.sem, war2)
                        v.tensor_tensor(out_f[idx % 2][:], psum[:, reg:reg+2048],
                                        res_f[idx % 2][:], ALU.add).then_inc(sem_v, 1)
                    em("vector", f)
                    f_ev.append(sv.inc())
                    evv = f_ev[-1]
                    def f(g, row=row, evv=evv, idx=idx, sdoc=sdoc):
                        g.wait_ge(sem_v, evv)
                        g.dma_start(out=d["out"][row:row+128, :],
                                    in_=out_f[idx % 2][:]).then_inc(sdoc.sem, 16)
                    em("gpsimd", f)
                    out_stores.append(sdoc.inc())

            fin = (sdo[0].n, sdo[1].n)
            def f(g, fin=fin):
                g.wait_ge(sdo[0].sem, fin[0])
                g.wait_ge(sdo[1].sem, fin[1])
            em("gpsimd", f)

        with nc.Block() as blk:
            @blk.sync
            def _(eng):
                for fn in streams["sync"]:
                    fn(eng)
            @blk.gpsimd
            def _(eng):
                for fn in streams["gpsimd"]:
                    fn(eng)
            @blk.vector
            def _(eng):
                for fn in streams["vector"]:
                    fn(eng)
            @blk.scalar
            def _(eng):
                for fn in streams["scalar"]:
                    fn(eng)
            @blk.tensor
            def _(eng):
                for fn in streams["tensor"]:
                    fn(eng)

    return nc


def kernel(**inputs):
    in_maps, gidx = _host_layout(inputs)
    nc = build_nc()
    res = run_bass_kernel_spmd(nc, in_maps, core_ids=list(range(8)))
    out = np.zeros((B, T, H), np.float32)
    for c in range(8):
        out[c // 2, gidx[c]] = np.asarray(res.results[c]["out"], np.float32)
    return out


# version 26 (final)
# speedup vs baseline: 1.1763x; 1.0950x over previous
import math
import numpy as np
import ml_dtypes
from contextlib import ExitStack

import concourse.bass as bass
import concourse.mybir as mybir
from concourse.bass_utils import run_bass_kernel_spmd

F32 = mybir.dt.float32
BF16 = mybir.dt.bfloat16
AF = mybir.ActivationFunctionType
ALU = mybir.AluOpType
AX = mybir.AxisListType

B, T, H = 4, 4096, 2048
NH, HS = 16, 128
OWN = T // 2
NTB = T // 128
NCB = H // 128
FFH = 5120
NFB = FFH // 128
SCALE = 1.0 / math.sqrt(HS)
W = 256
NCH = OWN // W


def _bf(x):
    return np.ascontiguousarray(np.asarray(x, np.float32).astype(ml_dtypes.bfloat16))


def _host_layout(inputs):
    x = np.ascontiguousarray(np.asarray(inputs["L0_attn_x"], np.float32))
    Wq = np.asarray(inputs["L0_attn_Wq"], np.float32)
    bq = np.asarray(inputs["L0_attn_bq"], np.float32)
    Wk = np.asarray(inputs["L0_attn_Wk"], np.float32)
    bk = np.asarray(inputs["L0_attn_bk"], np.float32)
    Wv = np.ascontiguousarray(np.asarray(inputs["L0_attn_Wv"], np.float32))
    bv = np.asarray(inputs["L0_attn_bv"], np.float32)
    Wo = np.ascontiguousarray(np.asarray(inputs["L0_attn_Wo"], np.float32))
    bo = np.asarray(inputs["L0_attn_bo"], np.float32)
    wfc = np.asarray(inputs["L0_ffn_w_fc"], np.float32)
    wpr = np.asarray(inputs["L0_ffn_w_proj"], np.float32)

    perm = np.concatenate([np.concatenate([np.arange(h*HS, (h+1)*HS, 2),
                                           np.arange(h*HS+1, (h+1)*HS, 2)])
                           for h in range(NH)])
    Wq_p = np.ascontiguousarray(Wq[:, perm])
    Wk_p = np.ascontiguousarray(Wk[:, perm])
    bqc = np.ascontiguousarray(bq[perm].reshape(NH, 128).T)
    bkc = np.ascontiguousarray(bk[perm].reshape(NH, 128).T)
    bvr = np.ascontiguousarray(bv.reshape(1, H))
    bor = np.ascontiguousarray(bo.reshape(1, H))
    wfcT = np.ascontiguousarray(wfc.T)
    wprT = np.ascontiguousarray(wpr.T)
    wvr = np.ascontiguousarray(Wv.reshape(16, 128, 8, 256).transpose(2, 1, 0, 3)
                               .reshape(8, 128, 4096))
    wfc3 = np.ascontiguousarray(wfc.reshape(40, 128, 16, 128).transpose(0, 3, 2, 1)
                                .reshape(40, 128, 2048))

    t = np.arange(T, dtype=np.float32)
    inv = 1.0 / 10000.0 ** (np.arange(0, HS, 2, dtype=np.float32) / HS)
    ang = np.outer(inv, t)
    cosg, sing = np.cos(ang), np.sin(ang)

    kl = np.arange(128)[:, None]
    ql = np.arange(128)[None, :]
    mA = (kl <= ql).astype(np.float32)
    mB_even = (kl < ql).astype(np.float32)

    in_maps, gidx = [], []
    for c in range(8):
        b, p = c // 2, c % 2
        own = 2 * np.arange(OWN) + p
        other = 2 * np.arange(OWN) + (1 - p)
        g = np.concatenate([own, other])
        gidx.append(own)
        cosp, sinp = cosg[:, g], sing[:, g]
        in_maps.append({
            "xp": np.ascontiguousarray(x[b][g]),
            "wq": Wq_p, "wk": Wk_p, "wvr": wvr, "wo": Wo,
            "bqc": bqc, "bkc": bkc, "bvr": bvr, "bor": bor,
            "cos2": _bf(np.concatenate([cosp, cosp], 0)),
            "sin2s": _bf(np.concatenate([-sinp, sinp], 0)),
            "mka": _bf(mA),
            "mkb": _bf(mB_even if p == 0 else mA),
            "wfcT": wfcT, "wprT": wprT, "wfc3": wfc3,
        })
    return in_maps, gidx


def build_nc():
    nc = bass.Bass("TRN2", target_bir_lowering=False)
    d = {}
    for nm, shp, dt in [("xp", [T, H], F32), ("wq", [H, H], F32), ("wk", [H, H], F32),
                        ("wvr", [8, 128, 4096], F32), ("wo", [H, H], F32),
                        ("bqc", [128, NH], F32), ("bkc", [128, NH], F32),
                        ("bvr", [1, H], F32), ("bor", [1, H], F32),
                        ("cos2", [128, T], BF16), ("sin2s", [128, T], BF16),
                        ("mka", [128, 128], BF16), ("mkb", [128, 128], BF16),
                        ("wfcT", [H, FFH], F32), ("wprT", [FFH, H], F32),
                        ("wfc3", [NFB, 128, NCB * 128], F32)]:
        d[nm] = nc.dram_tensor(nm, shp, dt, kind="ExternalInput").ap()
    d["out"] = nc.dram_tensor("out", [OWN, H], F32, kind="ExternalOutput").ap()
    d["xn_scr"] = nc.dram_tensor("xn_scr", [T, H], BF16, kind="Internal").ap()
    d["v_scr"] = nc.dram_tensor("v_scr", [T, H], BF16, kind="Internal").ap()
    d["o_scr"] = nc.dram_tensor("o_scr", [H, OWN], BF16, kind="Internal").ap()
    d["hT_scr"] = nc.dram_tensor("hT_scr", [NCH, 128, NFB, 256], BF16, kind="Internal").ap()
    d["res_scr"] = nc.dram_tensor("res_scr", [OWN, H], F32, kind="Internal").ap()
    d["qpr_scr"] = nc.dram_tensor("qpr_scr", [NFB, 4, 128, 512], BF16, kind="Internal").ap()

    streams = {"sync": [], "gpsimd": [], "vector": [], "scalar": [], "tensor": []}

    def em(name, fn):
        streams[name].append(fn)

    class Ctr:
        def __init__(self, sem, step=1):
            self.sem, self.step, self.n = sem, step, 0
        def inc(self):
            self.n += self.step
            return self.n

    with ExitStack() as S0:
        E0 = S0.enter_context
        psum = E0(nc.psum_tensor("psum", [128, 4096], F32))
        ones_cb = E0(nc.sbuf_tensor("ones_cb", [128, 128], BF16))
        ones_cf = E0(nc.sbuf_tensor("ones_cf", [128, 1], F32))
        ones_rf = E0(nc.sbuf_tensor("ones_rf", [1, 128], F32))
        ones_rb = E0(nc.sbuf_tensor("ones_rb", [1, 128], BF16))
        bqc_s = E0(nc.sbuf_tensor("bqc_s", [128, NH], F32))
        bkc_s = E0(nc.sbuf_tensor("bkc_s", [128, NH], F32))
        tcols = E0(nc.sbuf_tensor("tcols", [128, 6], F32))
        xnT_stack = ExitStack()
        xnT = xnT_stack.enter_context(nc.sbuf_tensor("xnT", [128, NCB * T], BF16, side="right", is_transpose_dest=True))
        mka_s = E0(nc.sbuf_tensor("mka_s", [128, 128], BF16))
        mkb_s = E0(nc.sbuf_tensor("mkb_s", [128, 128], BF16))
        am11 = E0(nc.sbuf_tensor("am11", [1, 8], F32))
        wqk_stack = ExitStack()
        wqk_q = wqk_stack.enter_context(nc.sbuf_tensor("wqk_q", [128, H], BF16, side="right"))
        wqk_k = wqk_stack.enter_context(nc.sbuf_tensor("wqk_k", [128, H], BF16, side="right"))
        eps_c = E0(nc.sbuf_tensor("eps_c", [128, 1], F32))
        eps2_c = E0(nc.sbuf_tensor("eps2_c", [128, 1], F32))
        lnh_c = E0(nc.sbuf_tensor("lnh_c", [128, 1], F32))

        sem_v = E0(nc.semaphore("sv"))
        sem_s = E0(nc.semaphore("ss"))
        sem_t = E0(nc.semaphore("st"))
        sv, ss, st = Ctr(sem_v), Ctr(sem_s), Ctr(sem_t)

        ALL_DSEMS = []

        def dsem(name):
            c = Ctr(E0(nc.semaphore(name)), 16)
            ALL_DSEMS.append(c)
            return c

        sx = [dsem(f"sx{i}") for i in range(3)]
        sxn = [dsem("sxn0"), dsem("sxn1")]
        scst = dsem("scst")
        sbv = dsem("sbv")
        sxnt = [dsem(f"sxnt{i}") for i in range(4)]
        swv = [dsem(f"swv{i}") for i in range(3)]
        svst = [dsem(f"svst{i}") for i in range(4)]
        stab = dsem("stab")
        swq = dsem("swq")
        swk = dsem("swk")
        sstat = [dsem("sstat0"), dsem("sstat1"), dsem("sstat2"), dsem("sstat3")]
        svh = dsem("svh")
        ssw = dsem("ssw")
        sost = dsem("sost")
        swfc = [dsem("swfc0"), dsem("swfc1")]
        shT = [dsem("shT0"), dsem("shT1")]
        swpr = [dsem("swpr0"), dsem("swpr1")]
        shtc = [dsem(f"shtc{i}") for i in range(3)]
        sxe = [dsem("sxe0"), dsem("sxe1")]
        srst = [dsem("srst0"), dsem("srst1")]
        swo = [dsem(f"swo{i}") for i in range(4)]
        sbo = dsem("sbo")
        sot = [dsem("sot0"), dsem("sot1")]
        srf = [dsem("srf0"), dsem("srf1")]
        sdo = [dsem("sdo0"), dsem("sdo1")]

        def barrier():
            vals = tuple((c.sem, c.n) for c in (ALL_DSEMS + [sv, ss, st]) if c.n)
            for e in ("sync", "gpsimd", "vector", "scalar", "tensor"):
                def f(eng, vals=vals):
                    for s, v in vals:
                        eng.wait_ge(s, v)
                em(e, f)

        def f(v):
            v.memset(ones_cb[:], 1.0)
            v.memset(ones_cf[:], 1.0)
            v.memset(ones_rf[:], 1.0)
            v.memset(ones_rb[:], 1.0)
            v.memset(eps_c[:], 1e-5)
            v.memset(eps2_c[:], 41.94304)
            v.memset(lnh_c[:], 7.6246189861593985).then_inc(sem_v, 1)
        em("vector", f)
        const_v = sv.inc()
        def f(sy):
            sy.dma_start(out=bqc_s[:], in_=d["bqc"]).then_inc(scst.sem, 16)
            sy.dma_start(out=bkc_s[:], in_=d["bkc"]).then_inc(scst.sem, 16)
            sy.dma_start(out=mka_s[:], in_=d["mka"]).then_inc(scst.sem, 16)
            sy.dma_start(out=mkb_s[:], in_=d["mkb"]).then_inc(scst.sem, 16)
        em("sync", f)
        for _ in range(4):
            scst.inc()

        with ExitStack() as S1:
            E1 = S1.enter_context
            x_t = [E1(nc.sbuf_tensor(f"x_t{i}", [128, H], F32)) for i in range(3)]
            xn_t = [E1(nc.sbuf_tensor(f"xn_t{i}", [128, H], BF16)) for i in range(2)]
            sums = E1(nc.sbuf_tensor("sums", [128, NTB], F32))
            sumsq = E1(nc.sbuf_tensor("sumsq", [128, NTB], F32))
            lnv = E1(nc.sbuf_tensor("lnv", [128, NTB], F32))
            rstd = E1(nc.sbuf_tensor("rstd", [128, NTB], F32))
            negmr = E1(nc.sbuf_tensor("negmr", [128, NTB], F32))
            bvb = E1(nc.sbuf_tensor("bvb", [128, H], BF16))
            bvr_s = E1(nc.sbuf_tensor("bvr_s", [1, H], BF16))
            wv_sl = [E1(nc.sbuf_tensor(f"wv_sl{i}", [128, NCB * 256], BF16)) for i in range(3)]
            v_ev = [E1(nc.sbuf_tensor(f"v_ev{i}", [128, 256], BF16)) for i in range(4)]

            def f(g):
                g.dma_start(out=bvr_s[:], in_=d["bvr"]).then_inc(sbv.sem, 16)
            em("gpsimd", f)
            bvl = sbv.inc()
            def f(t, bvl=bvl, const_v=const_v):
                t.wait_ge(sbv.sem, bvl)
                t.wait_ge(sem_v, const_v)
                for j in range(4):
                    ins = t.matmul(psum[:, 2048+j*512:2048+(j+1)*512], ones_rb[0:1, :],
                                   bvr_s[0:1, j*512:(j+1)*512], start=True, stop=True)
                ins.then_inc(sem_t, 1)
            em("tensor", f); bvt = st.inc()
            def f(v, bvt=bvt):
                v.wait_ge(sem_t, bvt)
                v.tensor_copy(bvb[:], psum[:, 2048:4096]).then_inc(sem_v, 1)
            em("vector", f); bvb_ready = sv.inc()

            wv_load, wv_last_mm, vproj_mm, vproj_ev, v_store = [], [], [], [], []
            def plan_wv_load(li):
                dch = li % 8
                buf = wv_sl[li % 3]
                war = wv_last_mm[li - 3] if li >= 3 else None
                swvc = swv[li % 3]
                def f(g, dch=dch, buf=buf, war=war, swvc=swvc):
                    if war is not None:
                        g.wait_ge(sem_t, war)
                    g.dma_start(out=buf[:], in_=d["wvr"][dch]).then_inc(swvc.sem, 16)
                em("gpsimd", f)
                wv_load.append((swvc.sem, swvc.inc()))
            plan_wv_load(0)
            plan_wv_load(1)
            plan_wv_load(2)

            def plan_v_mm(idx):
                half, rem = idx // 128, idx % 128
                dch, tb = rem // 16, 16 * (idx // 128) + rem % 16
                li = idx // 16
                bank = (idx % 4) * 512
                lw = wv_load[li]
                qrd = xnq_ready[tb // 8]
                war = vproj_ev[idx - 4] if idx >= 4 else None
                def f(t, li=li, tb=tb, bank=bank, lw=lw, qrd=qrd, war=war):
                    t.wait_ge(*lw)
                    t.wait_ge(*qrd)
                    if war is not None:
                        t.wait_ge(sem_v, war)
                    for cb in range(NCB):
                        ins = t.matmul(psum[:, bank:bank+256],
                                       xnT[:, cb*T + tb*128: cb*T + (tb+1)*128],
                                       wv_sl[li % 3][:, cb*256:(cb+1)*256],
                                       start=(cb == 0), stop=(cb == NCB-1))
                    ins.then_inc(sem_t, 1)
                em("tensor", f)
                vproj_mm.append(st.inc())
                if idx % 16 == 15:
                    wv_last_mm.append(vproj_mm[-1])

            def plan_v_ev(idx):
                dch = (idx % 128) // 16
                bank = (idx % 4) * 512
                mmw = vproj_mm[idx]
                war = v_store[idx - 4] if idx >= 4 else None
                svc = svst[idx % 4]
                def f(v, dch=dch, bank=bank, idx=idx, mmw=mmw, war=war, svc=svc):
                    v.wait_ge(sem_t, mmw)
                    v.wait_ge(sem_v, bvb_ready)
                    if war is not None:
                        v.wait_ge(svc.sem, war)
                    v.tensor_tensor(v_ev[idx % 4][:], psum[:, bank:bank+256],
                                    bvb[:, dch*256:(dch+1)*256], ALU.add).then_inc(sem_v, 1)
                em("vector", f)
                vproj_ev.append(sv.inc())

            def plan_v_st(idx):
                dch = (idx % 128) // 16
                tb = 16 * (idx // 128) + idx % 16
                evw = vproj_ev[idx]
                svc = svst[idx % 4]
                def f(g, dch=dch, tb=tb, idx=idx, evw=evw, svc=svc):
                    g.wait_ge(sem_v, evw)
                    g.dma_start(out=d["v_scr"][tb*128:(tb+1)*128, dch*256:(dch+1)*256],
                                in_=v_ev[idx % 4][:]).then_inc(svc.sem, 16)
                em("gpsimd", f)
                v_store.append(svc.inc())

            vp = {"ld": 3, "mm": 0, "ev": 0, "st": 0}
            NV = 256
            def plan_v(tb):
                qdone = min((tb + 1) // 8, 4)
                tmm = max(0, int((tb - 9) * 6.3))
                tev = max(0, int((tb - 10) * 6.3))
                tld = min(16, 3 + max(0, int((tb - 8) * 0.6)))
                while True:
                    did = False
                    if vp["ld"] < tld and len(wv_last_mm) >= vp["ld"] - 2:
                        plan_wv_load(vp["ld"]); vp["ld"] += 1; did = True
                    if vp["mm"] < tmm:
                        idx = vp["mm"]
                        if (idx // 16 < len(wv_load)
                                and (16 * (idx // 128) + idx % 16) // 8 < len(xnq_ready)
                                and (idx < 4 or idx - 4 < len(vproj_ev))):
                            plan_v_mm(idx); vp["mm"] += 1; did = True
                    if (vp["ev"] < min(tev, len(vproj_mm))
                            and (vp["ev"] < 4 or vp["ev"] - 4 < len(v_store))):
                        plan_v_ev(vp["ev"]); vp["ev"] += 1; did = True
                    if vp["st"] < len(vproj_ev):
                        plan_v_st(vp["st"]); vp["st"] += 1; did = True
                    if not did:
                        break

            sl = d["wq"][:, 0:128].rearrange("(cb p) n -> p cb n", p=128)
            def f(g, sl=sl):
                g.dma_start(out=wqk_q[:].rearrange("p (cb n) -> p cb n", n=128),
                            in_=sl).then_inc(swq.sem, 16)
            em("gpsimd", f)
            wq0_ld = swq.inc()
            sl = d["wk"][:, 0:128].rearrange("(cb p) n -> p cb n", p=128)
            def f(g, sl=sl):
                g.dma_start(out=wqk_k[:].rearrange("p (cb n) -> p cb n", n=128),
                            in_=sl).then_inc(swk.sem, 16)
            em("gpsimd", f)
            wk0_ld = swk.inc()

            LNH = float(H)
            xn_act, xn_store, xnq_ready = [], [], []
            for tb in range(NTB):
                buf = x_t[tb % 3]
                war = xn_act[tb-3] if tb >= 3 else None
                sxc = sx[tb % 3]
                def f(sy, tb=tb, buf=buf, war=war, sxc=sxc):
                    if war is not None:
                        sy.wait_ge(sem_s, war)
                    sy.dma_start(out=buf[:], in_=d["xp"][tb*128:(tb+1)*128, :]).then_inc(sxc.sem, 16)
                em("sync", f)
                ld = sxc.inc()
                def f(v, buf=buf, ld=ld, tb=tb, sxc=sxc):
                    v.wait_ge(sxc.sem, ld)
                    v.reduce_sum(sums[:, tb:tb+1], buf[:], axis=AX.X).then_inc(sem_v, 1)
                em("vector", f)
                vred = sv.inc()
                def f(s, buf=buf, ld=ld, tb=tb, sxc=sxc):
                    s.wait_ge(sxc.sem, ld)
                    if tb == 0:
                        s.wait_ge(sem_v, bvb_ready)
                    s.activation(psum[:, 2048:4096], buf[:], AF.Square,
                                 accum_out=sumsq[:, tb:tb+1]).then_inc(sem_s, 1)
                em("scalar", f)
                sred = ss.inc()
                def f(v, tb=tb, vred=vred):
                    v.wait_ge(sem_v, vred)
                    v.tensor_tensor(lnv[:, tb:tb+1], sums[:, tb:tb+1], sums[:, tb:tb+1],
                                    ALU.mult).then_inc(sem_v, 1)
                em("vector", f)
                vt1 = sv.inc()
                def f(v, tb=tb, vt1=vt1, sred=sred):
                    v.wait_ge(sem_v, vt1)
                    v.wait_ge(sem_s, sred)
                    v.scalar_tensor_tensor(sumsq[:, tb:tb+1], sumsq[:, tb:tb+1], LNH,
                                           lnv[:, tb:tb+1], ALU.mult,
                                           ALU.subtract).then_inc(sem_v, 1)
                em("vector", f)
                vX = sv.inc()
                def f(s, tb=tb, vX=vX):
                    s.wait_ge(sem_v, vX)
                    s.activation(lnv[:, tb:tb+1], sumsq[:, tb:tb+1], AF.Ln,
                                 bias=eps2_c[:, 0:1]).then_inc(sem_s, 1)
                em("scalar", f)
                sln = ss.inc()
                def f(s, tb=tb, sln=sln):
                    s.wait_ge(sem_s, sln)
                    s.activation(rstd[:, tb:tb+1], lnv[:, tb:tb+1], AF.Exp, scale=-0.5,
                                 bias=lnh_c[:, 0:1]).then_inc(sem_s, 1)
                em("scalar", f)
                srs = ss.inc()
                def f(v, tb=tb, srs=srs):
                    v.wait_ge(sem_s, srs)
                    v.scalar_tensor_tensor(negmr[:, tb:tb+1], sums[:, tb:tb+1], -1.0/LNH,
                                           rstd[:, tb:tb+1], ALU.mult,
                                           ALU.mult).then_inc(sem_v, 1)
                em("vector", f)
                vng = sv.inc()
                sxnc = sxn[tb % 2]
                war2 = xn_store[tb-2] if tb >= 2 else None
                def f(s, tb=tb, buf=buf, vng=vng, war2=war2, sxnc=sxnc):
                    s.wait_ge(sem_v, vng)
                    if war2 is not None:
                        s.wait_ge(sxnc.sem, war2)
                    s.activation(xn_t[tb % 2][:], buf[:], AF.Identity,
                                 bias=negmr[:, tb:tb+1], scale=rstd[:, tb:tb+1]).then_inc(sem_s, 1)
                em("scalar", f)
                xn_act.append(ss.inc())
                act = xn_act[tb]
                def f(g, tb=tb, act=act, sxnc=sxnc):
                    g.wait_ge(sem_s, act)
                    g.dma_start(out=d["xn_scr"][tb*128:(tb+1)*128, :],
                                in_=xn_t[tb % 2][:]).then_inc(sxnc.sem, 16)
                em("gpsimd", f)
                xn_store.append(sxnc.inc())
                if (tb + 1) % 8 == 0:
                    qt = tb // 8
                    snap = (sxn[0].n, sxn[1].n)
                    sxq = sxnt[qt]
                    def f(sy, qt=qt, snap=snap, sxq=sxq):
                        sy.wait_ge(sxn[0].sem, snap[0])
                        sy.wait_ge(sxn[1].sem, snap[1])
                        for cb in range(NCB):
                            sy.dma_start(out=xnT[:, cb*T + qt*1024: cb*T + (qt+1)*1024],
                                         in_=d["xn_scr"][qt*1024:(qt+1)*1024,
                                                         cb*128:(cb+1)*128],
                                         transpose=True).then_inc(sxq.sem, 16)
                    em("sync", f)
                    for _ in range(NCB):
                        sxq.inc()
                    xnq_ready.append((sxq.sem, sxq.n))
                plan_v(tb)

            xnT_ready = list(xnq_ready)

            while vp["mm"] < NV or vp["ev"] < NV or vp["st"] < NV or vp["ld"] < 16:
                plan_v(999)
            v_ready = tuple((c.sem, c.n) for c in svst)


        barrier()
        with ExitStack() as S2:
            E2 = S2.enter_context
            expT = E2(nc.sbuf_tensor("expT", [128, 32 * W], BF16))
            cos_s = E2(nc.sbuf_tensor("cos_s", [128, T], BF16))
            sin_s = E2(nc.sbuf_tensor("sin_s", [128, T], BF16))
            qrotT = E2(nc.sbuf_tensor("qrotT", [128, OWN], BF16))
            krotT = E2(nc.sbuf_tensor("krotT", [128, T], BF16))
            v_head = E2(nc.sbuf_tensor("v_head", [128, 32 * 128], BF16))

            def f(sy):
                sy.dma_start(out=cos_s[:], in_=d["cos2"]).then_inc(stab.sem, 16)
                sy.dma_start(out=sin_s[:], in_=d["sin2s"]).then_inc(stab.sem, 16)
            em("sync", f)
            stab.inc()
            tables_ready = stab.inc()
            stb = [E2(nc.sbuf_tensor(f"stb{i}", [128, 512], F32)) for i in range(3)]
            fcr = E2(nc.sbuf_tensor("fcr", [128, 160], F32))
            prr = E2(nc.sbuf_tensor("prr", [128, 160], F32))
            fsum2 = E2(nc.sbuf_tensor("fsum2", [128, 2], F32))
            dmp2 = E2(nc.sbuf_tensor("dmp2", [128, 512], BF16))
            qe_b = E2(nc.sbuf_tensor("qe_b", [128, 1024], BF16))
            qsw_b = E2(nc.sbuf_tensor("qsw_b", [128, 1024], BF16))
            rt1 = E2(nc.sbuf_tensor("rt1", [128, 1024], BF16))
            rt2 = E2(nc.sbuf_tensor("rt2", [128, 1024], BF16))
            rb_sb = E2(nc.sbuf_tensor("rb_sb", [128, W], F32))
            o_sb = [E2(nc.sbuf_tensor("o_sb0", [128, W], BF16))] * 2

            REG = [0, 1024, 3072]
            DF0, RB0, OT0 = 2048, 2304, 2560
            QKB = 3072
            GROUPS = [list(range(4*g, 4*g+4)) for g in range(8)]

            wfc_flat = d["wfcT"].rearrange("a b -> (a b)")
            wpr_flat = d["wprT"].rearrange("a b -> (a b)")
            stat_load_done = [None] * 320
            stat_red_done = [None] * 320

            def emit_stat_load(i):
                buf = stb[i % 3]
                src = wfc_flat if i < 160 else wpr_flat
                j = i if i < 160 else i - 160
                sl = src[j * 65536:(j + 1) * 65536].rearrange("(p m) -> p m", p=128)
                sstc = sstat[i % 3]
                def f(sy, sl=sl, buf=buf, i=i, sstc=sstc):
                    if i >= 3:
                        war = stat_red_done[i - 3]
                        sy.wait_ge(war[0], war[1])
                    sy.dma_start(out=buf[:], in_=sl).then_inc(sstc.sem, 16)
                em("sync", f)
                stat_load_done[i] = (sstc.sem, sstc.inc())

            def emit_stat_red(i):
                buf = stb[i % 3]
                ld = stat_load_done[i]
                if i < 160:
                    def f(v, buf=buf, ld=ld, i=i):
                        v.wait_ge(*ld)
                        v.reduce_sum(fcr[:, i:i+1], buf[:], axis=AX.X,
                                     apply_absolute_value=True).then_inc(sem_v, 1)
                    em("vector", f)
                    stat_red_done[i] = (sem_v, sv.inc())
                else:
                    pa = stat_red_done[i-1] if (i > 160 and stat_red_done[i-1] is not None
                                                and stat_red_done[i-1][0] is sem_s) else None
                    def f(s, buf=buf, ld=ld, i=i, pa=pa):
                        s.wait_ge(*ld)
                        if pa is not None:
                            s.wait_ge(*pa)
                        s.activation(dmp2[:], buf[:], AF.Abs,
                                     accum_out=prr[:, i-160:i-159]).then_inc(sem_s, 1)
                    em("scalar", f)
                    stat_red_done[i] = (sem_s, ss.inc())

            HEAD_MIL = []
            PENDING_WQK = {}
            P3 = [0]
            LASTREG = [None]
            REGION_LAST_EXP = {}
            last_c_exp = [None]

            def plan_head(h):
                mil = {}
                prev = HEAD_MIL[h-1] if h > 0 else None

                if h == 0:
                    mil["wq_ld"] = wq0_ld
                    mil["wk_ld"] = wk0_ld
                else:
                    mil["wq_ld"], mil["wk_ld"] = PENDING_WQK[h]
                for k in range(20*h, 20*h+20):
                    emit_stat_load(k)
                RQ = list(range(20*h, 20*h+20))
                def reds_until(done_groups, RQ=RQ):
                    tgt = (done_groups * 20) // 64
                    while (20 - len(RQ)) < tgt:
                        emit_stat_red(RQ.pop(0))

                src = d["v_scr"][:, h*128:(h+1)*128].rearrange("(kb p) n -> p kb n", p=128)
                def f(g, src=src, prev=prev):
                    if prev is not None:
                        g.wait_ge(sem_t, prev["t_last"])
                    else:
                        for _s, _v in v_ready:
                            g.wait_ge(_s, _v)
                    g.dma_start(out=v_head[:].rearrange("p (kb n) -> p kb n", n=128),
                                in_=src).then_inc(svh.sem, 16)
                em("gpsimd", f)
                mil["v_ld"] = svh.inc()

                qkb_war = last_c_exp[0]
                war3 = prev["t_last"] if prev else None
                qev = []
                rope_piece = []
                for j in range(4):
                    bank = QKB + (j % 2) * 512
                    wq_ld = mil["wq_ld"]
                    war = qev[j-2] if j >= 2 else (prev["kev_last"] if prev else None)
                    def f(t, j=j, bank=bank, wq_ld=wq_ld, war=war, h=h, qkb_war=qkb_war):
                        t.wait_ge(swq.sem, wq_ld)
                        if h == 0 and j == 0:
                            for _sw, _sv in xnT_ready:
                                t.wait_ge(_sw, _sv)
                        if j == 0 and qkb_war is not None:
                            t.wait_ge(sem_s, qkb_war)
                        if war is not None:
                            t.wait_ge(sem_s, war)
                        for cb in range(NCB):
                            ins = t.matmul(psum[:, bank:bank+512],
                                           wqk_q[:, cb*128:(cb+1)*128],
                                           xnT[:, cb*T + j*512: cb*T + (j+1)*512],
                                           start=(cb == 0), stop=(cb == NCB-1))
                            if cb == NCB-1:
                                ins.then_inc(sem_t, 1)
                    em("tensor", f)
                    mm = st.inc()
                    war2 = rope_piece[j//2 - 1] if j >= 2 else (prev["ropek1"] if prev else None)
                    def f(s, j=j, bank=bank, mm=mm, h=h, war2=war2):
                        s.wait_ge(sem_t, mm)
                        if j % 2 == 0 and war2 is not None:
                            s.wait_ge(sem_v, war2)
                        s.activation(qe_b[:, (j % 2)*512:(j % 2)*512+512], psum[:, bank:bank+512],
                                     AF.Identity, bias=bqc_s[:, h:h+1]).then_inc(sem_s, 1)
                    em("scalar", f)
                    qev.append(ss.inc())
                    if j % 2 == 1:
                        p = j // 2
                        ev2 = qev[-1]
                        def f(g, ev2=ev2):
                            g.wait_ge(sem_s, ev2)
                            g.dma_start(out=qsw_b[0:64, :], in_=qe_b[64:128, :]).then_inc(ssw.sem, 16)
                            g.dma_start(out=qsw_b[64:128, :], in_=qe_b[0:64, :]).then_inc(ssw.sem, 16)
                        em("gpsimd", f)
                        ssw.inc()
                        swp = ssw.inc()
                        def f(v, swp=swp, p=p, h=h, war3=war3):
                            v.wait_ge(ssw.sem, swp)
                            if p == 0 and h == 0:
                                v.wait_ge(stab.sem, tables_ready)
                            v.tensor_tensor(rt1[:], qe_b[:], cos_s[:, p*1024:(p+1)*1024], ALU.mult)
                            v.tensor_tensor(rt2[:], qsw_b[:], sin_s[:, p*1024:(p+1)*1024],
                                            ALU.mult).then_inc(sem_v, 1)
                        em("vector", f)
                        eq = sv.inc()
                        def f(v, eq=eq, war3=war3, p=p):
                            v.wait_ge(sem_v, eq)
                            if p == 0 and war3 is not None:
                                v.wait_ge(sem_t, war3)
                            v.tensor_tensor(qrotT[:, p*1024:(p+1)*1024], rt1[:], rt2[:],
                                            ALU.add).then_inc(sem_v, 1)
                        em("vector", f)
                        rope_piece.append(sv.inc())
                mil["qproj_mm"] = st.n
                mil["ropeq"] = rope_piece[-1]

                kev_prev = qev
                krope_piece = list(rope_piece)
                for kh in range(2):
                    kev = []
                    for j in range(4):
                        bank = QKB + (j % 2) * 512
                        tcol = kh * 2048 + j * 512
                        wk_ld = mil["wk_ld"]
                        war = kev[j-2] if j >= 2 else kev_prev[2+j]
                        def f(t, bank=bank, tcol=tcol, wk_ld=wk_ld, war=war):
                            t.wait_ge(swk.sem, wk_ld)
                            t.wait_ge(sem_s, war)
                            for cb in range(NCB):
                                ins = t.matmul(psum[:, bank:bank+512],
                                               wqk_k[:, cb*128:(cb+1)*128],
                                               xnT[:, cb*T + tcol: cb*T + tcol + 512],
                                               start=(cb == 0), stop=(cb == NCB-1))
                                if cb == NCB-1:
                                    ins.then_inc(sem_t, 1)
                        em("tensor", f)
                        mm = st.inc()
                        war2 = krope_piece[-1] if j % 2 == 0 else None
                        def f(s, j=j, bank=bank, mm=mm, h=h, war2=war2):
                            s.wait_ge(sem_t, mm)
                            if war2 is not None:
                                s.wait_ge(sem_v, war2)
                            s.activation(qe_b[:, (j % 2)*512:(j % 2)*512+512],
                                         psum[:, bank:bank+512],
                                         AF.Identity, bias=bkc_s[:, h:h+1]).then_inc(sem_s, 1)
                        em("scalar", f)
                        kev.append(ss.inc())
                        if j % 2 == 1:
                            p = j // 2
                            off = kh*2048 + p*1024
                            ev2 = kev[-1]
                            def f(g, ev2=ev2):
                                g.wait_ge(sem_s, ev2)
                                g.dma_start(out=qsw_b[0:64, :], in_=qe_b[64:128, :]).then_inc(ssw.sem, 16)
                                g.dma_start(out=qsw_b[64:128, :], in_=qe_b[0:64, :]).then_inc(ssw.sem, 16)
                            em("gpsimd", f)
                            ssw.inc()
                            swp = ssw.inc()
                            def f(v, swp=swp, off=off):
                                v.wait_ge(ssw.sem, swp)
                                v.tensor_tensor(rt1[:], qe_b[:], cos_s[:, off:off+1024], ALU.mult)
                                v.tensor_tensor(rt2[:], qsw_b[:], sin_s[:, off:off+1024],
                                                ALU.mult).then_inc(sem_v, 1)
                            em("vector", f)
                            ek = sv.inc()
                            war4 = prev["t_last"] if (prev and kh == 0 and p == 0) else None
                            def f(v, ek=ek, war4=war4, off=off):
                                v.wait_ge(sem_v, ek)
                                if war4 is not None:
                                    v.wait_ge(sem_t, war4)
                                v.tensor_tensor(krotT[:, off:off+1024], rt1[:], rt2[:],
                                                ALU.add).then_inc(sem_v, 1)
                            em("vector", f)
                            krope_piece.append(sv.inc())
                    kev_prev = kev
                    mil["ropek%d" % kh] = krope_piece[-1]
                mil["kev_last"] = kev_prev[3]
                mil["kproj_mm"] = st.n

                NG = 8
                exp_done = [[None]*NG for _ in range(NCH)]
                ones_done = [[None, None] for _ in range(NCH)]
                mask_done = [[None, None] for _ in range(NCH)]
                recip_done = [None] * NCH
                av_done = [[None, None] for _ in range(NCH)]
                evac_done = [None] * NCH
                o_stores = [None] * NCH

                def emit_group(c, gi):
                    lg = c * 8 + gi
                    if lg < 58:
                        reg = REG[P3[0] % 3]
                        P3[0] += 1
                        if reg == LASTREG[0]:
                            reg = REG[P3[0] % 3]
                            P3[0] += 1
                    else:
                        reg = REG[0] if LASTREG[0] != REG[0] else REG[1]
                    LASTREG[0] = reg
                    grp = GROUPS[gi]
                    war = REGION_LAST_EXP.get(reg)
                    rk = mil["ropek0"] if gi < 4 else mil["ropek1"]
                    kevw = mil["kev_last"] if reg == QKB else None
                    def f(t, c=c, reg=reg, grp=tuple(grp), war=war, rk=rk, kevw=kevw):
                        if war is not None:
                            t.wait_ge(sem_s, war)
                        if kevw is not None:
                            t.wait_ge(sem_s, kevw)
                        t.wait_ge(sem_v, rk)
                        for kb in grp:
                            pos = kb - grp[0]
                            ins = t.matmul(psum[:, reg+pos*256: reg+pos*256+256],
                                           krotT[:, kb*128:(kb+1)*128],
                                           qrotT[:, c*W:(c+1)*W],
                                           start=(pos % 2 == 0), stop=(pos % 2 == 1))
                            if kb == grp[-1]:
                                ins.then_inc(sem_t, 1)
                    em("tensor", f)
                    gmm = st.inc()
                    g0 = GROUPS[gi][0]
                    half = 0 if gi < NG // 2 else 1
                    if c >= 1:
                        free_war = ("avh", c - 1, half)
                    elif prev is not None:
                        free_war = ("pt", prev["t_last"], None)
                    else:
                        free_war = None
                    def f(s, reg=reg, g0=g0, gmm=gmm, free_war=free_war):
                        s.wait_ge(sem_t, gmm)
                        if free_war is not None:
                            if free_war[0] == "avh":
                                s.wait_ge(sem_t, av_done[free_war[1]][free_war[2]])
                            else:
                                s.wait_ge(sem_t, free_war[1])
                        s.activation(expT[:, g0*W:(g0+4)*W], psum[:, reg:reg+1024],
                                     AF.Exp, scale=SCALE).then_inc(sem_s, 1)
                    em("scalar", f)
                    ev = ss.inc()
                    exp_done[c][gi] = ev
                    REGION_LAST_EXP[reg] = ev
                    if reg == REG[2]:
                        last_c_exp[0] = ev

                def emit_ones(c, half):
                    kbs = range(16) if half == 0 else range(16, 32)
                    war = (recip_done[c-1] if c >= 1 else
                           (prev["recip_last"] if prev else None)) if half == 0 else None
                    lastexp = exp_done[c][3 if half == 0 else 7]
                    def f(t, kbs=tuple(kbs), war=war, half=half, lastexp=lastexp):
                        t.wait_ge(sem_s, lastexp)
                        if war is not None:
                            t.wait_ge(sem_v, war)
                        for kb in kbs:
                            ins = t.matmul(psum[:, DF0:DF0+W], ones_cb[:],
                                           expT[:, kb*W:(kb+1)*W],
                                           start=(kb == 0), stop=(kb == 31))
                            if kb == kbs[-1]:
                                ins.then_inc(sem_t, 1)
                    em("tensor", f)
                    ones_done[c][half] = st.inc()

                def emit_masks(c, half):
                    j0, j1 = 2*c, 2*c+1
                    if half == 0:
                        slices = [(j0*W, j0*W+128, mka_s), (j1*W+128, j1*W+256, mka_s)]
                    else:
                        slices = [((16+j0)*W, (16+j0)*W+128, mkb_s),
                                  ((16+j1)*W+128, (16+j1)*W+256, mkb_s)]
                    def f(v, c=c, half=half, slices=tuple(slices)):
                        v.wait_ge(sem_t, ones_done[c][half])
                        for a, b, mk in slices:
                            ins = v.tensor_tensor(expT[:, a:b], expT[:, a:b], mk[:], ALU.mult)
                        ins.then_inc(sem_v, 1)
                    em("vector", f)
                    mask_done[c][half] = sv.inc()

                def emit_av(c, half):
                    valid = [kb for kb in (range(16) if half == 0 else range(16, 32))
                             if (kb % 16) <= 2*c+1]
                    vld = mil["v_ld"]
                    war = (evac_done[c-1] if c >= 1 else
                           (prev["evac_last"] if prev else None)) if half == 0 else None
                    def f(t, c=c, half=half, valid=tuple(valid), vld=vld, war=war):
                        t.wait_ge(sem_v, mask_done[c][half])
                        t.wait_ge(svh.sem, vld)
                        if war is not None:
                            t.wait_ge(sem_v, war)
                        for i, kb in enumerate(valid):
                            kj = kb % 16
                            cs = 128 if kj == 2*c+1 else 0
                            ins = t.matmul(psum[:, OT0+cs:OT0+256],
                                           v_head[:, kb*128:(kb+1)*128],
                                           expT[:, kb*W+cs:(kb+1)*W],
                                           start=(half == 0 and i == 0),
                                           stop=(half == 1 and i == len(valid)-1))
                            if i == len(valid)-1:
                                ins.then_inc(sem_t, 1)
                    em("tensor", f)
                    av_done[c][half] = st.inc()

                def emit_recip(c):
                    war = evac_done[c-1] if c >= 1 else (prev["evac_last"] if prev else None)
                    def f(v, c=c, war=war):
                        v.wait_ge(sem_t, ones_done[c][1])
                        if war is not None:
                            v.wait_ge(sem_v, war)
                        v.reciprocal(rb_sb[:], psum[:, DF0:DF0+W]).then_inc(sem_v, 1)
                    em("vector", f)
                    recip_done[c] = sv.inc()

                def emit_bcast(c):
                    war = o_stores[c-1] if c >= 1 else (prev["o_last"] if prev else None)
                    def f(v, c=c, war=war):
                        v.wait_ge(sem_v, recip_done[c])
                        v.wait_ge(sem_t, av_done[c][1])
                        if war is not None:
                            v.wait_ge(sost.sem, war)
                        v.tensor_tensor(o_sb[c % 2][:], psum[:, OT0:OT0+W], rb_sb[:],
                                        ALU.mult).then_inc(sem_v, 1)
                    em("vector", f)
                    evac_done[c] = sv.inc()
                    def f(g, c=c, h=h):
                        g.wait_ge(sem_v, evac_done[c])
                        g.dma_start(out=d["o_scr"][h*128:(h+1)*128, c*W:(c+1)*W],
                                    in_=o_sb[c % 2][:]).then_inc(sost.sem, 16)
                    em("gpsimd", f)
                    o_stores[c] = sost.inc()

                def emit_p2(c):
                    emit_ones(c, 0)
                    emit_masks(c, 0)
                    emit_av(c, 0)
                    emit_ones(c, 1)
                    emit_masks(c, 1)
                    emit_recip(c)
                    emit_av(c, 1)
                    emit_bcast(c)

                for c in range(NCH):
                    if c >= 1:
                        emit_ones(c-1, 0)
                        emit_masks(c-1, 0)
                        emit_av(c-1, 0)
                    emit_group(c, 0)
                    emit_group(c, 1)
                    if c >= 1:
                        emit_ones(c-1, 1)
                        emit_masks(c-1, 1)
                        emit_recip(c-1)
                    emit_group(c, 2)
                    reds_until(c*8 + 3)
                    emit_group(c, 3)
                    if c >= 1:
                        emit_av(c-1, 1)
                        emit_bcast(c-1)
                    emit_group(c, 4)
                    reds_until(c*8 + 5)
                    emit_group(c, 5)
                    emit_group(c, 6)
                    reds_until(c*8 + 7)
                    emit_group(c, 7)
                    reds_until(c*8 + 8)
                    if c == 1 and h + 1 < NH:
                        sl = d["wq"][:, (h+1)*128:(h+2)*128].rearrange(
                            "(cb p) n -> p cb n", p=128)
                        qpw = mil["qproj_mm"]
                        def f(g, sl=sl, qpw=qpw):
                            g.wait_ge(sem_t, qpw)
                            g.dma_start(out=wqk_q[:].rearrange("p (cb n) -> p cb n", n=128),
                                        in_=sl).then_inc(swq.sem, 16)
                        em("gpsimd", f)
                        wqv = swq.inc()
                        sl = d["wk"][:, (h+1)*128:(h+2)*128].rearrange(
                            "(cb p) n -> p cb n", p=128)
                        kpw = mil["kproj_mm"]
                        def f(g, sl=sl, kpw=kpw):
                            g.wait_ge(sem_t, kpw)
                            g.dma_start(out=wqk_k[:].rearrange("p (cb n) -> p cb n", n=128),
                                        in_=sl).then_inc(swk.sem, 16)
                        em("gpsimd", f)
                        PENDING_WQK[h+1] = (wqv, swk.inc())
                emit_p2(NCH-1)
                while RQ:
                    emit_stat_red(RQ.pop(0))

                mil["t_last"] = st.n
                mil["exp_last"] = exp_done[NCH-1][NG-1]
                mil["evac_last"] = evac_done[NCH-1]
                mil["recip_last"] = recip_done[NCH-1]
                mil["o_last"] = o_stores[NCH-1]
                return mil

            for h in range(NH):
                HEAD_MIL.append(plan_head(h))
            o_ready = HEAD_MIL[-1]["o_last"]

            vW2, sW2, tW2 = sv.n, ss.n, st.n
            def f(v, vW2=vW2, sW2=sW2):
                v.wait_ge(sem_v, vW2)
                v.wait_ge(sem_s, sW2)
                v.reduce_sum(fsum2[:, 0:1], fcr[:], axis=AX.X)
                v.reduce_sum(fsum2[:, 1:2], prr[:], axis=AX.X).then_inc(sem_v, 1)
            em("vector", f); fin1 = sv.inc()
            def f(t, fin1=fin1, tW2=tW2):
                t.wait_ge(sem_v, fin1)
                t.wait_ge(sem_t, tW2)
                t.matmul(psum[0:1, 0:1], fsum2[:, 0:1], ones_cf[:], start=True, stop=False)
                t.matmul(psum[0:1, 1:2], fsum2[:, 1:2], ones_cf[:], start=False,
                         stop=True).then_inc(sem_t, 1)
            em("tensor", f); fin2 = st.inc()
            NWF = float(H * FFH)
            def f(v, fin2=fin2):
                v.wait_ge(sem_t, fin2)
                v.tensor_scalar(am11[0:1, 0:2], psum[0:1, 0:2], 1.0/NWF, 1e-5,
                                ALU.mult, ALU.max).then_inc(sem_v, 1)
            em("vector", f); fin3 = sv.inc()
            def f(v, fin3=fin3):
                v.wait_ge(sem_v, fin3)
                v.tensor_scalar(am11[0:1, 2:3], am11[0:1, 0:1], 0.7, None, ALU.mult)
                v.tensor_scalar(am11[0:1, 3:4], am11[0:1, 0:1], -0.7, None, ALU.mult)
                v.tensor_scalar(am11[0:1, 4:5], am11[0:1, 0:1], -1.0, None, ALU.mult)
                v.tensor_scalar(am11[0:1, 5:6], am11[0:1, 1:2], 0.7, None, ALU.mult)
                v.tensor_scalar(am11[0:1, 6:7], am11[0:1, 1:2], -0.7, None, ALU.mult)
                v.tensor_scalar(am11[0:1, 7:8], am11[0:1, 1:2], -1.0, None,
                                ALU.mult).then_inc(sem_v, 1)
            em("vector", f); fin4 = sv.inc()
            def f(t, fin4=fin4):
                t.wait_ge(sem_v, fin4)
                for j in range(6):
                    ins = t.matmul(psum[:, 8+j:9+j], ones_rf[0:1, :], am11[0:1, 2+j:3+j],
                                   start=(j == 0), stop=(j == 5))
                ins.then_inc(sem_t, 1)
            em("tensor", f); fin5 = st.inc()
            def f(v, fin5=fin5):
                v.wait_ge(sem_t, fin5)
                v.tensor_copy(tcols[:], psum[:, 8:14]).then_inc(sem_v, 1)
            em("vector", f); tern_ready = sv.inc()

        wqk_stack.close()

        barrier()
        with ExitStack() as S3:
            E3 = S3.enter_context
            wfc_b = [E3(nc.sbuf_tensor(f"wfc_b{i}", [128, H], F32)) for i in range(2)]
            wpr_b = [E3(nc.sbuf_tensor(f"wpr_b{i}", [128, H], F32)) for i in range(2)]
            qprs = [E3(nc.sbuf_tensor("qprs0", [128, H], BF16))] * 2
            qfc = [E3(nc.sbuf_tensor(f"qfc{i}", [128, H], BF16)) for i in range(2)]
            gtb = E3(nc.sbuf_tensor("gtb", [128, H], BF16))
            relu_t = E3(nc.sbuf_tensor("relu_t", [128, 1024], F32))
            hT_t = [E3(nc.sbuf_tensor("hT_t0", [128, OWN], BF16))] * 2

            pr_tern_d, pr_store_d = [], []
            def emit_wpr_strip(fb):
                sl = d["wprT"][fb*128:(fb+1)*128, :]
                war = pr_store_d[fb-2] if fb >= 2 else None
                spc = swpr[fb % 2]
                def f(sy, sl=sl, war=war, fb=fb, spc=spc):
                    if war is not None:
                        sy.wait_ge(*war)
                    sy.dma_start(out=wpr_b[fb % 2][:], in_=sl).then_inc(spc.sem, 16)
                em("sync", f)
                ld = spc.inc()
                gwar = fc_tern[-1] if fc_tern else None
                gwar2 = pr_tern_d[-1] if pr_tern_d else None
                def f(v, fb=fb, ld=ld, spc=spc, gwar=gwar, gwar2=gwar2):
                    v.wait_ge(spc.sem, ld)
                    v.wait_ge(sem_v, tern_ready)
                    if gwar is not None:
                        v.wait_ge(sem_v, gwar)
                    if gwar2 is not None:
                        v.wait_ge(sem_v, gwar2)
                    v.tensor_scalar(gtb[:], wpr_b[fb % 2][:], tcols[:, 3:4], None,
                                    ALU.is_gt).then_inc(sem_v, 1)
                em("vector", f)
                ed = sv.inc()
                swar = pr_store_d[-1] if pr_store_d else None
                def f(v, fb=fb, ed=ed, swar=swar):
                    v.wait_ge(sem_v, ed)
                    if swar is not None:
                        v.wait_ge(*swar)
                    v.scalar_tensor_tensor(qprs[fb % 2][:], wpr_b[fb % 2][:],
                                           tcols[:, 4:5], gtb[:], ALU.is_lt,
                                           ALU.subtract).then_inc(sem_v, 1)
                em("vector", f)
                pr_tern_d.append(sv.inc())
                trn = pr_tern_d[-1]
                srsc = srst[fb % 2]
                def f(g, fb=fb, trn=trn, srsc=srsc):
                    g.wait_ge(sem_v, trn)
                    for q in range(4):
                        g.dma_start(out=d["qpr_scr"][fb][q],
                                    in_=qprs[fb % 2][:, q*512:(q+1)*512]).then_inc(srsc.sem, 16)
                em("gpsimd", f)
                for _ in range(4):
                    srsc.inc()
                pr_store_d.append((srsc.sem, srsc.n))

            fc_ld, fc_tern, fc_mm, fc_relu, fc_sq, fc_st = [], [], [], [], [], []
            for fb in range(NFB):
                sl = d["wfc3"][fb]
                war = fc_tern[fb-2] if fb >= 2 else None
                sfc = swfc[fb % 2]
                def f(sy, sl=sl, fb=fb, war=war, sfc=sfc):
                    if war is not None:
                        sy.wait_ge(sem_v, war)
                    sy.dma_start(out=wfc_b[fb % 2][:], in_=sl).then_inc(sfc.sem, 16)
                em("sync", f)
                fc_ld.append(sfc.inc())
                ld = fc_ld[fb]
                war = fc_mm[fb-2] if fb >= 2 else None
                gwar = pr_tern_d[-1] if pr_tern_d else None
                def f(v, fb=fb, ld=ld, war=war, sfc=sfc, gwar=gwar):
                    v.wait_ge(sfc.sem, ld)
                    v.wait_ge(sem_v, tern_ready)
                    if war is not None:
                        v.wait_ge(sem_t, war)
                    if gwar is not None:
                        v.wait_ge(sem_v, gwar)
                    v.tensor_scalar(gtb[:], wfc_b[fb % 2][:], tcols[:, 0:1], None,
                                    ALU.is_gt).then_inc(sem_v, 1)
                em("vector", f)
                ed = sv.inc()
                def f(v, fb=fb, ed=ed):
                    v.wait_ge(sem_v, ed)
                    v.scalar_tensor_tensor(qfc[fb % 2][:], wfc_b[fb % 2][:], tcols[:, 1:2],
                                           gtb[:], ALU.is_lt, ALU.subtract).then_inc(sem_v, 1)
                em("vector", f)
                fc_tern.append(sv.inc())
                emit_wpr_strip(fb)
                reg = (fb % 2) * 2048
                trn = fc_tern[fb]
                war = fc_relu[4*(fb-2)+3] if fb >= 2 else None
                def f(t, fb=fb, reg=reg, trn=trn, war=war):
                    t.wait_ge(sem_v, trn)
                    if war is not None:
                        t.wait_ge(sem_s, war)
                    for tc in range(4):
                        for cb in range(NCB):
                            ins = t.matmul(psum[:, reg + tc*512: reg + (tc+1)*512],
                                           qfc[fb % 2][:, cb*128:(cb+1)*128],
                                           xnT[:, cb*T + tc*512: cb*T + (tc+1)*512],
                                           start=(cb == 0), stop=(cb == NCB-1))
                        ins.then_inc(sem_t, 1)
                em("tensor", f)
                mmv = [st.inc() for _ in range(4)]
                fc_mm.append(mmv[3])
                for hh in range(4):
                    gi = 4*fb + hh
                    rwar = fc_sq[gi-2] if gi >= 2 else None
                    mmw = mmv[hh]
                    def f(s, reg=reg, hh=hh, rwar=rwar, mmw=mmw):
                        s.wait_ge(sem_t, mmw)
                        if rwar is not None:
                            s.wait_ge(sem_s, rwar)
                        s.activation(relu_t[:, (hh % 2)*512:(hh % 2)*512+512],
                                     psum[:, reg+hh*512:reg+(hh+1)*512],
                                     AF.Relu, scale=tcols[:, 2:3]).then_inc(sem_s, 1)
                    em("scalar", f)
                    fc_relu.append(ss.inc())
                    rv = fc_relu[-1]
                    war2 = (shT[(fb-1) % 2].sem, fc_st[fb-1]) if (hh == 0 and fb >= 1) else None
                    def f(s, fb=fb, hh=hh, rv=rv, war2=war2):
                        s.wait_ge(sem_s, rv)
                        if war2 is not None:
                            s.wait_ge(*war2)
                        s.activation(hT_t[fb % 2][:, hh*512:(hh+1)*512],
                                     relu_t[:, (hh % 2)*512:(hh % 2)*512+512],
                                     AF.Square).then_inc(sem_s, 1)
                    em("scalar", f)
                    fc_sq.append(ss.inc())
                shc = shT[fb % 2]
                def f(g, fb=fb, shc=shc):
                    g.wait_ge(sem_s, fc_sq[4*fb+3])
                    for tcx in range(NCH):
                        g.dma_start(out=d["hT_scr"][tcx][:, fb],
                                    in_=hT_t[fb % 2][:, tcx*256:(tcx+1)*256]).then_inc(shc.sem, 16)
                em("gpsimd", f)
                for _ in range(NCH):
                    shc.inc()
                fc_st.append(shc.n)
            hT_ready = (shT[0].n, shT[1].n)
        xnT_stack.close()

        barrier()
        with ExitStack() as S4:
            E4 = S4.enter_context
            qpr = [E4(nc.sbuf_tensor(f"qpr{i}", [128, NFB * 512], BF16)) for i in range(2)]
            hT_tc = [E4(nc.sbuf_tensor(f"hT_tc{i}", [128, NFB * 256], BF16)) for i in range(3)]
            x_e = [E4(nc.sbuf_tensor(f"x_e{i}", [128, 512], F32)) for i in range(2)]
            res_e = [E4(nc.sbuf_tensor(f"res_e{i}", [128, 512], F32)) for i in range(2)]

            res_stores, ev_all, mm_all = [], [], []
            qpr_ready_vals = (srst[0].n, srst[1].n)
            qpr_ld = []
            def emit_qpr_load(q):
                war = mm_all[16*(q-1) - 1] if q >= 2 else None
                spc = swpr[q % 2]
                def f(sy, q=q, war=war, spc=spc):
                    if q < 2:
                        sy.wait_ge(srst[0].sem, qpr_ready_vals[0])
                        sy.wait_ge(srst[1].sem, qpr_ready_vals[1])
                    if war is not None:
                        sy.wait_ge(sem_t, war)
                    for fb in range(NFB):
                        sy.dma_start(out=qpr[q % 2][:, fb*512:(fb+1)*512],
                                     in_=d["qpr_scr"][fb][q]).then_inc(spc.sem, 16)
                em("sync", f)
                for _ in range(NFB):
                    spc.inc()
                qpr_ld.append((spc.sem, spc.n))

            def emit_quarter(q):
                tc_ld = []
                for tc in range(NCH):
                    gi = q * NCH + tc
                    war = mm_all[-5] if len(mm_all) >= 5 else None
                    shtcc = shtc[gi % 3]
                    src = d["hT_scr"][tc]
                    def f(sy, src=src, tc=tc, gi=gi, q=q, war=war, shtcc=shtcc):
                        if tc < 3 and q == 0:
                            sy.wait_ge(shT[0].sem, hT_ready[0])
                            sy.wait_ge(shT[1].sem, hT_ready[1])
                        if war is not None:
                            sy.wait_ge(sem_t, war)
                        sy.dma_start(out=hT_tc[gi % 3][:].rearrange("p (fb n) -> p fb n", n=256),
                                     in_=src).then_inc(shtcc.sem, 16)
                    em("sync", f)
                    tc_ld.append((shtcc.sem, shtcc.inc()))
                    if tc == 0 and q + 1 < 4:
                        emit_qpr_load(q + 1)
                    for tb2 in range(2):
                        reg = (tc % 2) * 2048 + tb2 * 512
                        row = tc * 256 + tb2 * 128
                        tl_ = tc_ld[tc]
                        war = ev_all[-4] if len(ev_all) >= 4 else None
                        qld = qpr_ld[q]
                        def f(t, tc=tc, tb2=tb2, reg=reg, tl_=tl_, war=war, q=q, qld=qld):
                            t.wait_ge(*tl_)
                            t.wait_ge(*qld)
                            if war is not None:
                                t.wait_ge(sem_v, war)
                            for fb in range(NFB):
                                lhs = hT_tc[(q * NCH + tc) % 3][:, fb*256 + tb2*128: fb*256 + (tb2+1)*128]
                                ins = t.matmul(psum[:, reg:reg+512], lhs,
                                               qpr[q % 2][:, fb*512:(fb+1)*512],
                                               start=(fb == 0), stop=(fb == NFB-1))
                            ins.then_inc(sem_t, 1)
                        em("tensor", f)
                        mm_all.append(st.inc())
                        mmv = mm_all[-1]
                        war = ev_all[-2] if len(ev_all) >= 2 else None
                        sxec = sxe[len(ev_all) % 2]
                        srsc = srst[len(ev_all) % 2]
                        def f(sy, row=row, q=q, war=war, idx=len(ev_all), sxec=sxec):
                            if war is not None:
                                sy.wait_ge(sem_v, war)
                            sy.dma_start(out=x_e[idx % 2][:],
                                         in_=d["xp"][row:row+128, q*512:(q+1)*512]).then_inc(sxec.sem, 16)
                        em("sync", f)
                        xl = sxec.inc()
                        war2 = res_stores[-2] if len(res_stores) >= 2 else None
                        def f(v, reg=reg, xl=xl, mmv=mmv, war2=war2, idx=len(ev_all), sxec=sxec, srsc=srsc):
                            v.wait_ge(sem_t, mmv)
                            v.wait_ge(sxec.sem, xl)
                            if war2 is not None:
                                v.wait_ge(srsc.sem, war2)
                            v.scalar_tensor_tensor(res_e[idx % 2][:], psum[:, reg:reg+512],
                                                   tcols[:, 5:6], x_e[idx % 2][:],
                                                   ALU.mult, ALU.add).then_inc(sem_v, 1)
                        em("vector", f)
                        ev_all.append(sv.inc())
                        evv = ev_all[-1]
                        def f(g, row=row, q=q, evv=evv, idx=len(ev_all)-1, srsc=srsc):
                            g.wait_ge(sem_v, evv)
                            g.dma_start(out=d["res_scr"][row:row+128, q*512:(q+1)*512],
                                        in_=res_e[idx % 2][:]).then_inc(srsc.sem, 16)
                        em("gpsimd", f)
                        res_stores.append(srsc.inc())

            emit_qpr_load(0)
            for q in range(4):
                emit_quarter(q)
            res_ready = (srst[0].n, srst[1].n)
        barrier()
        with ExitStack() as S5:
            E5 = S5.enter_context
            wo_all = E5(nc.sbuf_tensor("wo_all", [128, NCB * H], BF16))
            oT_t = [E5(nc.sbuf_tensor(f"oT_t{i}", [128, NCB * 256], BF16)) for i in range(2)]
            res_f = [E5(nc.sbuf_tensor(f"res_f{i}", [128, H], F32)) for i in range(2)]
            out_f = [E5(nc.sbuf_tensor(f"out_f{i}", [128, H], F32)) for i in range(2)]
            bob = E5(nc.sbuf_tensor("bob", [1, H], BF16))

            for grp in range(4):
                sl = d["wo"][grp*512:(grp+1)*512, :].rearrange("(ob p) n -> p ob n", p=128)
                def f(g, sl=sl, grp=grp):
                    g.dma_start(out=wo_all[:, grp*4*H:(grp+1)*4*H].rearrange(
                        "p (ob n) -> p ob n", n=H), in_=sl).then_inc(swo[grp].sem, 16)
                em("gpsimd", f)
                swo[grp].inc()
            def f(g):
                g.dma_start(out=bob[:], in_=d["bor"]).then_inc(sbo.sem, 16)
            em("gpsimd", f)
            sbo.inc()

            f_mm, f_ev, out_stores = [], [], []
            for tc in range(NCH):
                src = d["o_scr"][:, tc*256:(tc+1)*256].rearrange("(ob p) n -> p ob n", p=128)
                war = f_mm[2*(tc-2)+1] if tc >= 2 else None
                sotc = sot[tc % 2]
                def f(sy, src=src, tc=tc, war=war, sotc=sotc):
                    if tc < 2:
                        sy.wait_ge(sost.sem, o_ready)
                    if war is not None:
                        sy.wait_ge(sem_t, war)
                    sy.dma_start(out=oT_t[tc % 2][:].rearrange("p (ob n) -> p ob n", n=256),
                                 in_=src).then_inc(sotc.sem, 16)
                em("sync", f)
                ol = (sotc.sem, sotc.inc())
                for tb2 in range(2):
                    reg = tb2 * 2048
                    row = tc * 256 + tb2 * 128
                    idx = 2*tc + tb2
                    war = f_ev[idx-2] if idx >= 2 else None
                    def f(t, tc=tc, tb2=tb2, reg=reg, ol=ol, war=war):
                        t.wait_ge(*ol)
                        if war is not None:
                            t.wait_ge(sem_v, war)
                        for ob in range(NCB):
                            if ob % 4 == 0:
                                t.wait_ge(swo[ob // 4].sem, 16)
                            if ob == NCB-1:
                                t.wait_ge(sbo.sem, 16)
                            lhs = oT_t[tc % 2][:, ob*256 + tb2*128: ob*256 + (tb2+1)*128]
                            for cc in range(4):
                                t.matmul(psum[:, reg + cc*512: reg + (cc+1)*512], lhs,
                                         wo_all[:, ob*H + cc*512: ob*H + (cc+1)*512],
                                         start=(ob == 0), stop=False)
                        for cc in range(4):
                            ins = t.matmul(psum[:, reg + cc*512: reg + (cc+1)*512],
                                           ones_rb[0:1, :], bob[0:1, cc*512:(cc+1)*512],
                                           start=False, stop=True)
                        ins.then_inc(sem_t, 1)
                    em("tensor", f)
                    f_mm.append(st.inc())
                    war = f_ev[idx-2] if idx >= 2 else None
                    srfc = srf[idx % 2]
                    def f(sy, row=row, war=war, srfc=srfc, idx=idx):
                        if idx < 2:
                            sy.wait_ge(srst[0].sem, res_ready[0])
                            sy.wait_ge(srst[1].sem, res_ready[1])
                        if war is not None:
                            sy.wait_ge(sem_v, war)
                        sy.dma_start(out=res_f[(row//128) % 2][:],
                                     in_=d["res_scr"][row:row+128, :]).then_inc(srfc.sem, 16)
                    em("sync", f)
                    rl = (srfc.sem, srfc.inc())
                    mmv = f_mm[-1]
                    war2 = out_stores[idx-2] if idx >= 2 else None
                    sdoc = sdo[idx % 2]
                    def f(v, reg=reg, rl=rl, mmv=mmv, war2=war2, idx=idx, sdoc=sdoc):
                        v.wait_ge(sem_t, mmv)
                        v.wait_ge(*rl)
                        if war2 is not None:
                            v.wait_ge(sdoc.sem, war2)
                        v.tensor_tensor(out_f[idx % 2][:], psum[:, reg:reg+2048],
                                        res_f[idx % 2][:], ALU.add).then_inc(sem_v, 1)
                    em("vector", f)
                    f_ev.append(sv.inc())
                    evv = f_ev[-1]
                    def f(g, row=row, evv=evv, idx=idx, sdoc=sdoc):
                        g.wait_ge(sem_v, evv)
                        g.dma_start(out=d["out"][row:row+128, :],
                                    in_=out_f[idx % 2][:]).then_inc(sdoc.sem, 16)
                    em("gpsimd", f)
                    out_stores.append(sdoc.inc())

            fin = (sdo[0].n, sdo[1].n)
            def f(g, fin=fin):
                g.wait_ge(sdo[0].sem, fin[0])
                g.wait_ge(sdo[1].sem, fin[1])
            em("gpsimd", f)

        with nc.Block() as blk:
            @blk.sync
            def _(eng):
                for fn in streams["sync"]:
                    fn(eng)
            @blk.gpsimd
            def _(eng):
                for fn in streams["gpsimd"]:
                    fn(eng)
            @blk.vector
            def _(eng):
                for fn in streams["vector"]:
                    fn(eng)
            @blk.scalar
            def _(eng):
                for fn in streams["scalar"]:
                    fn(eng)
            @blk.tensor
            def _(eng):
                for fn in streams["tensor"]:
                    fn(eng)

    return nc


def kernel(**inputs):
    in_maps, gidx = _host_layout(inputs)
    nc = build_nc()
    res = run_bass_kernel_spmd(nc, in_maps, core_ids=list(range(8)))
    out = np.zeros((B, T, H), np.float32)
    for c in range(8):
        out[c // 2, gidx[c]] = np.asarray(res.results[c]["out"], np.float32)
    return out


# version 28
# speedup vs baseline: 1.2202x; 1.0374x over previous
import math
import numpy as np
import ml_dtypes
from contextlib import ExitStack

import concourse.bass as bass
import concourse.mybir as mybir
from concourse.bass_utils import run_bass_kernel_spmd

F32 = mybir.dt.float32
BF16 = mybir.dt.bfloat16
AF = mybir.ActivationFunctionType
ALU = mybir.AluOpType
AX = mybir.AxisListType

B, T, H = 4, 4096, 2048
NH, HS = 16, 128
OWN = T // 2
NTB = T // 128
NCB = H // 128
FFH = 5120
NFB = FFH // 128
SCALE = 1.0 / math.sqrt(HS)
W = 256
NCH = OWN // W


def _bf(x):
    return np.ascontiguousarray(np.asarray(x, np.float32).astype(ml_dtypes.bfloat16))


def _host_layout(inputs):
    x = np.ascontiguousarray(np.asarray(inputs["L0_attn_x"], np.float32))
    Wq = np.asarray(inputs["L0_attn_Wq"], np.float32)
    bq = np.asarray(inputs["L0_attn_bq"], np.float32)
    Wk = np.asarray(inputs["L0_attn_Wk"], np.float32)
    bk = np.asarray(inputs["L0_attn_bk"], np.float32)
    Wv = np.ascontiguousarray(np.asarray(inputs["L0_attn_Wv"], np.float32))
    bv = np.asarray(inputs["L0_attn_bv"], np.float32)
    Wo = np.ascontiguousarray(np.asarray(inputs["L0_attn_Wo"], np.float32))
    bo = np.asarray(inputs["L0_attn_bo"], np.float32)
    wfc = np.asarray(inputs["L0_ffn_w_fc"], np.float32)
    wpr = np.asarray(inputs["L0_ffn_w_proj"], np.float32)

    perm = np.concatenate([np.concatenate([np.arange(h*HS, (h+1)*HS, 2),
                                           np.arange(h*HS+1, (h+1)*HS, 2)])
                           for h in range(NH)])
    Wq_p = np.ascontiguousarray(Wq[:, perm])
    Wk_p = np.ascontiguousarray(Wk[:, perm])
    bqc = np.ascontiguousarray(bq[perm].reshape(NH, 128).T)
    bkc = np.ascontiguousarray(bk[perm].reshape(NH, 128).T)
    bvr = np.ascontiguousarray(bv.reshape(1, H))
    bor = np.ascontiguousarray(bo.reshape(1, H))
    wfcT = np.ascontiguousarray(wfc.T)
    wprT = np.ascontiguousarray(wpr.T)
    wvr = np.ascontiguousarray(Wv.reshape(16, 128, 8, 256).transpose(2, 1, 0, 3)
                               .reshape(8, 128, 4096))
    wfc3 = np.ascontiguousarray(wfc.reshape(40, 128, 16, 128).transpose(0, 3, 2, 1)
                                .reshape(40, 128, 2048))

    t = np.arange(T, dtype=np.float32)
    inv = 1.0 / 10000.0 ** (np.arange(0, HS, 2, dtype=np.float32) / HS)
    ang = np.outer(inv, t)
    cosg, sing = np.cos(ang), np.sin(ang)

    kl = np.arange(128)[:, None]
    ql = np.arange(128)[None, :]
    mA = (kl <= ql).astype(np.float32)
    mB_even = (kl < ql).astype(np.float32)

    in_maps, gidx = [], []
    for c in range(8):
        b, p = c // 2, c % 2
        own = 2 * np.arange(OWN) + p
        other = 2 * np.arange(OWN) + (1 - p)
        g = np.concatenate([own, other])
        gidx.append(own)
        cosp, sinp = cosg[:, g], sing[:, g]
        in_maps.append({
            "xp": np.ascontiguousarray(x[b][g]),
            "wq": Wq_p, "wk": Wk_p, "wvr": wvr, "wo": Wo,
            "bqc": bqc, "bkc": bkc, "bvr": bvr, "bor": bor,
            "cos2": _bf(np.concatenate([cosp, cosp], 0)),
            "sin2s": _bf(np.concatenate([-sinp, sinp], 0)),
            "mka": _bf(mA),
            "mkb": _bf(mB_even if p == 0 else mA),
            "wfcT": wfcT, "wprT": wprT, "wfc3": wfc3,
        })
    return in_maps, gidx


def build_nc():
    nc = bass.Bass("TRN2", target_bir_lowering=False)
    d = {}
    for nm, shp, dt in [("xp", [T, H], F32), ("wq", [H, H], F32), ("wk", [H, H], F32),
                        ("wvr", [8, 128, 4096], F32), ("wo", [H, H], F32),
                        ("bqc", [128, NH], F32), ("bkc", [128, NH], F32),
                        ("bvr", [1, H], F32), ("bor", [1, H], F32),
                        ("cos2", [128, T], BF16), ("sin2s", [128, T], BF16),
                        ("mka", [128, 128], BF16), ("mkb", [128, 128], BF16),
                        ("wfcT", [H, FFH], F32), ("wprT", [FFH, H], F32),
                        ("wfc3", [NFB, 128, NCB * 128], F32)]:
        d[nm] = nc.dram_tensor(nm, shp, dt, kind="ExternalInput").ap()
    d["out"] = nc.dram_tensor("out", [OWN, H], F32, kind="ExternalOutput").ap()
    d["xn_scr"] = nc.dram_tensor("xn_scr", [T, H], BF16, kind="Internal").ap()
    d["v_scr"] = nc.dram_tensor("v_scr", [T, H], BF16, kind="Internal").ap()
    d["o_scr"] = nc.dram_tensor("o_scr", [H, OWN], BF16, kind="Internal").ap()
    d["hT_scr"] = nc.dram_tensor("hT_scr", [NCH, 128, NFB, 256], BF16, kind="Internal").ap()
    d["res_scr"] = nc.dram_tensor("res_scr", [OWN, H], F32, kind="Internal").ap()
    d["qpr_scr"] = nc.dram_tensor("qpr_scr", [NFB, 4, 128, 512], BF16, kind="Internal").ap()

    streams = {"sync": [], "gpsimd": [], "vector": [], "scalar": [], "tensor": []}

    def em(name, fn):
        streams[name].append(fn)

    class Ctr:
        def __init__(self, sem, step=1):
            self.sem, self.step, self.n = sem, step, 0
        def inc(self):
            self.n += self.step
            return self.n

    with ExitStack() as S0:
        E0 = S0.enter_context
        psum = E0(nc.psum_tensor("psum", [128, 4096], F32))
        ones_cb = E0(nc.sbuf_tensor("ones_cb", [128, 128], BF16))
        ones_cf = E0(nc.sbuf_tensor("ones_cf", [128, 1], F32))
        ones_rf = E0(nc.sbuf_tensor("ones_rf", [1, 128], F32))
        ones_rb = E0(nc.sbuf_tensor("ones_rb", [1, 128], BF16))
        bqc_s = E0(nc.sbuf_tensor("bqc_s", [128, NH], F32))
        bkc_s = E0(nc.sbuf_tensor("bkc_s", [128, NH], F32))
        tcols = E0(nc.sbuf_tensor("tcols", [128, 6], F32))
        xnT_stack = ExitStack()
        xnT = xnT_stack.enter_context(nc.sbuf_tensor("xnT", [128, NCB * T], BF16, side="right", is_transpose_dest=True))
        mka_s = E0(nc.sbuf_tensor("mka_s", [128, 128], BF16))
        mkb_s = E0(nc.sbuf_tensor("mkb_s", [128, 128], BF16))
        am11 = E0(nc.sbuf_tensor("am11", [1, 8], F32))
        wqk_stack = ExitStack()
        wqk_q = wqk_stack.enter_context(nc.sbuf_tensor("wqk_q", [128, H], BF16, side="right"))
        wqk_k = wqk_stack.enter_context(nc.sbuf_tensor("wqk_k", [128, H], BF16, side="right"))
        eps_c = E0(nc.sbuf_tensor("eps_c", [128, 1], F32))
        eps2_c = E0(nc.sbuf_tensor("eps2_c", [128, 1], F32))
        lnh_c = E0(nc.sbuf_tensor("lnh_c", [128, 1], F32))

        sem_v = E0(nc.semaphore("sv"))
        sem_s = E0(nc.semaphore("ss"))
        sem_t = E0(nc.semaphore("st"))
        sv, ss, st = Ctr(sem_v), Ctr(sem_s), Ctr(sem_t)

        ALL_DSEMS = []

        def dsem(name):
            c = Ctr(E0(nc.semaphore(name)), 16)
            ALL_DSEMS.append(c)
            return c

        sx = [dsem(f"sx{i}") for i in range(3)]
        sxn = [dsem("sxn0"), dsem("sxn1")]
        scst = dsem("scst")
        sbv = dsem("sbv")
        sxnt = [dsem(f"sxnt{i}") for i in range(4)]
        swv = [dsem(f"swv{i}") for i in range(3)]
        svst = [dsem(f"svst{i}") for i in range(4)]
        stab = dsem("stab")
        swq = dsem("swq")
        swk = dsem("swk")
        sstat = [dsem("sstat0"), dsem("sstat1"), dsem("sstat2"), dsem("sstat3")]
        svh = dsem("svh")
        ssw = dsem("ssw")
        sost = dsem("sost")
        swfc = [dsem("swfc0"), dsem("swfc1")]
        shT = [dsem("shT0"), dsem("shT1")]
        swpr = [dsem("swpr0"), dsem("swpr1")]
        shtc = [dsem(f"shtc{i}") for i in range(3)]
        sxe = [dsem("sxe0"), dsem("sxe1")]
        srst = [dsem("srst0"), dsem("srst1")]
        swo = [dsem(f"swo{i}") for i in range(4)]
        sbo = dsem("sbo")
        sot = [dsem("sot0"), dsem("sot1")]
        srf = [dsem("srf0"), dsem("srf1")]
        sdo = [dsem("sdo0"), dsem("sdo1")]

        def barrier():
            vals = tuple((c.sem, c.n) for c in (ALL_DSEMS + [sv, ss, st]) if c.n)
            for e in ("sync", "gpsimd", "vector", "scalar", "tensor"):
                def f(eng, vals=vals):
                    for s, v in vals:
                        eng.wait_ge(s, v)
                em(e, f)

        def f(v):
            v.memset(ones_cb[:], 1.0)
            v.memset(ones_cf[:], 1.0)
            v.memset(ones_rf[:], 1.0)
            v.memset(ones_rb[:], 1.0)
            v.memset(eps_c[:], 1e-5)
            v.memset(eps2_c[:], 41.94304)
            v.memset(lnh_c[:], 7.6246189861593985).then_inc(sem_v, 1)
        em("vector", f)
        const_v = sv.inc()
        def f(sy):
            sy.dma_start(out=bqc_s[:], in_=d["bqc"]).then_inc(scst.sem, 16)
            sy.dma_start(out=bkc_s[:], in_=d["bkc"]).then_inc(scst.sem, 16)
            sy.dma_start(out=mka_s[:], in_=d["mka"]).then_inc(scst.sem, 16)
            sy.dma_start(out=mkb_s[:], in_=d["mkb"]).then_inc(scst.sem, 16)
        em("sync", f)
        for _ in range(4):
            scst.inc()

        with ExitStack() as S1:
            E1 = S1.enter_context
            x_t = [E1(nc.sbuf_tensor(f"x_t{i}", [128, H], F32)) for i in range(3)]
            xn_t = [E1(nc.sbuf_tensor(f"xn_t{i}", [128, H], BF16)) for i in range(2)]
            sums = E1(nc.sbuf_tensor("sums", [128, NTB], F32))
            sumsq = E1(nc.sbuf_tensor("sumsq", [128, NTB], F32))
            lnv = E1(nc.sbuf_tensor("lnv", [128, NTB], F32))
            rstd = E1(nc.sbuf_tensor("rstd", [128, NTB], F32))
            negmr = E1(nc.sbuf_tensor("negmr", [128, NTB], F32))
            bvb = E1(nc.sbuf_tensor("bvb", [128, H], BF16))
            bvr_s = E1(nc.sbuf_tensor("bvr_s", [1, H], BF16))
            wv_sl = [E1(nc.sbuf_tensor(f"wv_sl{i}", [128, NCB * 256], BF16)) for i in range(3)]
            v_ev = [E1(nc.sbuf_tensor(f"v_ev{i}", [128, 256], BF16)) for i in range(4)]

            def f(g):
                g.dma_start(out=bvr_s[:], in_=d["bvr"]).then_inc(sbv.sem, 16)
            em("gpsimd", f)
            bvl = sbv.inc()
            def f(t, bvl=bvl, const_v=const_v):
                t.wait_ge(sbv.sem, bvl)
                t.wait_ge(sem_v, const_v)
                for j in range(4):
                    ins = t.matmul(psum[:, 2048+j*512:2048+(j+1)*512], ones_rb[0:1, :],
                                   bvr_s[0:1, j*512:(j+1)*512], start=True, stop=True)
                ins.then_inc(sem_t, 1)
            em("tensor", f); bvt = st.inc()
            def f(v, bvt=bvt):
                v.wait_ge(sem_t, bvt)
                v.tensor_copy(bvb[:], psum[:, 2048:4096]).then_inc(sem_v, 1)
            em("vector", f); bvb_ready = sv.inc()

            wv_load, wv_last_mm, vproj_mm, vproj_ev, v_store = [], [], [], [], []
            def plan_wv_load(li):
                dch = li % 8
                buf = wv_sl[li % 3]
                war = wv_last_mm[li - 3] if li >= 3 else None
                swvc = swv[li % 3]
                def f(g, dch=dch, buf=buf, war=war, swvc=swvc):
                    if war is not None:
                        g.wait_ge(sem_t, war)
                    g.dma_start(out=buf[:], in_=d["wvr"][dch]).then_inc(swvc.sem, 16)
                em("gpsimd", f)
                wv_load.append((swvc.sem, swvc.inc()))
            plan_wv_load(0)
            plan_wv_load(1)
            plan_wv_load(2)

            def plan_v_mm(idx):
                half, rem = idx // 128, idx % 128
                dch, tb = rem // 16, 16 * (idx // 128) + rem % 16
                li = idx // 16
                bank = (idx % 4) * 512
                lw = wv_load[li]
                qrd = xnq_ready[tb // 8]
                war = vproj_ev[idx - 4] if idx >= 4 else None
                def f(t, li=li, tb=tb, bank=bank, lw=lw, qrd=qrd, war=war):
                    t.wait_ge(*lw)
                    t.wait_ge(*qrd)
                    if war is not None:
                        t.wait_ge(sem_v, war)
                    for cb in range(NCB):
                        ins = t.matmul(psum[:, bank:bank+256],
                                       xnT[:, cb*T + tb*128: cb*T + (tb+1)*128],
                                       wv_sl[li % 3][:, cb*256:(cb+1)*256],
                                       start=(cb == 0), stop=(cb == NCB-1))
                    ins.then_inc(sem_t, 1)
                em("tensor", f)
                vproj_mm.append(st.inc())
                if idx % 16 == 15:
                    wv_last_mm.append(vproj_mm[-1])

            def plan_v_ev(idx):
                dch = (idx % 128) // 16
                bank = (idx % 4) * 512
                mmw = vproj_mm[idx]
                war = v_store[idx - 4] if idx >= 4 else None
                svc = svst[idx % 4]
                def f(v, dch=dch, bank=bank, idx=idx, mmw=mmw, war=war, svc=svc):
                    v.wait_ge(sem_t, mmw)
                    v.wait_ge(sem_v, bvb_ready)
                    if war is not None:
                        v.wait_ge(svc.sem, war)
                    v.tensor_tensor(v_ev[idx % 4][:], psum[:, bank:bank+256],
                                    bvb[:, dch*256:(dch+1)*256], ALU.add).then_inc(sem_v, 1)
                em("vector", f)
                vproj_ev.append(sv.inc())

            def plan_v_st(idx):
                dch = (idx % 128) // 16
                tb = 16 * (idx // 128) + idx % 16
                evw = vproj_ev[idx]
                svc = svst[idx % 4]
                def f(g, dch=dch, tb=tb, idx=idx, evw=evw, svc=svc):
                    g.wait_ge(sem_v, evw)
                    g.dma_start(out=d["v_scr"][tb*128:(tb+1)*128, dch*256:(dch+1)*256],
                                in_=v_ev[idx % 4][:]).then_inc(svc.sem, 16)
                em("gpsimd", f)
                v_store.append(svc.inc())

            vp = {"ld": 3, "mm": 0, "ev": 0, "st": 0}
            NV = 256
            def plan_v(tb):
                qdone = min((tb + 1) // 8, 4)
                tmm = max(0, int((tb - 9) * 6.3))
                tev = max(0, int((tb - 10) * 6.3))
                tld = min(16, 3 + max(0, int((tb - 8) * 0.6)))
                while True:
                    did = False
                    if vp["ld"] < tld and len(wv_last_mm) >= vp["ld"] - 2:
                        plan_wv_load(vp["ld"]); vp["ld"] += 1; did = True
                    if vp["mm"] < tmm:
                        idx = vp["mm"]
                        if (idx // 16 < len(wv_load)
                                and (16 * (idx // 128) + idx % 16) // 8 < len(xnq_ready)
                                and (idx < 4 or idx - 4 < len(vproj_ev))):
                            plan_v_mm(idx); vp["mm"] += 1; did = True
                    if (vp["ev"] < min(tev, len(vproj_mm))
                            and (vp["ev"] < 4 or vp["ev"] - 4 < len(v_store))):
                        plan_v_ev(vp["ev"]); vp["ev"] += 1; did = True
                    if vp["st"] < len(vproj_ev):
                        plan_v_st(vp["st"]); vp["st"] += 1; did = True
                    if not did:
                        break

            sl = d["wq"][:, 0:128].rearrange("(cb p) n -> p cb n", p=128)
            def f(g, sl=sl):
                g.dma_start(out=wqk_q[:].rearrange("p (cb n) -> p cb n", n=128),
                            in_=sl).then_inc(swq.sem, 16)
            em("gpsimd", f)
            wq0_ld = swq.inc()
            sl = d["wk"][:, 0:128].rearrange("(cb p) n -> p cb n", p=128)
            def f(g, sl=sl):
                g.dma_start(out=wqk_k[:].rearrange("p (cb n) -> p cb n", n=128),
                            in_=sl).then_inc(swk.sem, 16)
            em("gpsimd", f)
            wk0_ld = swk.inc()

            LNH = float(H)
            xn_act, xn_store, xnq_ready = [], [], []
            for tb in range(NTB):
                buf = x_t[tb % 3]
                war = xn_act[tb-3] if tb >= 3 else None
                sxc = sx[tb % 3]
                def f(sy, tb=tb, buf=buf, war=war, sxc=sxc):
                    if war is not None:
                        sy.wait_ge(sem_s, war)
                    sy.dma_start(out=buf[:], in_=d["xp"][tb*128:(tb+1)*128, :]).then_inc(sxc.sem, 16)
                em("sync", f)
                ld = sxc.inc()
                def f(v, buf=buf, ld=ld, tb=tb, sxc=sxc):
                    v.wait_ge(sxc.sem, ld)
                    v.reduce_sum(sums[:, tb:tb+1], buf[:], axis=AX.X).then_inc(sem_v, 1)
                em("vector", f)
                vred = sv.inc()
                def f(s, buf=buf, ld=ld, tb=tb, sxc=sxc):
                    s.wait_ge(sxc.sem, ld)
                    if tb == 0:
                        s.wait_ge(sem_v, bvb_ready)
                    s.activation(psum[:, 2048:4096], buf[:], AF.Square,
                                 accum_out=sumsq[:, tb:tb+1]).then_inc(sem_s, 1)
                em("scalar", f)
                sred = ss.inc()
                def f(v, tb=tb, vred=vred):
                    v.wait_ge(sem_v, vred)
                    v.tensor_tensor(lnv[:, tb:tb+1], sums[:, tb:tb+1], sums[:, tb:tb+1],
                                    ALU.mult).then_inc(sem_v, 1)
                em("vector", f)
                vt1 = sv.inc()
                def f(v, tb=tb, vt1=vt1, sred=sred):
                    v.wait_ge(sem_v, vt1)
                    v.wait_ge(sem_s, sred)
                    v.scalar_tensor_tensor(sumsq[:, tb:tb+1], sumsq[:, tb:tb+1], LNH,
                                           lnv[:, tb:tb+1], ALU.mult,
                                           ALU.subtract).then_inc(sem_v, 1)
                em("vector", f)
                vX = sv.inc()
                def f(s, tb=tb, vX=vX):
                    s.wait_ge(sem_v, vX)
                    s.activation(lnv[:, tb:tb+1], sumsq[:, tb:tb+1], AF.Ln,
                                 bias=eps2_c[:, 0:1]).then_inc(sem_s, 1)
                em("scalar", f)
                sln = ss.inc()
                def f(s, tb=tb, sln=sln):
                    s.wait_ge(sem_s, sln)
                    s.activation(rstd[:, tb:tb+1], lnv[:, tb:tb+1], AF.Exp, scale=-0.5,
                                 bias=lnh_c[:, 0:1]).then_inc(sem_s, 1)
                em("scalar", f)
                srs = ss.inc()
                def f(v, tb=tb, srs=srs):
                    v.wait_ge(sem_s, srs)
                    v.scalar_tensor_tensor(negmr[:, tb:tb+1], sums[:, tb:tb+1], -1.0/LNH,
                                           rstd[:, tb:tb+1], ALU.mult,
                                           ALU.mult).then_inc(sem_v, 1)
                em("vector", f)
                vng = sv.inc()
                sxnc = sxn[tb % 2]
                war2 = xn_store[tb-2] if tb >= 2 else None
                def f(s, tb=tb, buf=buf, vng=vng, war2=war2, sxnc=sxnc):
                    s.wait_ge(sem_v, vng)
                    if war2 is not None:
                        s.wait_ge(sxnc.sem, war2)
                    s.activation(xn_t[tb % 2][:], buf[:], AF.Identity,
                                 bias=negmr[:, tb:tb+1], scale=rstd[:, tb:tb+1]).then_inc(sem_s, 1)
                em("scalar", f)
                xn_act.append(ss.inc())
                act = xn_act[tb]
                def f(g, tb=tb, act=act, sxnc=sxnc):
                    g.wait_ge(sem_s, act)
                    g.dma_start(out=d["xn_scr"][tb*128:(tb+1)*128, :],
                                in_=xn_t[tb % 2][:]).then_inc(sxnc.sem, 16)
                em("gpsimd", f)
                xn_store.append(sxnc.inc())
                if (tb + 1) % 8 == 0:
                    qt = tb // 8
                    snap = (sxn[0].n, sxn[1].n)
                    sxq = sxnt[qt]
                    def f(sy, qt=qt, snap=snap, sxq=sxq):
                        sy.wait_ge(sxn[0].sem, snap[0])
                        sy.wait_ge(sxn[1].sem, snap[1])
                        for cb in range(NCB):
                            sy.dma_start(out=xnT[:, cb*T + qt*1024: cb*T + (qt+1)*1024],
                                         in_=d["xn_scr"][qt*1024:(qt+1)*1024,
                                                         cb*128:(cb+1)*128],
                                         transpose=True).then_inc(sxq.sem, 16)
                    em("sync", f)
                    for _ in range(NCB):
                        sxq.inc()
                    xnq_ready.append((sxq.sem, sxq.n))
                plan_v(tb)

            xnT_ready = list(xnq_ready)

            while vp["mm"] < NV or vp["ev"] < NV or vp["st"] < NV or vp["ld"] < 16:
                plan_v(999)
            v_ready = tuple((c.sem, c.n) for c in svst)


        barrier()
        with ExitStack() as S2:
            E2 = S2.enter_context
            expT = E2(nc.sbuf_tensor("expT", [128, 32 * W], BF16))
            cos_s = E2(nc.sbuf_tensor("cos_s", [128, T], BF16))
            sin_s = E2(nc.sbuf_tensor("sin_s", [128, T], BF16))
            qrotT = E2(nc.sbuf_tensor("qrotT", [128, OWN], BF16))
            krotT = E2(nc.sbuf_tensor("krotT", [128, T], BF16))
            v_head = E2(nc.sbuf_tensor("v_head", [128, 32 * 128], BF16))

            def f(sy):
                sy.dma_start(out=cos_s[:], in_=d["cos2"]).then_inc(stab.sem, 16)
                sy.dma_start(out=sin_s[:], in_=d["sin2s"]).then_inc(stab.sem, 16)
            em("sync", f)
            stab.inc()
            tables_ready = stab.inc()
            stb = [E2(nc.sbuf_tensor(f"stb{i}", [128, 512], F32)) for i in range(3)]
            fcr = E2(nc.sbuf_tensor("fcr", [128, 160], F32))
            prr = E2(nc.sbuf_tensor("prr", [128, 160], F32))
            fsum2 = E2(nc.sbuf_tensor("fsum2", [128, 2], F32))
            dmp2 = E2(nc.sbuf_tensor("dmp2", [128, 512], BF16))
            qe_b = E2(nc.sbuf_tensor("qe_b", [128, 1024], BF16))
            qsw_b = E2(nc.sbuf_tensor("qsw_b", [128, 1024], BF16))
            rt1 = E2(nc.sbuf_tensor("rt1", [128, 1024], BF16))
            rt2 = E2(nc.sbuf_tensor("rt2", [128, 1024], BF16))
            rb_sb = E2(nc.sbuf_tensor("rb_sb", [128, W], F32))
            o_sb = [E2(nc.sbuf_tensor("o_sb0", [128, W], BF16))] * 2

            REG = [0, 1024, 3072]
            DF0, RB0, OT0 = 2048, 2304, 2560
            QKB = 3072
            GROUPS = [list(range(4*g, 4*g+4)) for g in range(8)]

            wfc_flat = d["wfcT"].rearrange("a b -> (a b)")
            wpr_flat = d["wprT"].rearrange("a b -> (a b)")
            stat_load_done = [None] * 320
            stat_red_done = [None] * 320

            def emit_stat_load(i):
                buf = stb[i % 3]
                src = wfc_flat if i < 160 else wpr_flat
                j = i if i < 160 else i - 160
                sl = src[j * 65536:(j + 1) * 65536].rearrange("(p m) -> p m", p=128)
                sstc = sstat[i % 3]
                def f(sy, sl=sl, buf=buf, i=i, sstc=sstc):
                    if i >= 3:
                        war = stat_red_done[i - 3]
                        sy.wait_ge(war[0], war[1])
                    sy.dma_start(out=buf[:], in_=sl).then_inc(sstc.sem, 16)
                em("sync", f)
                stat_load_done[i] = (sstc.sem, sstc.inc())

            def emit_stat_red(i):
                buf = stb[i % 3]
                ld = stat_load_done[i]
                if i < 160:
                    def f(v, buf=buf, ld=ld, i=i):
                        v.wait_ge(*ld)
                        v.reduce_sum(fcr[:, i:i+1], buf[:], axis=AX.X,
                                     apply_absolute_value=True).then_inc(sem_v, 1)
                    em("vector", f)
                    stat_red_done[i] = (sem_v, sv.inc())
                else:
                    pa = stat_red_done[i-1] if (i > 160 and stat_red_done[i-1] is not None
                                                and stat_red_done[i-1][0] is sem_s) else None
                    def f(s, buf=buf, ld=ld, i=i, pa=pa):
                        s.wait_ge(*ld)
                        if pa is not None:
                            s.wait_ge(*pa)
                        s.activation(dmp2[:], buf[:], AF.Abs,
                                     accum_out=prr[:, i-160:i-159]).then_inc(sem_s, 1)
                    em("scalar", f)
                    stat_red_done[i] = (sem_s, ss.inc())

            HEAD_MIL = []
            PENDING_WQK = {}
            P3 = [0]
            LASTREG = [None]
            REGION_LAST_EXP = {}
            last_c_exp = [None]

            def plan_head(h):
                mil = {}
                prev = HEAD_MIL[h-1] if h > 0 else None

                if h == 0:
                    mil["wq_ld"] = wq0_ld
                    mil["wk_ld"] = wk0_ld
                else:
                    mil["wq_ld"], mil["wk_ld"] = PENDING_WQK[h]
                for k in range(20*h, 20*h+20):
                    emit_stat_load(k)
                RQ = list(range(20*h, 20*h+20))
                def reds_until(done_groups, RQ=RQ):
                    tgt = (done_groups * 20) // 64
                    while (20 - len(RQ)) < tgt:
                        emit_stat_red(RQ.pop(0))

                src = d["v_scr"][:, h*128:(h+1)*128].rearrange("(kb p) n -> p kb n", p=128)
                def f(g, src=src, prev=prev):
                    if prev is not None:
                        g.wait_ge(sem_t, prev["t_last"])
                    else:
                        for _s, _v in v_ready:
                            g.wait_ge(_s, _v)
                    g.dma_start(out=v_head[:].rearrange("p (kb n) -> p kb n", n=128),
                                in_=src).then_inc(svh.sem, 16)
                em("gpsimd", f)
                mil["v_ld"] = svh.inc()

                qkb_war = last_c_exp[0]
                war3 = prev["t_last"] if prev else None
                qev = []
                rope_piece = []
                for j in range(4):
                    bank = QKB + (j % 2) * 512
                    wq_ld = mil["wq_ld"]
                    war = qev[j-2] if j >= 2 else (prev["kev_last"] if prev else None)
                    def f(t, j=j, bank=bank, wq_ld=wq_ld, war=war, h=h, qkb_war=qkb_war):
                        t.wait_ge(swq.sem, wq_ld)
                        if h == 0 and j == 0:
                            for _sw, _sv in xnT_ready:
                                t.wait_ge(_sw, _sv)
                        if j == 0 and qkb_war is not None:
                            t.wait_ge(sem_s, qkb_war)
                        if war is not None:
                            t.wait_ge(sem_v, war)
                        for cb in range(NCB):
                            ins = t.matmul(psum[:, bank:bank+512],
                                           wqk_q[:, cb*128:(cb+1)*128],
                                           xnT[:, cb*T + j*512: cb*T + (j+1)*512],
                                           start=(cb == 0), stop=(cb == NCB-1))
                            if cb == NCB-1:
                                ins.then_inc(sem_t, 1)
                    em("tensor", f)
                    mm = st.inc()
                    war2 = rope_piece[j//2 - 1] if j >= 2 else (prev["ropek1"] if prev else None)
                    def f(v, j=j, bank=bank, mm=mm, h=h, war2=war2):
                        v.wait_ge(sem_t, mm)
                        if j % 2 == 0 and war2 is not None:
                            v.wait_ge(sem_v, war2)
                        v.tensor_scalar(qe_b[:, (j % 2)*512:(j % 2)*512+512], psum[:, bank:bank+512],
                                        bqc_s[:, h:h+1], None, ALU.add).then_inc(sem_v, 1)
                    em("vector", f)
                    qev.append(sv.inc())
                    if j % 2 == 1:
                        p = j // 2
                        ev2 = qev[-1]
                        def f(g, ev2=ev2):
                            g.wait_ge(sem_v, ev2)
                            g.dma_start(out=qsw_b[0:64, :], in_=qe_b[64:128, :]).then_inc(ssw.sem, 16)
                            g.dma_start(out=qsw_b[64:128, :], in_=qe_b[0:64, :]).then_inc(ssw.sem, 16)
                        em("gpsimd", f)
                        ssw.inc()
                        swp = ssw.inc()
                        def f(v, swp=swp, p=p, h=h, war3=war3):
                            v.wait_ge(ssw.sem, swp)
                            if p == 0 and h == 0:
                                v.wait_ge(stab.sem, tables_ready)
                            v.tensor_tensor(rt1[:], qe_b[:], cos_s[:, p*1024:(p+1)*1024], ALU.mult)
                            v.tensor_tensor(rt2[:], qsw_b[:], sin_s[:, p*1024:(p+1)*1024],
                                            ALU.mult).then_inc(sem_v, 1)
                        em("vector", f)
                        eq = sv.inc()
                        def f(v, eq=eq, war3=war3, p=p):
                            v.wait_ge(sem_v, eq)
                            if p == 0 and war3 is not None:
                                v.wait_ge(sem_t, war3)
                            v.tensor_tensor(qrotT[:, p*1024:(p+1)*1024], rt1[:], rt2[:],
                                            ALU.add).then_inc(sem_v, 1)
                        em("vector", f)
                        rope_piece.append(sv.inc())
                mil["qproj_mm"] = st.n
                mil["ropeq"] = rope_piece[-1]

                kev_prev = qev
                krope_piece = list(rope_piece)
                for kh in range(2):
                    kev = []
                    for j in range(4):
                        bank = QKB + (j % 2) * 512
                        tcol = kh * 2048 + j * 512
                        wk_ld = mil["wk_ld"]
                        war = kev[j-2] if j >= 2 else kev_prev[2+j]
                        def f(t, bank=bank, tcol=tcol, wk_ld=wk_ld, war=war):
                            t.wait_ge(swk.sem, wk_ld)
                            t.wait_ge(sem_v, war)
                            for cb in range(NCB):
                                ins = t.matmul(psum[:, bank:bank+512],
                                               wqk_k[:, cb*128:(cb+1)*128],
                                               xnT[:, cb*T + tcol: cb*T + tcol + 512],
                                               start=(cb == 0), stop=(cb == NCB-1))
                                if cb == NCB-1:
                                    ins.then_inc(sem_t, 1)
                        em("tensor", f)
                        mm = st.inc()
                        war2 = krope_piece[-1] if j % 2 == 0 else None
                        def f(v, j=j, bank=bank, mm=mm, h=h, war2=war2):
                            v.wait_ge(sem_t, mm)
                            if war2 is not None:
                                v.wait_ge(sem_v, war2)
                            v.tensor_scalar(qe_b[:, (j % 2)*512:(j % 2)*512+512],
                                            psum[:, bank:bank+512],
                                            bkc_s[:, h:h+1], None, ALU.add).then_inc(sem_v, 1)
                        em("vector", f)
                        kev.append(sv.inc())
                        if j % 2 == 1:
                            p = j // 2
                            off = kh*2048 + p*1024
                            ev2 = kev[-1]
                            def f(g, ev2=ev2):
                                g.wait_ge(sem_v, ev2)
                                g.dma_start(out=qsw_b[0:64, :], in_=qe_b[64:128, :]).then_inc(ssw.sem, 16)
                                g.dma_start(out=qsw_b[64:128, :], in_=qe_b[0:64, :]).then_inc(ssw.sem, 16)
                            em("gpsimd", f)
                            ssw.inc()
                            swp = ssw.inc()
                            def f(v, swp=swp, off=off):
                                v.wait_ge(ssw.sem, swp)
                                v.tensor_tensor(rt1[:], qe_b[:], cos_s[:, off:off+1024], ALU.mult)
                                v.tensor_tensor(rt2[:], qsw_b[:], sin_s[:, off:off+1024],
                                                ALU.mult).then_inc(sem_v, 1)
                            em("vector", f)
                            ek = sv.inc()
                            war4 = prev["t_last"] if (prev and kh == 0 and p == 0) else None
                            def f(v, ek=ek, war4=war4, off=off):
                                v.wait_ge(sem_v, ek)
                                if war4 is not None:
                                    v.wait_ge(sem_t, war4)
                                v.tensor_tensor(krotT[:, off:off+1024], rt1[:], rt2[:],
                                                ALU.add).then_inc(sem_v, 1)
                            em("vector", f)
                            krope_piece.append(sv.inc())
                    kev_prev = kev
                    mil["ropek%d" % kh] = krope_piece[-1]
                mil["kev_last"] = kev_prev[3]
                mil["kproj_mm"] = st.n

                NG = 8
                exp_done = [[None]*NG for _ in range(NCH)]
                ones_done = [[None, None] for _ in range(NCH)]
                mask_done = [[None, None] for _ in range(NCH)]
                recip_done = [None] * NCH
                av_done = [[None, None] for _ in range(NCH)]
                evac_done = [None] * NCH
                o_stores = [None] * NCH

                def emit_group(c, gi):
                    lg = c * 8 + gi
                    if lg < 58:
                        reg = REG[P3[0] % 3]
                        P3[0] += 1
                        if reg == LASTREG[0]:
                            reg = REG[P3[0] % 3]
                            P3[0] += 1
                    else:
                        reg = REG[0] if LASTREG[0] != REG[0] else REG[1]
                    LASTREG[0] = reg
                    grp = GROUPS[gi]
                    war = REGION_LAST_EXP.get(reg)
                    rk = mil["ropek0"] if gi < 4 else mil["ropek1"]
                    kevw = mil["kev_last"] if reg == QKB else None
                    def f(t, c=c, reg=reg, grp=tuple(grp), war=war, rk=rk, kevw=kevw):
                        if war is not None:
                            t.wait_ge(sem_s, war)
                        if kevw is not None:
                            t.wait_ge(sem_v, kevw)
                        t.wait_ge(sem_v, rk)
                        for kb in grp:
                            pos = kb - grp[0]
                            ins = t.matmul(psum[:, reg+pos*256: reg+pos*256+256],
                                           krotT[:, kb*128:(kb+1)*128],
                                           qrotT[:, c*W:(c+1)*W],
                                           start=(pos % 2 == 0), stop=(pos % 2 == 1))
                            if kb == grp[-1]:
                                ins.then_inc(sem_t, 1)
                    em("tensor", f)
                    gmm = st.inc()
                    g0 = GROUPS[gi][0]
                    half = 0 if gi < NG // 2 else 1
                    if c >= 1:
                        free_war = ("avh", c - 1, half)
                    elif prev is not None:
                        free_war = ("pt", prev["t_last"], None)
                    else:
                        free_war = None
                    def f(s, reg=reg, g0=g0, gmm=gmm, free_war=free_war):
                        s.wait_ge(sem_t, gmm)
                        if free_war is not None:
                            if free_war[0] == "avh":
                                s.wait_ge(sem_t, av_done[free_war[1]][free_war[2]])
                            else:
                                s.wait_ge(sem_t, free_war[1])
                        s.activation(expT[:, g0*W:(g0+4)*W], psum[:, reg:reg+1024],
                                     AF.Exp, scale=SCALE).then_inc(sem_s, 1)
                    em("scalar", f)
                    ev = ss.inc()
                    exp_done[c][gi] = ev
                    REGION_LAST_EXP[reg] = ev
                    if reg == REG[2]:
                        last_c_exp[0] = ev

                def emit_ones(c, half):
                    kbs = range(16) if half == 0 else range(16, 32)
                    war = (recip_done[c-1] if c >= 1 else
                           (prev["recip_last"] if prev else None)) if half == 0 else None
                    lastexp = exp_done[c][3 if half == 0 else 7]
                    def f(t, kbs=tuple(kbs), war=war, half=half, lastexp=lastexp):
                        t.wait_ge(sem_s, lastexp)
                        if war is not None:
                            t.wait_ge(sem_v, war)
                        for kb in kbs:
                            ins = t.matmul(psum[:, DF0:DF0+W], ones_cb[:],
                                           expT[:, kb*W:(kb+1)*W],
                                           start=(kb == 0), stop=(kb == 31))
                            if kb == kbs[-1]:
                                ins.then_inc(sem_t, 1)
                    em("tensor", f)
                    ones_done[c][half] = st.inc()

                def emit_masks(c, half):
                    j0, j1 = 2*c, 2*c+1
                    if half == 0:
                        slices = [(j0*W, j0*W+128, mka_s), (j1*W+128, j1*W+256, mka_s)]
                    else:
                        slices = [((16+j0)*W, (16+j0)*W+128, mkb_s),
                                  ((16+j1)*W+128, (16+j1)*W+256, mkb_s)]
                    def f(v, c=c, half=half, slices=tuple(slices)):
                        v.wait_ge(sem_t, ones_done[c][half])
                        for a, b, mk in slices:
                            ins = v.tensor_tensor(expT[:, a:b], expT[:, a:b], mk[:], ALU.mult)
                        ins.then_inc(sem_v, 1)
                    em("vector", f)
                    mask_done[c][half] = sv.inc()

                def emit_av(c, half):
                    valid = [kb for kb in (range(16) if half == 0 else range(16, 32))
                             if (kb % 16) <= 2*c+1]
                    vld = mil["v_ld"]
                    war = (evac_done[c-1] if c >= 1 else
                           (prev["evac_last"] if prev else None)) if half == 0 else None
                    def f(t, c=c, half=half, valid=tuple(valid), vld=vld, war=war):
                        t.wait_ge(sem_v, mask_done[c][half])
                        t.wait_ge(svh.sem, vld)
                        if war is not None:
                            t.wait_ge(sem_v, war)
                        for i, kb in enumerate(valid):
                            kj = kb % 16
                            cs = 128 if kj == 2*c+1 else 0
                            ins = t.matmul(psum[:, OT0+cs:OT0+256],
                                           v_head[:, kb*128:(kb+1)*128],
                                           expT[:, kb*W+cs:(kb+1)*W],
                                           start=(half == 0 and i == 0),
                                           stop=(half == 1 and i == len(valid)-1))
                            if i == len(valid)-1:
                                ins.then_inc(sem_t, 1)
                    em("tensor", f)
                    av_done[c][half] = st.inc()

                def emit_recip(c):
                    war = evac_done[c-1] if c >= 1 else (prev["evac_last"] if prev else None)
                    def f(v, c=c, war=war):
                        v.wait_ge(sem_t, ones_done[c][1])
                        if war is not None:
                            v.wait_ge(sem_v, war)
                        v.reciprocal(rb_sb[:], psum[:, DF0:DF0+W]).then_inc(sem_v, 1)
                    em("vector", f)
                    recip_done[c] = sv.inc()

                def emit_bcast(c):
                    war = o_stores[c-1] if c >= 1 else (prev["o_last"] if prev else None)
                    def f(v, c=c, war=war):
                        v.wait_ge(sem_v, recip_done[c])
                        v.wait_ge(sem_t, av_done[c][1])
                        if war is not None:
                            v.wait_ge(sost.sem, war)
                        v.tensor_tensor(o_sb[c % 2][:], psum[:, OT0:OT0+W], rb_sb[:],
                                        ALU.mult).then_inc(sem_v, 1)
                    em("vector", f)
                    evac_done[c] = sv.inc()
                    def f(g, c=c, h=h):
                        g.wait_ge(sem_v, evac_done[c])
                        g.dma_start(out=d["o_scr"][h*128:(h+1)*128, c*W:(c+1)*W],
                                    in_=o_sb[c % 2][:]).then_inc(sost.sem, 16)
                    em("gpsimd", f)
                    o_stores[c] = sost.inc()

                def emit_p2(c):
                    emit_ones(c, 0)
                    emit_masks(c, 0)
                    emit_av(c, 0)
                    emit_ones(c, 1)
                    emit_masks(c, 1)
                    emit_recip(c)
                    emit_av(c, 1)
                    emit_bcast(c)

                for c in range(NCH):
                    if c >= 1:
                        emit_ones(c-1, 0)
                        emit_masks(c-1, 0)
                        emit_av(c-1, 0)
                    emit_group(c, 0)
                    emit_group(c, 1)
                    if c >= 1:
                        emit_ones(c-1, 1)
                        emit_masks(c-1, 1)
                        emit_recip(c-1)
                    emit_group(c, 2)
                    reds_until(c*8 + 3)
                    emit_group(c, 3)
                    if c >= 1:
                        emit_av(c-1, 1)
                        emit_bcast(c-1)
                    emit_group(c, 4)
                    reds_until(c*8 + 5)
                    emit_group(c, 5)
                    emit_group(c, 6)
                    reds_until(c*8 + 7)
                    emit_group(c, 7)
                    reds_until(c*8 + 8)
                    if c == 1 and h + 1 < NH:
                        sl = d["wq"][:, (h+1)*128:(h+2)*128].rearrange(
                            "(cb p) n -> p cb n", p=128)
                        qpw = mil["qproj_mm"]
                        def f(g, sl=sl, qpw=qpw):
                            g.wait_ge(sem_t, qpw)
                            g.dma_start(out=wqk_q[:].rearrange("p (cb n) -> p cb n", n=128),
                                        in_=sl).then_inc(swq.sem, 16)
                        em("gpsimd", f)
                        wqv = swq.inc()
                        sl = d["wk"][:, (h+1)*128:(h+2)*128].rearrange(
                            "(cb p) n -> p cb n", p=128)
                        kpw = mil["kproj_mm"]
                        def f(g, sl=sl, kpw=kpw):
                            g.wait_ge(sem_t, kpw)
                            g.dma_start(out=wqk_k[:].rearrange("p (cb n) -> p cb n", n=128),
                                        in_=sl).then_inc(swk.sem, 16)
                        em("gpsimd", f)
                        PENDING_WQK[h+1] = (wqv, swk.inc())
                emit_p2(NCH-1)
                while RQ:
                    emit_stat_red(RQ.pop(0))

                mil["t_last"] = st.n
                mil["exp_last"] = exp_done[NCH-1][NG-1]
                mil["evac_last"] = evac_done[NCH-1]
                mil["recip_last"] = recip_done[NCH-1]
                mil["o_last"] = o_stores[NCH-1]
                return mil

            for h in range(NH):
                HEAD_MIL.append(plan_head(h))
            o_ready = HEAD_MIL[-1]["o_last"]

            vW2, sW2, tW2 = sv.n, ss.n, st.n
            def f(v, vW2=vW2, sW2=sW2):
                v.wait_ge(sem_v, vW2)
                v.wait_ge(sem_s, sW2)
                v.reduce_sum(fsum2[:, 0:1], fcr[:], axis=AX.X)
                v.reduce_sum(fsum2[:, 1:2], prr[:], axis=AX.X).then_inc(sem_v, 1)
            em("vector", f); fin1 = sv.inc()
            def f(t, fin1=fin1, tW2=tW2):
                t.wait_ge(sem_v, fin1)
                t.wait_ge(sem_t, tW2)
                t.matmul(psum[0:1, 0:1], fsum2[:, 0:1], ones_cf[:], start=True, stop=False)
                t.matmul(psum[0:1, 1:2], fsum2[:, 1:2], ones_cf[:], start=False,
                         stop=True).then_inc(sem_t, 1)
            em("tensor", f); fin2 = st.inc()
            NWF = float(H * FFH)
            def f(v, fin2=fin2):
                v.wait_ge(sem_t, fin2)
                v.tensor_scalar(am11[0:1, 0:2], psum[0:1, 0:2], 1.0/NWF, 1e-5,
                                ALU.mult, ALU.max).then_inc(sem_v, 1)
            em("vector", f); fin3 = sv.inc()
            def f(v, fin3=fin3):
                v.wait_ge(sem_v, fin3)
                v.tensor_scalar(am11[0:1, 2:3], am11[0:1, 0:1], 0.7, None, ALU.mult)
                v.tensor_scalar(am11[0:1, 3:4], am11[0:1, 0:1], -0.7, None, ALU.mult)
                v.tensor_scalar(am11[0:1, 4:5], am11[0:1, 0:1], -1.0, None, ALU.mult)
                v.tensor_scalar(am11[0:1, 5:6], am11[0:1, 1:2], 0.7, None, ALU.mult)
                v.tensor_scalar(am11[0:1, 6:7], am11[0:1, 1:2], -0.7, None, ALU.mult)
                v.tensor_scalar(am11[0:1, 7:8], am11[0:1, 1:2], -1.0, None,
                                ALU.mult).then_inc(sem_v, 1)
            em("vector", f); fin4 = sv.inc()
            def f(t, fin4=fin4):
                t.wait_ge(sem_v, fin4)
                for j in range(6):
                    ins = t.matmul(psum[:, 8+j:9+j], ones_rf[0:1, :], am11[0:1, 2+j:3+j],
                                   start=(j == 0), stop=(j == 5))
                ins.then_inc(sem_t, 1)
            em("tensor", f); fin5 = st.inc()
            def f(v, fin5=fin5):
                v.wait_ge(sem_t, fin5)
                v.tensor_copy(tcols[:], psum[:, 8:14]).then_inc(sem_v, 1)
            em("vector", f); tern_ready = sv.inc()

        wqk_stack.close()

        barrier()
        with ExitStack() as S3:
            E3 = S3.enter_context
            wfc_b = [E3(nc.sbuf_tensor(f"wfc_b{i}", [128, H], F32)) for i in range(2)]
            wpr_b = [E3(nc.sbuf_tensor(f"wpr_b{i}", [128, H], F32)) for i in range(2)]
            qprs = [E3(nc.sbuf_tensor("qprs0", [128, H], BF16))] * 2
            qfc = [E3(nc.sbuf_tensor(f"qfc{i}", [128, H], BF16)) for i in range(2)]
            gtb = E3(nc.sbuf_tensor("gtb", [128, H], BF16))
            relu_t = E3(nc.sbuf_tensor("relu_t", [128, 1024], F32))
            hT_t = [E3(nc.sbuf_tensor("hT_t0", [128, OWN], BF16))] * 2

            pr_tern_d, pr_store_d = [], []
            def emit_wpr_strip(fb):
                sl = d["wprT"][fb*128:(fb+1)*128, :]
                war = pr_store_d[fb-2] if fb >= 2 else None
                spc = swpr[fb % 2]
                def f(sy, sl=sl, war=war, fb=fb, spc=spc):
                    if war is not None:
                        sy.wait_ge(*war)
                    sy.dma_start(out=wpr_b[fb % 2][:], in_=sl).then_inc(spc.sem, 16)
                em("sync", f)
                ld = spc.inc()
                gwar = fc_tern[-1] if fc_tern else None
                gwar2 = pr_tern_d[-1] if pr_tern_d else None
                def f(v, fb=fb, ld=ld, spc=spc, gwar=gwar, gwar2=gwar2):
                    v.wait_ge(spc.sem, ld)
                    v.wait_ge(sem_v, tern_ready)
                    if gwar is not None:
                        v.wait_ge(sem_v, gwar)
                    if gwar2 is not None:
                        v.wait_ge(sem_v, gwar2)
                    v.tensor_scalar(gtb[:], wpr_b[fb % 2][:], tcols[:, 3:4], None,
                                    ALU.is_gt).then_inc(sem_v, 1)
                em("vector", f)
                ed = sv.inc()
                swar = pr_store_d[-1] if pr_store_d else None
                def f(v, fb=fb, ed=ed, swar=swar):
                    v.wait_ge(sem_v, ed)
                    if swar is not None:
                        v.wait_ge(*swar)
                    v.scalar_tensor_tensor(qprs[fb % 2][:], wpr_b[fb % 2][:],
                                           tcols[:, 4:5], gtb[:], ALU.is_lt,
                                           ALU.subtract).then_inc(sem_v, 1)
                em("vector", f)
                pr_tern_d.append(sv.inc())
                trn = pr_tern_d[-1]
                srsc = srst[fb % 2]
                def f(g, fb=fb, trn=trn, srsc=srsc):
                    g.wait_ge(sem_v, trn)
                    for q in range(4):
                        g.dma_start(out=d["qpr_scr"][fb][q],
                                    in_=qprs[fb % 2][:, q*512:(q+1)*512]).then_inc(srsc.sem, 16)
                em("gpsimd", f)
                for _ in range(4):
                    srsc.inc()
                pr_store_d.append((srsc.sem, srsc.n))

            fc_ld, fc_tern, fc_mm, fc_relu, fc_sq, fc_st = [], [], [], [], [], []
            for fb in range(NFB):
                sl = d["wfc3"][fb]
                war = fc_tern[fb-2] if fb >= 2 else None
                sfc = swfc[fb % 2]
                def f(sy, sl=sl, fb=fb, war=war, sfc=sfc):
                    if war is not None:
                        sy.wait_ge(sem_v, war)
                    sy.dma_start(out=wfc_b[fb % 2][:], in_=sl).then_inc(sfc.sem, 16)
                em("sync", f)
                fc_ld.append(sfc.inc())
                ld = fc_ld[fb]
                war = fc_mm[fb-2] if fb >= 2 else None
                gwar = pr_tern_d[-1] if pr_tern_d else None
                def f(v, fb=fb, ld=ld, war=war, sfc=sfc, gwar=gwar):
                    v.wait_ge(sfc.sem, ld)
                    v.wait_ge(sem_v, tern_ready)
                    if war is not None:
                        v.wait_ge(sem_t, war)
                    if gwar is not None:
                        v.wait_ge(sem_v, gwar)
                    v.tensor_scalar(gtb[:], wfc_b[fb % 2][:], tcols[:, 0:1], None,
                                    ALU.is_gt).then_inc(sem_v, 1)
                em("vector", f)
                ed = sv.inc()
                def f(v, fb=fb, ed=ed):
                    v.wait_ge(sem_v, ed)
                    v.scalar_tensor_tensor(qfc[fb % 2][:], wfc_b[fb % 2][:], tcols[:, 1:2],
                                           gtb[:], ALU.is_lt, ALU.subtract).then_inc(sem_v, 1)
                em("vector", f)
                fc_tern.append(sv.inc())
                emit_wpr_strip(fb)
                reg = (fb % 2) * 2048
                trn = fc_tern[fb]
                war = fc_relu[4*(fb-2)+3] if fb >= 2 else None
                def f(t, fb=fb, reg=reg, trn=trn, war=war):
                    t.wait_ge(sem_v, trn)
                    if war is not None:
                        t.wait_ge(sem_s, war)
                    for tc in range(4):
                        for cb in range(NCB):
                            ins = t.matmul(psum[:, reg + tc*512: reg + (tc+1)*512],
                                           qfc[fb % 2][:, cb*128:(cb+1)*128],
                                           xnT[:, cb*T + tc*512: cb*T + (tc+1)*512],
                                           start=(cb == 0), stop=(cb == NCB-1))
                        ins.then_inc(sem_t, 1)
                em("tensor", f)
                mmv = [st.inc() for _ in range(4)]
                fc_mm.append(mmv[3])
                for hh in range(4):
                    gi = 4*fb + hh
                    rwar = fc_sq[gi-2] if gi >= 2 else None
                    mmw = mmv[hh]
                    def f(s, reg=reg, hh=hh, rwar=rwar, mmw=mmw):
                        s.wait_ge(sem_t, mmw)
                        if rwar is not None:
                            s.wait_ge(sem_s, rwar)
                        s.activation(relu_t[:, (hh % 2)*512:(hh % 2)*512+512],
                                     psum[:, reg+hh*512:reg+(hh+1)*512],
                                     AF.Relu, scale=tcols[:, 2:3]).then_inc(sem_s, 1)
                    em("scalar", f)
                    fc_relu.append(ss.inc())
                    rv = fc_relu[-1]
                    war2 = (shT[(fb-1) % 2].sem, fc_st[fb-1]) if (hh == 0 and fb >= 1) else None
                    def f(s, fb=fb, hh=hh, rv=rv, war2=war2):
                        s.wait_ge(sem_s, rv)
                        if war2 is not None:
                            s.wait_ge(*war2)
                        s.activation(hT_t[fb % 2][:, hh*512:(hh+1)*512],
                                     relu_t[:, (hh % 2)*512:(hh % 2)*512+512],
                                     AF.Square).then_inc(sem_s, 1)
                    em("scalar", f)
                    fc_sq.append(ss.inc())
                shc = shT[fb % 2]
                def f(g, fb=fb, shc=shc):
                    g.wait_ge(sem_s, fc_sq[4*fb+3])
                    for tcx in range(NCH):
                        g.dma_start(out=d["hT_scr"][tcx][:, fb],
                                    in_=hT_t[fb % 2][:, tcx*256:(tcx+1)*256]).then_inc(shc.sem, 16)
                em("gpsimd", f)
                for _ in range(NCH):
                    shc.inc()
                fc_st.append(shc.n)
            hT_ready = (shT[0].n, shT[1].n)
        xnT_stack.close()

        barrier()
        wo_sl = [d["wo"][grp*512:(grp+1)*512, :].rearrange("(ob p) n -> p ob n", p=128)
                 for grp in range(4)]
        wo_h01 = E0(nc.sbuf_tensor("wo_h01", [128, 8 * H], BF16))
        bob = E0(nc.sbuf_tensor("bob", [1, H], BF16))
        def f(g):
            g.dma_start(out=wo_h01[:, 0:4*H].rearrange("p (ob n) -> p ob n", n=H),
                        in_=wo_sl[0]).then_inc(swo[0].sem, 16)
            g.dma_start(out=bob[:], in_=d["bor"]).then_inc(sbo.sem, 16)
            g.dma_start(out=wo_h01[:, 4*H:8*H].rearrange("p (ob n) -> p ob n", n=H),
                        in_=wo_sl[1]).then_inc(swo[1].sem, 16)
        em("gpsimd", f)
        swo[0].inc(); sbo.inc(); swo[1].inc()
        with ExitStack() as S4:
            E4 = S4.enter_context
            qpr = [E4(nc.sbuf_tensor(f"qpr{i}", [128, NFB * 512], BF16)) for i in range(2)]
            hT_tc = [E4(nc.sbuf_tensor(f"hT_tc{i}", [128, NFB * 256], BF16)) for i in range(3)]
            x_e = [E4(nc.sbuf_tensor(f"x_e{i}", [128, 512], F32)) for i in range(2)]
            res_e = [E4(nc.sbuf_tensor(f"res_e{i}", [128, 512], F32)) for i in range(2)]

            res_stores, ev_all, mm_all = [], [], []
            qpr_ready_vals = (srst[0].n, srst[1].n)
            qpr_ld = []
            def emit_qpr_load(q):
                war = mm_all[16*(q-1) - 1] if q >= 2 else None
                spc = swpr[q % 2]
                def f(sy, q=q, war=war, spc=spc):
                    if q < 2:
                        sy.wait_ge(srst[0].sem, qpr_ready_vals[0])
                        sy.wait_ge(srst[1].sem, qpr_ready_vals[1])
                    if war is not None:
                        sy.wait_ge(sem_t, war)
                    for fb in range(NFB):
                        sy.dma_start(out=qpr[q % 2][:, fb*512:(fb+1)*512],
                                     in_=d["qpr_scr"][fb][q]).then_inc(spc.sem, 16)
                em("sync", f)
                for _ in range(NFB):
                    spc.inc()
                qpr_ld.append((spc.sem, spc.n))

            def emit_quarter(q):
                tc_ld = []
                for tc in range(NCH):
                    gi = q * NCH + tc
                    war = mm_all[-5] if len(mm_all) >= 5 else None
                    shtcc = shtc[gi % 3]
                    src = d["hT_scr"][tc]
                    def f(sy, src=src, tc=tc, gi=gi, q=q, war=war, shtcc=shtcc):
                        if tc < 3 and q == 0:
                            sy.wait_ge(shT[0].sem, hT_ready[0])
                            sy.wait_ge(shT[1].sem, hT_ready[1])
                        if war is not None:
                            sy.wait_ge(sem_t, war)
                        sy.dma_start(out=hT_tc[gi % 3][:].rearrange("p (fb n) -> p fb n", n=256),
                                     in_=src).then_inc(shtcc.sem, 16)
                    em("sync", f)
                    tc_ld.append((shtcc.sem, shtcc.inc()))
                    if tc == 0 and q + 1 < 4:
                        emit_qpr_load(q + 1)
                    for tb2 in range(2):
                        reg = (tc % 2) * 2048 + tb2 * 512
                        row = tc * 256 + tb2 * 128
                        tl_ = tc_ld[tc]
                        war = ev_all[-4] if len(ev_all) >= 4 else None
                        qld = qpr_ld[q]
                        def f(t, tc=tc, tb2=tb2, reg=reg, tl_=tl_, war=war, q=q, qld=qld):
                            t.wait_ge(*tl_)
                            t.wait_ge(*qld)
                            if war is not None:
                                t.wait_ge(sem_v, war)
                            for fb in range(NFB):
                                lhs = hT_tc[(q * NCH + tc) % 3][:, fb*256 + tb2*128: fb*256 + (tb2+1)*128]
                                ins = t.matmul(psum[:, reg:reg+512], lhs,
                                               qpr[q % 2][:, fb*512:(fb+1)*512],
                                               start=(fb == 0), stop=(fb == NFB-1))
                            ins.then_inc(sem_t, 1)
                        em("tensor", f)
                        mm_all.append(st.inc())
                        mmv = mm_all[-1]
                        war = ev_all[-2] if len(ev_all) >= 2 else None
                        sxec = sxe[len(ev_all) % 2]
                        srsc = srst[len(ev_all) % 2]
                        def f(sy, row=row, q=q, war=war, idx=len(ev_all), sxec=sxec):
                            if war is not None:
                                sy.wait_ge(sem_v, war)
                            sy.dma_start(out=x_e[idx % 2][:],
                                         in_=d["xp"][row:row+128, q*512:(q+1)*512]).then_inc(sxec.sem, 16)
                        em("sync", f)
                        xl = sxec.inc()
                        war2 = res_stores[-2] if len(res_stores) >= 2 else None
                        def f(v, reg=reg, xl=xl, mmv=mmv, war2=war2, idx=len(ev_all), sxec=sxec, srsc=srsc):
                            v.wait_ge(sem_t, mmv)
                            v.wait_ge(sxec.sem, xl)
                            if war2 is not None:
                                v.wait_ge(srsc.sem, war2)
                            v.scalar_tensor_tensor(res_e[idx % 2][:], psum[:, reg:reg+512],
                                                   tcols[:, 5:6], x_e[idx % 2][:],
                                                   ALU.mult, ALU.add).then_inc(sem_v, 1)
                        em("vector", f)
                        ev_all.append(sv.inc())
                        evv = ev_all[-1]
                        def f(g, row=row, q=q, evv=evv, idx=len(ev_all)-1, srsc=srsc):
                            g.wait_ge(sem_v, evv)
                            g.dma_start(out=d["res_scr"][row:row+128, q*512:(q+1)*512],
                                        in_=res_e[idx % 2][:]).then_inc(srsc.sem, 16)
                        em("gpsimd", f)
                        res_stores.append(srsc.inc())

            emit_qpr_load(0)
            for q in range(4):
                emit_quarter(q)
            res_ready = (srst[0].n, srst[1].n)
        barrier()
        with ExitStack() as S5:
            E5 = S5.enter_context
            wo_h23 = E5(nc.sbuf_tensor("wo_h23", [128, 8 * H], BF16))
            oT_t = [E5(nc.sbuf_tensor(f"oT_t{i}", [128, NCB * 256], BF16)) for i in range(2)]
            res_f = [E5(nc.sbuf_tensor(f"res_f{i}", [128, H], F32)) for i in range(2)]
            out_f = [E5(nc.sbuf_tensor(f"out_f{i}", [128, H], F32)) for i in range(2)]

            def f(g):
                g.dma_start(out=wo_h23[:, 0:4*H].rearrange(
                    "p (ob n) -> p ob n", n=H), in_=wo_sl[2]).then_inc(swo[2].sem, 16)
                g.dma_start(out=wo_h23[:, 4*H:8*H].rearrange(
                    "p (ob n) -> p ob n", n=H), in_=wo_sl[3]).then_inc(swo[3].sem, 16)
            em("gpsimd", f)
            swo[2].inc(); swo[3].inc()

            f_mm, f_ev, out_stores = [], [], []
            for tc in range(NCH):
                src = d["o_scr"][:, tc*256:(tc+1)*256].rearrange("(ob p) n -> p ob n", p=128)
                war = f_mm[2*(tc-2)+1] if tc >= 2 else None
                sotc = sot[tc % 2]
                def f(sy, src=src, tc=tc, war=war, sotc=sotc):
                    if tc < 2:
                        sy.wait_ge(sost.sem, o_ready)
                    if war is not None:
                        sy.wait_ge(sem_t, war)
                    sy.dma_start(out=oT_t[tc % 2][:].rearrange("p (ob n) -> p ob n", n=256),
                                 in_=src).then_inc(sotc.sem, 16)
                em("sync", f)
                ol = (sotc.sem, sotc.inc())
                for tb2 in range(2):
                    reg = tb2 * 2048
                    row = tc * 256 + tb2 * 128
                    idx = 2*tc + tb2
                    war = f_ev[idx-2] if idx >= 2 else None
                    def f(t, tc=tc, tb2=tb2, reg=reg, ol=ol, war=war):
                        t.wait_ge(*ol)
                        if war is not None:
                            t.wait_ge(sem_v, war)
                        for ob in range(NCB):
                            if ob % 4 == 0:
                                t.wait_ge(swo[ob // 4].sem, 16)
                            if ob == NCB-1:
                                t.wait_ge(sbo.sem, 16)
                            lhs = oT_t[tc % 2][:, ob*256 + tb2*128: ob*256 + (tb2+1)*128]
                            wo_buf = wo_h01 if ob < 8 else wo_h23
                            for cc in range(4):
                                t.matmul(psum[:, reg + cc*512: reg + (cc+1)*512], lhs,
                                         wo_buf[:, (ob % 8)*H + cc*512: (ob % 8)*H + (cc+1)*512],
                                         start=(ob == 0), stop=False)
                        for cc in range(4):
                            ins = t.matmul(psum[:, reg + cc*512: reg + (cc+1)*512],
                                           ones_rb[0:1, :], bob[0:1, cc*512:(cc+1)*512],
                                           start=False, stop=True)
                        ins.then_inc(sem_t, 1)
                    em("tensor", f)
                    f_mm.append(st.inc())
                    war = f_ev[idx-2] if idx >= 2 else None
                    srfc = srf[idx % 2]
                    def f(sy, row=row, war=war, srfc=srfc, idx=idx):
                        if idx < 2:
                            sy.wait_ge(srst[0].sem, res_ready[0])
                            sy.wait_ge(srst[1].sem, res_ready[1])
                        if war is not None:
                            sy.wait_ge(sem_v, war)
                        sy.dma_start(out=res_f[(row//128) % 2][:],
                                     in_=d["res_scr"][row:row+128, :]).then_inc(srfc.sem, 16)
                    em("sync", f)
                    rl = (srfc.sem, srfc.inc())
                    mmv = f_mm[-1]
                    war2 = out_stores[idx-2] if idx >= 2 else None
                    sdoc = sdo[idx % 2]
                    def f(v, reg=reg, rl=rl, mmv=mmv, war2=war2, idx=idx, sdoc=sdoc):
                        v.wait_ge(sem_t, mmv)
                        v.wait_ge(*rl)
                        if war2 is not None:
                            v.wait_ge(sdoc.sem, war2)
                        v.tensor_tensor(out_f[idx % 2][:], psum[:, reg:reg+2048],
                                        res_f[idx % 2][:], ALU.add).then_inc(sem_v, 1)
                    em("vector", f)
                    f_ev.append(sv.inc())
                    evv = f_ev[-1]
                    def f(g, row=row, evv=evv, idx=idx, sdoc=sdoc):
                        g.wait_ge(sem_v, evv)
                        g.dma_start(out=d["out"][row:row+128, :],
                                    in_=out_f[idx % 2][:]).then_inc(sdoc.sem, 16)
                    em("gpsimd", f)
                    out_stores.append(sdoc.inc())

            fin = (sdo[0].n, sdo[1].n)
            def f(g, fin=fin):
                g.wait_ge(sdo[0].sem, fin[0])
                g.wait_ge(sdo[1].sem, fin[1])
            em("gpsimd", f)

        with nc.Block() as blk:
            @blk.sync
            def _(eng):
                for fn in streams["sync"]:
                    fn(eng)
            @blk.gpsimd
            def _(eng):
                for fn in streams["gpsimd"]:
                    fn(eng)
            @blk.vector
            def _(eng):
                for fn in streams["vector"]:
                    fn(eng)
            @blk.scalar
            def _(eng):
                for fn in streams["scalar"]:
                    fn(eng)
            @blk.tensor
            def _(eng):
                for fn in streams["tensor"]:
                    fn(eng)

    return nc


def kernel(**inputs):
    in_maps, gidx = _host_layout(inputs)
    nc = build_nc()
    res = run_bass_kernel_spmd(nc, in_maps, core_ids=list(range(8)))
    out = np.zeros((B, T, H), np.float32)
    for c in range(8):
        out[c // 2, gidx[c]] = np.asarray(res.results[c]["out"], np.float32)
    return out


# version 29
# speedup vs baseline: 1.2421x; 1.0179x over previous
import math
import numpy as np
import ml_dtypes
from contextlib import ExitStack

import concourse.bass as bass
import concourse.mybir as mybir
from concourse.bass_utils import run_bass_kernel_spmd

F32 = mybir.dt.float32
BF16 = mybir.dt.bfloat16
AF = mybir.ActivationFunctionType
ALU = mybir.AluOpType
AX = mybir.AxisListType

B, T, H = 4, 4096, 2048
NH, HS = 16, 128
OWN = T // 2
NTB = T // 128
NCB = H // 128
FFH = 5120
NFB = FFH // 128
SCALE = 1.0 / math.sqrt(HS)
W = 256
NCH = OWN // W


def _bf(x):
    return np.ascontiguousarray(np.asarray(x, np.float32).astype(ml_dtypes.bfloat16))


def _host_layout(inputs):
    x = np.ascontiguousarray(np.asarray(inputs["L0_attn_x"], np.float32))
    Wq = np.asarray(inputs["L0_attn_Wq"], np.float32)
    bq = np.asarray(inputs["L0_attn_bq"], np.float32)
    Wk = np.asarray(inputs["L0_attn_Wk"], np.float32)
    bk = np.asarray(inputs["L0_attn_bk"], np.float32)
    Wv = np.ascontiguousarray(np.asarray(inputs["L0_attn_Wv"], np.float32))
    bv = np.asarray(inputs["L0_attn_bv"], np.float32)
    Wo = np.ascontiguousarray(np.asarray(inputs["L0_attn_Wo"], np.float32))
    bo = np.asarray(inputs["L0_attn_bo"], np.float32)
    wfc = np.asarray(inputs["L0_ffn_w_fc"], np.float32)
    wpr = np.asarray(inputs["L0_ffn_w_proj"], np.float32)

    perm = np.concatenate([np.concatenate([np.arange(h*HS, (h+1)*HS, 2),
                                           np.arange(h*HS+1, (h+1)*HS, 2)])
                           for h in range(NH)])
    Wq_p = np.ascontiguousarray(Wq[:, perm])
    Wk_p = np.ascontiguousarray(Wk[:, perm])
    bqc = np.ascontiguousarray(bq[perm].reshape(NH, 128).T)
    bkc = np.ascontiguousarray(bk[perm].reshape(NH, 128).T)
    bvr = np.ascontiguousarray(bv.reshape(1, H))
    bor = np.ascontiguousarray(bo.reshape(1, H))
    wfcT = np.ascontiguousarray(wfc.T)
    wprT = np.ascontiguousarray(wpr.T)
    wvr = np.ascontiguousarray(Wv.reshape(16, 128, 8, 256).transpose(2, 1, 0, 3)
                               .reshape(8, 128, 4096))
    wfc3 = np.ascontiguousarray(wfc.reshape(40, 128, 16, 128).transpose(0, 3, 2, 1)
                                .reshape(40, 128, 2048))

    t = np.arange(T, dtype=np.float32)
    inv = 1.0 / 10000.0 ** (np.arange(0, HS, 2, dtype=np.float32) / HS)
    ang = np.outer(inv, t)
    cosg, sing = np.cos(ang), np.sin(ang)

    kl = np.arange(128)[:, None]
    ql = np.arange(128)[None, :]
    mA = (kl <= ql).astype(np.float32)
    mB_even = (kl < ql).astype(np.float32)

    in_maps, gidx = [], []
    for c in range(8):
        b, p = c // 2, c % 2
        own = 2 * np.arange(OWN) + p
        other = 2 * np.arange(OWN) + (1 - p)
        g = np.concatenate([own, other])
        gidx.append(own)
        cosp, sinp = cosg[:, g], sing[:, g]
        in_maps.append({
            "xp": np.ascontiguousarray(x[b][g]),
            "wq": Wq_p, "wk": Wk_p, "wvr": wvr, "wo": Wo,
            "bqc": bqc, "bkc": bkc, "bvr": bvr, "bor": bor,
            "cos2": _bf(np.concatenate([cosp, cosp], 0)),
            "sin2s": _bf(np.concatenate([-sinp, sinp], 0)),
            "mka": _bf(mA),
            "mkb": _bf(mB_even if p == 0 else mA),
            "wfcT": wfcT, "wprT": wprT, "wfc3": wfc3,
        })
    return in_maps, gidx


def build_nc():
    nc = bass.Bass("TRN2", target_bir_lowering=False)
    d = {}
    for nm, shp, dt in [("xp", [T, H], F32), ("wq", [H, H], F32), ("wk", [H, H], F32),
                        ("wvr", [8, 128, 4096], F32), ("wo", [H, H], F32),
                        ("bqc", [128, NH], F32), ("bkc", [128, NH], F32),
                        ("bvr", [1, H], F32), ("bor", [1, H], F32),
                        ("cos2", [128, T], BF16), ("sin2s", [128, T], BF16),
                        ("mka", [128, 128], BF16), ("mkb", [128, 128], BF16),
                        ("wfcT", [H, FFH], F32), ("wprT", [FFH, H], F32),
                        ("wfc3", [NFB, 128, NCB * 128], F32)]:
        d[nm] = nc.dram_tensor(nm, shp, dt, kind="ExternalInput").ap()
    d["out"] = nc.dram_tensor("out", [OWN, H], F32, kind="ExternalOutput").ap()
    d["xn_scr"] = nc.dram_tensor("xn_scr", [T, H], BF16, kind="Internal").ap()
    d["v_scr"] = nc.dram_tensor("v_scr", [T, H], BF16, kind="Internal").ap()
    d["o_scr"] = nc.dram_tensor("o_scr", [H, OWN], BF16, kind="Internal").ap()
    d["hT_scr"] = nc.dram_tensor("hT_scr", [NCH, 128, NFB, 256], BF16, kind="Internal").ap()
    d["res_scr"] = nc.dram_tensor("res_scr", [OWN, H], F32, kind="Internal").ap()
    d["qpr_scr"] = nc.dram_tensor("qpr_scr", [NFB, 4, 128, 512], BF16, kind="Internal").ap()

    streams = {"sync": [], "gpsimd": [], "vector": [], "scalar": [], "tensor": []}

    def em(name, fn):
        streams[name].append(fn)

    class Ctr:
        def __init__(self, sem, step=1):
            self.sem, self.step, self.n = sem, step, 0
        def inc(self):
            self.n += self.step
            return self.n

    with ExitStack() as S0:
        E0 = S0.enter_context
        psum = E0(nc.psum_tensor("psum", [128, 4096], F32))
        ones_cb = E0(nc.sbuf_tensor("ones_cb", [128, 128], BF16))
        ones_cf = E0(nc.sbuf_tensor("ones_cf", [128, 1], F32))
        ones_rf = E0(nc.sbuf_tensor("ones_rf", [1, 128], F32))
        ones_rb = E0(nc.sbuf_tensor("ones_rb", [1, 128], BF16))
        bqc_s = E0(nc.sbuf_tensor("bqc_s", [128, NH], F32))
        bkc_s = E0(nc.sbuf_tensor("bkc_s", [128, NH], F32))
        tcols = E0(nc.sbuf_tensor("tcols", [128, 6], F32))
        xnT_stack = ExitStack()
        xnT = xnT_stack.enter_context(nc.sbuf_tensor("xnT", [128, NCB * T], BF16, side="right", is_transpose_dest=True))
        mka_s = E0(nc.sbuf_tensor("mka_s", [128, 128], BF16))
        mkb_s = E0(nc.sbuf_tensor("mkb_s", [128, 128], BF16))
        am11 = E0(nc.sbuf_tensor("am11", [1, 8], F32))
        wqk_stack = ExitStack()
        wqk_q = wqk_stack.enter_context(nc.sbuf_tensor("wqk_q", [128, H], BF16, side="right"))
        wqk_k = wqk_stack.enter_context(nc.sbuf_tensor("wqk_k", [128, H], BF16, side="right"))
        eps_c = E0(nc.sbuf_tensor("eps_c", [128, 1], F32))
        eps2_c = E0(nc.sbuf_tensor("eps2_c", [128, 1], F32))
        lnh_c = E0(nc.sbuf_tensor("lnh_c", [128, 1], F32))

        sem_v = E0(nc.semaphore("sv"))
        sem_s = E0(nc.semaphore("ss"))
        sem_t = E0(nc.semaphore("st"))
        sv, ss, st = Ctr(sem_v), Ctr(sem_s), Ctr(sem_t)

        ALL_DSEMS = []

        def dsem(name):
            c = Ctr(E0(nc.semaphore(name)), 16)
            ALL_DSEMS.append(c)
            return c

        sx = [dsem(f"sx{i}") for i in range(3)]
        sxn = [dsem("sxn0"), dsem("sxn1")]
        scst = dsem("scst")
        sbv = dsem("sbv")
        sxnt = [dsem(f"sxnt{i}") for i in range(4)]
        swv = [dsem(f"swv{i}") for i in range(3)]
        svst = [dsem(f"svst{i}") for i in range(4)]
        stab = dsem("stab")
        swq = dsem("swq")
        swk = dsem("swk")
        sstat = [dsem("sstat0"), dsem("sstat1"), dsem("sstat2"), dsem("sstat3")]
        svh = dsem("svh")
        ssw = dsem("ssw")
        sost = dsem("sost")
        swfc = [dsem("swfc0"), dsem("swfc1")]
        shT = [dsem("shT0"), dsem("shT1")]
        swpr = [dsem("swpr0"), dsem("swpr1")]
        shtc = [dsem(f"shtc{i}") for i in range(3)]
        sxe = [dsem("sxe0"), dsem("sxe1")]
        srst = [dsem("srst0"), dsem("srst1")]
        swo = [dsem(f"swo{i}") for i in range(4)]
        sbo = dsem("sbo")
        sot = [dsem("sot0"), dsem("sot1")]
        srf = [dsem("srf0"), dsem("srf1")]
        sdo = [dsem("sdo0"), dsem("sdo1")]

        def barrier():
            vals = tuple((c.sem, c.n) for c in (ALL_DSEMS + [sv, ss, st]) if c.n)
            for e in ("sync", "gpsimd", "vector", "scalar", "tensor"):
                def f(eng, vals=vals):
                    for s, v in vals:
                        eng.wait_ge(s, v)
                em(e, f)

        def f(v):
            v.memset(ones_cb[:], 1.0)
            v.memset(ones_cf[:], 1.0)
            v.memset(ones_rf[:], 1.0)
            v.memset(ones_rb[:], 1.0)
            v.memset(eps_c[:], 1e-5)
            v.memset(eps2_c[:], 41.94304)
            v.memset(lnh_c[:], 7.6246189861593985).then_inc(sem_v, 1)
        em("vector", f)
        const_v = sv.inc()
        def f(sy):
            sy.dma_start(out=bqc_s[:], in_=d["bqc"]).then_inc(scst.sem, 16)
            sy.dma_start(out=bkc_s[:], in_=d["bkc"]).then_inc(scst.sem, 16)
            sy.dma_start(out=mka_s[:], in_=d["mka"]).then_inc(scst.sem, 16)
            sy.dma_start(out=mkb_s[:], in_=d["mkb"]).then_inc(scst.sem, 16)
        em("sync", f)
        for _ in range(4):
            scst.inc()

        with ExitStack() as S1:
            E1 = S1.enter_context
            x_t = [E1(nc.sbuf_tensor(f"x_t{i}", [128, H], F32)) for i in range(3)]
            xn_t = [E1(nc.sbuf_tensor(f"xn_t{i}", [128, H], BF16)) for i in range(2)]
            sums = E1(nc.sbuf_tensor("sums", [128, NTB], F32))
            sumsq = E1(nc.sbuf_tensor("sumsq", [128, NTB], F32))
            lnv = E1(nc.sbuf_tensor("lnv", [128, NTB], F32))
            rstd = E1(nc.sbuf_tensor("rstd", [128, NTB], F32))
            negmr = E1(nc.sbuf_tensor("negmr", [128, NTB], F32))
            bvb = E1(nc.sbuf_tensor("bvb", [128, H], BF16))
            bvr_s = E1(nc.sbuf_tensor("bvr_s", [1, H], BF16))
            wv_sl = [E1(nc.sbuf_tensor(f"wv_sl{i}", [128, NCB * 256], BF16)) for i in range(3)]
            v_ev = [E1(nc.sbuf_tensor(f"v_ev{i}", [128, 256], BF16)) for i in range(4)]

            def f(g):
                g.dma_start(out=bvr_s[:], in_=d["bvr"]).then_inc(sbv.sem, 16)
            em("gpsimd", f)
            bvl = sbv.inc()
            def f(t, bvl=bvl, const_v=const_v):
                t.wait_ge(sbv.sem, bvl)
                t.wait_ge(sem_v, const_v)
                for j in range(4):
                    ins = t.matmul(psum[:, 2048+j*512:2048+(j+1)*512], ones_rb[0:1, :],
                                   bvr_s[0:1, j*512:(j+1)*512], start=True, stop=True)
                ins.then_inc(sem_t, 1)
            em("tensor", f); bvt = st.inc()
            def f(v, bvt=bvt):
                v.wait_ge(sem_t, bvt)
                v.tensor_copy(bvb[:], psum[:, 2048:4096]).then_inc(sem_v, 1)
            em("vector", f); bvb_ready = sv.inc()

            wv_load, wv_last_mm, vproj_mm, vproj_ev, v_store = [], [], [], [], []
            def plan_wv_load(li):
                dch = li % 8
                buf = wv_sl[li % 3]
                war = wv_last_mm[li - 3] if li >= 3 else None
                swvc = swv[li % 3]
                def f(g, dch=dch, buf=buf, war=war, swvc=swvc):
                    if war is not None:
                        g.wait_ge(sem_t, war)
                    g.dma_start(out=buf[:], in_=d["wvr"][dch]).then_inc(swvc.sem, 16)
                em("gpsimd", f)
                wv_load.append((swvc.sem, swvc.inc()))
            plan_wv_load(0)
            plan_wv_load(1)
            plan_wv_load(2)

            def plan_v_mm(idx):
                half, rem = idx // 128, idx % 128
                dch, tb = rem // 16, 16 * (idx // 128) + rem % 16
                li = idx // 16
                bank = (idx % 4) * 512
                lw = wv_load[li]
                qrd = xnq_ready[tb // 8]
                war = vproj_ev[idx - 4] if idx >= 4 else None
                def f(t, li=li, tb=tb, bank=bank, lw=lw, qrd=qrd, war=war):
                    t.wait_ge(*lw)
                    t.wait_ge(*qrd)
                    if war is not None:
                        t.wait_ge(sem_v, war)
                    for cb in range(NCB):
                        ins = t.matmul(psum[:, bank:bank+256],
                                       xnT[:, cb*T + tb*128: cb*T + (tb+1)*128],
                                       wv_sl[li % 3][:, cb*256:(cb+1)*256],
                                       start=(cb == 0), stop=(cb == NCB-1))
                    ins.then_inc(sem_t, 1)
                em("tensor", f)
                vproj_mm.append(st.inc())
                if idx % 16 == 15:
                    wv_last_mm.append(vproj_mm[-1])

            def plan_v_ev(idx):
                dch = (idx % 128) // 16
                bank = (idx % 4) * 512
                mmw = vproj_mm[idx]
                war = v_store[idx - 4] if idx >= 4 else None
                svc = svst[idx % 4]
                def f(v, dch=dch, bank=bank, idx=idx, mmw=mmw, war=war, svc=svc):
                    v.wait_ge(sem_t, mmw)
                    v.wait_ge(sem_v, bvb_ready)
                    if war is not None:
                        v.wait_ge(svc.sem, war)
                    v.tensor_tensor(v_ev[idx % 4][:], psum[:, bank:bank+256],
                                    bvb[:, dch*256:(dch+1)*256], ALU.add).then_inc(sem_v, 1)
                em("vector", f)
                vproj_ev.append(sv.inc())

            def plan_v_st(idx):
                dch = (idx % 128) // 16
                tb = 16 * (idx // 128) + idx % 16
                evw = vproj_ev[idx]
                svc = svst[idx % 4]
                def f(g, dch=dch, tb=tb, idx=idx, evw=evw, svc=svc):
                    g.wait_ge(sem_v, evw)
                    g.dma_start(out=d["v_scr"][tb*128:(tb+1)*128, dch*256:(dch+1)*256],
                                in_=v_ev[idx % 4][:]).then_inc(svc.sem, 16)
                em("gpsimd", f)
                v_store.append(svc.inc())

            vp = {"ld": 3, "mm": 0, "ev": 0, "st": 0}
            NV = 256
            def plan_v(tb):
                qdone = min((tb + 1) // 8, 4)
                tmm = max(0, int((tb - 9) * 6.3))
                tev = max(0, int((tb - 10) * 6.3))
                tld = min(16, 3 + max(0, int((tb - 8) * 0.6)))
                while True:
                    did = False
                    if vp["ld"] < tld and len(wv_last_mm) >= vp["ld"] - 2:
                        plan_wv_load(vp["ld"]); vp["ld"] += 1; did = True
                    if vp["mm"] < tmm:
                        idx = vp["mm"]
                        if (idx // 16 < len(wv_load)
                                and (16 * (idx // 128) + idx % 16) // 8 < len(xnq_ready)
                                and (idx < 4 or idx - 4 < len(vproj_ev))):
                            plan_v_mm(idx); vp["mm"] += 1; did = True
                    if (vp["ev"] < min(tev, len(vproj_mm))
                            and (vp["ev"] < 4 or vp["ev"] - 4 < len(v_store))):
                        plan_v_ev(vp["ev"]); vp["ev"] += 1; did = True
                    if vp["st"] < len(vproj_ev):
                        plan_v_st(vp["st"]); vp["st"] += 1; did = True
                    if not did:
                        break

            sl = d["wq"][:, 0:128].rearrange("(cb p) n -> p cb n", p=128)
            def f(g, sl=sl):
                g.dma_start(out=wqk_q[:].rearrange("p (cb n) -> p cb n", n=128),
                            in_=sl).then_inc(swq.sem, 16)
            em("gpsimd", f)
            wq0_ld = swq.inc()
            sl = d["wk"][:, 0:128].rearrange("(cb p) n -> p cb n", p=128)
            def f(g, sl=sl):
                g.dma_start(out=wqk_k[:].rearrange("p (cb n) -> p cb n", n=128),
                            in_=sl).then_inc(swk.sem, 16)
            em("gpsimd", f)
            wk0_ld = swk.inc()

            LNH = float(H)
            xn_act, xn_store, xnq_ready = [], [], []
            for tb in range(NTB):
                buf = x_t[tb % 3]
                war = xn_act[tb-3] if tb >= 3 else None
                sxc = sx[tb % 3]
                def f(sy, tb=tb, buf=buf, war=war, sxc=sxc):
                    if war is not None:
                        sy.wait_ge(sem_s, war)
                    sy.dma_start(out=buf[:], in_=d["xp"][tb*128:(tb+1)*128, :]).then_inc(sxc.sem, 16)
                em("sync", f)
                ld = sxc.inc()
                def f(v, buf=buf, ld=ld, tb=tb, sxc=sxc):
                    v.wait_ge(sxc.sem, ld)
                    v.reduce_sum(sums[:, tb:tb+1], buf[:], axis=AX.X).then_inc(sem_v, 1)
                em("vector", f)
                vred = sv.inc()
                def f(s, buf=buf, ld=ld, tb=tb, sxc=sxc):
                    s.wait_ge(sxc.sem, ld)
                    if tb == 0:
                        s.wait_ge(sem_v, bvb_ready)
                    s.activation(psum[:, 2048:4096], buf[:], AF.Square,
                                 accum_out=sumsq[:, tb:tb+1]).then_inc(sem_s, 1)
                em("scalar", f)
                sred = ss.inc()
                def f(v, tb=tb, vred=vred):
                    v.wait_ge(sem_v, vred)
                    v.tensor_tensor(lnv[:, tb:tb+1], sums[:, tb:tb+1], sums[:, tb:tb+1],
                                    ALU.mult).then_inc(sem_v, 1)
                em("vector", f)
                vt1 = sv.inc()
                def f(v, tb=tb, vt1=vt1, sred=sred):
                    v.wait_ge(sem_v, vt1)
                    v.wait_ge(sem_s, sred)
                    v.scalar_tensor_tensor(sumsq[:, tb:tb+1], sumsq[:, tb:tb+1], LNH,
                                           lnv[:, tb:tb+1], ALU.mult,
                                           ALU.subtract).then_inc(sem_v, 1)
                em("vector", f)
                vX = sv.inc()
                def f(s, tb=tb, vX=vX):
                    s.wait_ge(sem_v, vX)
                    s.activation(lnv[:, tb:tb+1], sumsq[:, tb:tb+1], AF.Ln,
                                 bias=eps2_c[:, 0:1]).then_inc(sem_s, 1)
                em("scalar", f)
                sln = ss.inc()
                def f(s, tb=tb, sln=sln):
                    s.wait_ge(sem_s, sln)
                    s.activation(rstd[:, tb:tb+1], lnv[:, tb:tb+1], AF.Exp, scale=-0.5,
                                 bias=lnh_c[:, 0:1]).then_inc(sem_s, 1)
                em("scalar", f)
                srs = ss.inc()
                def f(v, tb=tb, srs=srs):
                    v.wait_ge(sem_s, srs)
                    v.scalar_tensor_tensor(negmr[:, tb:tb+1], sums[:, tb:tb+1], -1.0/LNH,
                                           rstd[:, tb:tb+1], ALU.mult,
                                           ALU.mult).then_inc(sem_v, 1)
                em("vector", f)
                vng = sv.inc()
                sxnc = sxn[tb % 2]
                war2 = xn_store[tb-2] if tb >= 2 else None
                def f(s, tb=tb, buf=buf, vng=vng, war2=war2, sxnc=sxnc):
                    s.wait_ge(sem_v, vng)
                    if war2 is not None:
                        s.wait_ge(sxnc.sem, war2)
                    s.activation(xn_t[tb % 2][:], buf[:], AF.Identity,
                                 bias=negmr[:, tb:tb+1], scale=rstd[:, tb:tb+1]).then_inc(sem_s, 1)
                em("scalar", f)
                xn_act.append(ss.inc())
                act = xn_act[tb]
                def f(g, tb=tb, act=act, sxnc=sxnc):
                    g.wait_ge(sem_s, act)
                    g.dma_start(out=d["xn_scr"][tb*128:(tb+1)*128, :],
                                in_=xn_t[tb % 2][:]).then_inc(sxnc.sem, 16)
                em("gpsimd", f)
                xn_store.append(sxnc.inc())
                if (tb + 1) % 8 == 0:
                    qt = tb // 8
                    snap = (sxn[0].n, sxn[1].n)
                    sxq = sxnt[qt]
                    def f(sy, qt=qt, snap=snap, sxq=sxq):
                        sy.wait_ge(sxn[0].sem, snap[0])
                        sy.wait_ge(sxn[1].sem, snap[1])
                        for cb in range(NCB):
                            sy.dma_start(out=xnT[:, cb*T + qt*1024: cb*T + (qt+1)*1024],
                                         in_=d["xn_scr"][qt*1024:(qt+1)*1024,
                                                         cb*128:(cb+1)*128],
                                         transpose=True).then_inc(sxq.sem, 16)
                    em("sync", f)
                    for _ in range(NCB):
                        sxq.inc()
                    xnq_ready.append((sxq.sem, sxq.n))
                plan_v(tb)

            xnT_ready = list(xnq_ready)

            while vp["mm"] < NV or vp["ev"] < NV or vp["st"] < NV or vp["ld"] < 16:
                plan_v(999)
            v_ready = tuple((c.sem, c.n) for c in svst)


        barrier()
        with ExitStack() as S2:
            E2 = S2.enter_context
            expT = E2(nc.sbuf_tensor("expT", [128, 32 * W], BF16))
            cos_s = E2(nc.sbuf_tensor("cos_s", [128, T], BF16))
            sin_s = E2(nc.sbuf_tensor("sin_s", [128, T], BF16))
            qrotT = E2(nc.sbuf_tensor("qrotT", [128, OWN], BF16))
            krotT = E2(nc.sbuf_tensor("krotT", [128, T], BF16))
            v_head = E2(nc.sbuf_tensor("v_head", [128, 32 * 128], BF16))

            def f(sy):
                sy.dma_start(out=cos_s[:], in_=d["cos2"]).then_inc(stab.sem, 16)
                sy.dma_start(out=sin_s[:], in_=d["sin2s"]).then_inc(stab.sem, 16)
            em("sync", f)
            stab.inc()
            tables_ready = stab.inc()
            stb = [E2(nc.sbuf_tensor(f"stb{i}", [128, 512], F32)) for i in range(3)]
            fcr = E2(nc.sbuf_tensor("fcr", [128, 160], F32))
            prr = E2(nc.sbuf_tensor("prr", [128, 160], F32))
            fsum2 = E2(nc.sbuf_tensor("fsum2", [128, 2], F32))
            dmp2 = E2(nc.sbuf_tensor("dmp2", [128, 512], BF16))
            qe_b = E2(nc.sbuf_tensor("qe_b", [128, 1024], BF16))
            qsw_b = E2(nc.sbuf_tensor("qsw_b", [128, 1024], BF16))
            rt1 = E2(nc.sbuf_tensor("rt1", [128, 1024], BF16))
            rt2 = E2(nc.sbuf_tensor("rt2", [128, 1024], BF16))
            rb_sb = E2(nc.sbuf_tensor("rb_sb", [128, W], F32))
            o_sb = [E2(nc.sbuf_tensor("o_sb0", [128, W], BF16))] * 2

            REG = [0, 1024, 3072]
            DF0, RB0, OT0 = 2048, 2304, 2560
            QKB = 3072
            GROUPS = [list(range(4*g, 4*g+4)) for g in range(8)]

            wfc_flat = d["wfcT"].rearrange("a b -> (a b)")
            wpr_flat = d["wprT"].rearrange("a b -> (a b)")
            stat_load_done = [None] * 320
            stat_red_done = [None] * 320

            def emit_stat_load(i):
                buf = stb[i % 3]
                src = wfc_flat if i < 160 else wpr_flat
                j = i if i < 160 else i - 160
                sl = src[j * 65536:(j + 1) * 65536].rearrange("(p m) -> p m", p=128)
                sstc = sstat[i % 3]
                def f(sy, sl=sl, buf=buf, i=i, sstc=sstc):
                    if i >= 3:
                        war = stat_red_done[i - 3]
                        sy.wait_ge(war[0], war[1])
                    sy.dma_start(out=buf[:], in_=sl).then_inc(sstc.sem, 16)
                em("sync", f)
                stat_load_done[i] = (sstc.sem, sstc.inc())

            def emit_stat_red(i):
                buf = stb[i % 3]
                ld = stat_load_done[i]
                if i < 160:
                    def f(v, buf=buf, ld=ld, i=i):
                        v.wait_ge(*ld)
                        v.reduce_sum(fcr[:, i:i+1], buf[:], axis=AX.X,
                                     apply_absolute_value=True).then_inc(sem_v, 1)
                    em("vector", f)
                    stat_red_done[i] = (sem_v, sv.inc())
                else:
                    pa = stat_red_done[i-1] if (i > 160 and stat_red_done[i-1] is not None
                                                and stat_red_done[i-1][0] is sem_s) else None
                    def f(s, buf=buf, ld=ld, i=i, pa=pa):
                        s.wait_ge(*ld)
                        if pa is not None:
                            s.wait_ge(*pa)
                        s.activation(dmp2[:], buf[:], AF.Abs,
                                     accum_out=prr[:, i-160:i-159]).then_inc(sem_s, 1)
                    em("scalar", f)
                    stat_red_done[i] = (sem_s, ss.inc())

            HEAD_MIL = []
            PENDING_WQK = {}
            P3 = [0]
            LASTREG = [None]
            REGION_LAST_EXP = {}
            last_c_exp = [None]

            def plan_head(h):
                mil = {}
                prev = HEAD_MIL[h-1] if h > 0 else None

                if h == 0:
                    mil["wq_ld"] = wq0_ld
                    mil["wk_ld"] = wk0_ld
                else:
                    mil["wq_ld"], mil["wk_ld"] = PENDING_WQK[h]
                for k in range(20*h, 20*h+20):
                    emit_stat_load(k)
                RQ = list(range(20*h, 20*h+20))
                def reds_until(done_groups, RQ=RQ):
                    tgt = (done_groups * 20) // 64
                    while (20 - len(RQ)) < tgt:
                        emit_stat_red(RQ.pop(0))

                src = d["v_scr"][:, h*128:(h+1)*128].rearrange("(kb p) n -> p kb n", p=128)
                def f(g, src=src, prev=prev):
                    if prev is not None:
                        g.wait_ge(sem_t, prev["t_last"])
                    else:
                        for _s, _v in v_ready:
                            g.wait_ge(_s, _v)
                    g.dma_start(out=v_head[:].rearrange("p (kb n) -> p kb n", n=128),
                                in_=src).then_inc(svh.sem, 16)
                em("gpsimd", f)
                mil["v_ld"] = svh.inc()

                qkb_war = last_c_exp[0]
                war3 = prev["t_last"] if prev else None
                qev = []
                rope_piece = []
                for j in range(4):
                    bank = QKB + (j % 2) * 512
                    wq_ld = mil["wq_ld"]
                    war = qev[j-2] if j >= 2 else (prev["kev_last"] if prev else None)
                    def f(t, j=j, bank=bank, wq_ld=wq_ld, war=war, h=h, qkb_war=qkb_war):
                        t.wait_ge(swq.sem, wq_ld)
                        if h == 0 and j == 0:
                            for _sw, _sv in xnT_ready:
                                t.wait_ge(_sw, _sv)
                        if j == 0 and qkb_war is not None:
                            t.wait_ge(sem_s, qkb_war)
                        if war is not None:
                            t.wait_ge(sem_s, war)
                        for cb in range(NCB):
                            ins = t.matmul(psum[:, bank:bank+512],
                                           wqk_q[:, cb*128:(cb+1)*128],
                                           xnT[:, cb*T + j*512: cb*T + (j+1)*512],
                                           start=(cb == 0), stop=(cb == NCB-1))
                            if cb == NCB-1:
                                ins.then_inc(sem_t, 1)
                    em("tensor", f)
                    mm = st.inc()
                    war2 = rope_piece[j//2 - 1] if j >= 2 else (prev["ropek1"] if prev else None)
                    def f(s, j=j, bank=bank, mm=mm, h=h, war2=war2):
                        s.wait_ge(sem_t, mm)
                        if j % 2 == 0 and war2 is not None:
                            s.wait_ge(sem_v, war2)
                        s.activation(qe_b[:, (j % 2)*512:(j % 2)*512+512], psum[:, bank:bank+512],
                                     AF.Identity, bias=bqc_s[:, h:h+1]).then_inc(sem_s, 1)
                    em("scalar", f)
                    qev.append(ss.inc())
                    if j % 2 == 1:
                        p = j // 2
                        ev2 = qev[-1]
                        def f(g, ev2=ev2):
                            g.wait_ge(sem_s, ev2)
                            g.dma_start(out=qsw_b[0:64, :], in_=qe_b[64:128, :]).then_inc(ssw.sem, 16)
                            g.dma_start(out=qsw_b[64:128, :], in_=qe_b[0:64, :]).then_inc(ssw.sem, 16)
                        em("gpsimd", f)
                        ssw.inc()
                        swp = ssw.inc()
                        def f(v, swp=swp, p=p, h=h, war3=war3):
                            v.wait_ge(ssw.sem, swp)
                            if p == 0 and h == 0:
                                v.wait_ge(stab.sem, tables_ready)
                            v.tensor_tensor(rt1[:], qe_b[:], cos_s[:, p*1024:(p+1)*1024], ALU.mult)
                            v.tensor_tensor(rt2[:], qsw_b[:], sin_s[:, p*1024:(p+1)*1024],
                                            ALU.mult).then_inc(sem_v, 1)
                        em("vector", f)
                        eq = sv.inc()
                        def f(v, eq=eq, war3=war3, p=p):
                            v.wait_ge(sem_v, eq)
                            if p == 0 and war3 is not None:
                                v.wait_ge(sem_t, war3)
                            v.tensor_tensor(qrotT[:, p*1024:(p+1)*1024], rt1[:], rt2[:],
                                            ALU.add).then_inc(sem_v, 1)
                        em("vector", f)
                        rope_piece.append(sv.inc())
                mil["qproj_mm"] = st.n
                mil["ropeq"] = rope_piece[-1]

                kev_prev = qev
                krope_piece = list(rope_piece)
                for kh in range(2):
                    kev = []
                    for j in range(4):
                        bank = QKB + (j % 2) * 512
                        tcol = kh * 2048 + j * 512
                        wk_ld = mil["wk_ld"]
                        war = kev[j-2] if j >= 2 else kev_prev[2+j]
                        def f(t, bank=bank, tcol=tcol, wk_ld=wk_ld, war=war):
                            t.wait_ge(swk.sem, wk_ld)
                            t.wait_ge(sem_s, war)
                            for cb in range(NCB):
                                ins = t.matmul(psum[:, bank:bank+512],
                                               wqk_k[:, cb*128:(cb+1)*128],
                                               xnT[:, cb*T + tcol: cb*T + tcol + 512],
                                               start=(cb == 0), stop=(cb == NCB-1))
                                if cb == NCB-1:
                                    ins.then_inc(sem_t, 1)
                        em("tensor", f)
                        mm = st.inc()
                        war2 = krope_piece[-1] if j % 2 == 0 else None
                        def f(s, j=j, bank=bank, mm=mm, h=h, war2=war2):
                            s.wait_ge(sem_t, mm)
                            if war2 is not None:
                                s.wait_ge(sem_v, war2)
                            s.activation(qe_b[:, (j % 2)*512:(j % 2)*512+512],
                                         psum[:, bank:bank+512],
                                         AF.Identity, bias=bkc_s[:, h:h+1]).then_inc(sem_s, 1)
                        em("scalar", f)
                        kev.append(ss.inc())
                        if j % 2 == 1:
                            p = j // 2
                            off = kh*2048 + p*1024
                            ev2 = kev[-1]
                            def f(g, ev2=ev2):
                                g.wait_ge(sem_s, ev2)
                                g.dma_start(out=qsw_b[0:64, :], in_=qe_b[64:128, :]).then_inc(ssw.sem, 16)
                                g.dma_start(out=qsw_b[64:128, :], in_=qe_b[0:64, :]).then_inc(ssw.sem, 16)
                            em("gpsimd", f)
                            ssw.inc()
                            swp = ssw.inc()
                            def f(v, swp=swp, off=off):
                                v.wait_ge(ssw.sem, swp)
                                v.tensor_tensor(rt1[:], qe_b[:], cos_s[:, off:off+1024], ALU.mult)
                                v.tensor_tensor(rt2[:], qsw_b[:], sin_s[:, off:off+1024],
                                                ALU.mult).then_inc(sem_v, 1)
                            em("vector", f)
                            ek = sv.inc()
                            war4 = prev["t_last"] if (prev and kh == 0 and p == 0) else None
                            def f(v, ek=ek, war4=war4, off=off):
                                v.wait_ge(sem_v, ek)
                                if war4 is not None:
                                    v.wait_ge(sem_t, war4)
                                v.tensor_tensor(krotT[:, off:off+1024], rt1[:], rt2[:],
                                                ALU.add).then_inc(sem_v, 1)
                            em("vector", f)
                            krope_piece.append(sv.inc())
                    kev_prev = kev
                    mil["ropek%d" % kh] = krope_piece[-1]
                mil["kev_last"] = kev_prev[3]
                mil["kproj_mm"] = st.n

                NG = 8
                exp_done = [[None]*NG for _ in range(NCH)]
                ones_done = [[None, None] for _ in range(NCH)]
                mask_done = [[None, None] for _ in range(NCH)]
                recip_done = [None] * NCH
                av_done = [[None, None] for _ in range(NCH)]
                evac_done = [None] * NCH
                o_stores = [None] * NCH

                def emit_group(c, gi):
                    lg = c * 8 + gi
                    if lg < 58:
                        reg = REG[P3[0] % 3]
                        P3[0] += 1
                        if reg == LASTREG[0]:
                            reg = REG[P3[0] % 3]
                            P3[0] += 1
                    else:
                        reg = REG[0] if LASTREG[0] != REG[0] else REG[1]
                    LASTREG[0] = reg
                    grp = GROUPS[gi]
                    war = REGION_LAST_EXP.get(reg)
                    rk = mil["ropek0"] if gi < 4 else mil["ropek1"]
                    kevw = mil["kev_last"] if reg == QKB else None
                    def f(t, c=c, reg=reg, grp=tuple(grp), war=war, rk=rk, kevw=kevw):
                        if war is not None:
                            t.wait_ge(sem_s, war)
                        if kevw is not None:
                            t.wait_ge(sem_s, kevw)
                        t.wait_ge(sem_v, rk)
                        for kb in grp:
                            pos = kb - grp[0]
                            ins = t.matmul(psum[:, reg+pos*256: reg+pos*256+256],
                                           krotT[:, kb*128:(kb+1)*128],
                                           qrotT[:, c*W:(c+1)*W],
                                           start=(pos % 2 == 0), stop=(pos % 2 == 1))
                            if kb == grp[-1]:
                                ins.then_inc(sem_t, 1)
                    em("tensor", f)
                    gmm = st.inc()
                    g0 = GROUPS[gi][0]
                    half = 0 if gi < NG // 2 else 1
                    if c >= 1:
                        free_war = ("avh", c - 1, half)
                    elif prev is not None:
                        free_war = ("pt", prev["t_last"], None)
                    else:
                        free_war = None
                    def f(s, reg=reg, g0=g0, gmm=gmm, free_war=free_war):
                        s.wait_ge(sem_t, gmm)
                        if free_war is not None:
                            if free_war[0] == "avh":
                                s.wait_ge(sem_t, av_done[free_war[1]][free_war[2]])
                            else:
                                s.wait_ge(sem_t, free_war[1])
                        s.activation(expT[:, g0*W:(g0+4)*W], psum[:, reg:reg+1024],
                                     AF.Exp, scale=SCALE).then_inc(sem_s, 1)
                    em("scalar", f)
                    ev = ss.inc()
                    exp_done[c][gi] = ev
                    REGION_LAST_EXP[reg] = ev
                    if reg == REG[2]:
                        last_c_exp[0] = ev

                def emit_ones(c, half):
                    kbs = range(16) if half == 0 else range(16, 32)
                    war = (recip_done[c-1] if c >= 1 else
                           (prev["recip_last"] if prev else None)) if half == 0 else None
                    lastexp = exp_done[c][3 if half == 0 else 7]
                    def f(t, kbs=tuple(kbs), war=war, half=half, lastexp=lastexp):
                        t.wait_ge(sem_s, lastexp)
                        if war is not None:
                            t.wait_ge(sem_v, war)
                        for kb in kbs:
                            ins = t.matmul(psum[:, DF0:DF0+W], ones_cb[:],
                                           expT[:, kb*W:(kb+1)*W],
                                           start=(kb == 0), stop=(kb == 31))
                            if kb == kbs[-1]:
                                ins.then_inc(sem_t, 1)
                    em("tensor", f)
                    ones_done[c][half] = st.inc()

                def emit_masks(c, half):
                    j0, j1 = 2*c, 2*c+1
                    if half == 0:
                        slices = [(j0*W, j0*W+128, mka_s), (j1*W+128, j1*W+256, mka_s)]
                    else:
                        slices = [((16+j0)*W, (16+j0)*W+128, mkb_s),
                                  ((16+j1)*W+128, (16+j1)*W+256, mkb_s)]
                    def f(v, c=c, half=half, slices=tuple(slices)):
                        v.wait_ge(sem_t, ones_done[c][half])
                        for a, b, mk in slices:
                            ins = v.tensor_tensor(expT[:, a:b], expT[:, a:b], mk[:], ALU.mult)
                        ins.then_inc(sem_v, 1)
                    em("vector", f)
                    mask_done[c][half] = sv.inc()

                def emit_av(c, half):
                    valid = [kb for kb in (range(16) if half == 0 else range(16, 32))
                             if (kb % 16) <= 2*c+1]
                    vld = mil["v_ld"]
                    war = (evac_done[c-1] if c >= 1 else
                           (prev["evac_last"] if prev else None)) if half == 0 else None
                    def f(t, c=c, half=half, valid=tuple(valid), vld=vld, war=war):
                        t.wait_ge(sem_v, mask_done[c][half])
                        t.wait_ge(svh.sem, vld)
                        if war is not None:
                            t.wait_ge(sem_v, war)
                        for i, kb in enumerate(valid):
                            kj = kb % 16
                            cs = 128 if kj == 2*c+1 else 0
                            ins = t.matmul(psum[:, OT0+cs:OT0+256],
                                           v_head[:, kb*128:(kb+1)*128],
                                           expT[:, kb*W+cs:(kb+1)*W],
                                           start=(half == 0 and i == 0),
                                           stop=(half == 1 and i == len(valid)-1))
                            if i == len(valid)-1:
                                ins.then_inc(sem_t, 1)
                    em("tensor", f)
                    av_done[c][half] = st.inc()

                def emit_recip(c):
                    war = evac_done[c-1] if c >= 1 else (prev["evac_last"] if prev else None)
                    def f(v, c=c, war=war):
                        v.wait_ge(sem_t, ones_done[c][1])
                        if war is not None:
                            v.wait_ge(sem_v, war)
                        v.reciprocal(rb_sb[:], psum[:, DF0:DF0+W]).then_inc(sem_v, 1)
                    em("vector", f)
                    recip_done[c] = sv.inc()

                def emit_bcast(c):
                    war = o_stores[c-1] if c >= 1 else (prev["o_last"] if prev else None)
                    def f(v, c=c, war=war):
                        v.wait_ge(sem_v, recip_done[c])
                        v.wait_ge(sem_t, av_done[c][1])
                        if war is not None:
                            v.wait_ge(sost.sem, war)
                        v.tensor_tensor(o_sb[c % 2][:], psum[:, OT0:OT0+W], rb_sb[:],
                                        ALU.mult).then_inc(sem_v, 1)
                    em("vector", f)
                    evac_done[c] = sv.inc()
                    def f(g, c=c, h=h):
                        g.wait_ge(sem_v, evac_done[c])
                        g.dma_start(out=d["o_scr"][h*128:(h+1)*128, c*W:(c+1)*W],
                                    in_=o_sb[c % 2][:]).then_inc(sost.sem, 16)
                    em("gpsimd", f)
                    o_stores[c] = sost.inc()

                def emit_p2(c):
                    emit_ones(c, 0)
                    emit_masks(c, 0)
                    emit_av(c, 0)
                    emit_ones(c, 1)
                    emit_masks(c, 1)
                    emit_recip(c)
                    emit_av(c, 1)
                    emit_bcast(c)

                for c in range(NCH):
                    if c >= 1:
                        emit_ones(c-1, 0)
                        emit_masks(c-1, 0)
                        emit_av(c-1, 0)
                    emit_group(c, 0)
                    emit_group(c, 1)
                    if c >= 1:
                        emit_ones(c-1, 1)
                        emit_masks(c-1, 1)
                        emit_recip(c-1)
                    emit_group(c, 2)
                    reds_until(c*8 + 3)
                    emit_group(c, 3)
                    if c >= 1:
                        emit_av(c-1, 1)
                        emit_bcast(c-1)
                    emit_group(c, 4)
                    reds_until(c*8 + 5)
                    emit_group(c, 5)
                    emit_group(c, 6)
                    reds_until(c*8 + 7)
                    emit_group(c, 7)
                    reds_until(c*8 + 8)
                    if c == 1 and h + 1 < NH:
                        sl = d["wq"][:, (h+1)*128:(h+2)*128].rearrange(
                            "(cb p) n -> p cb n", p=128)
                        qpw = mil["qproj_mm"]
                        def f(g, sl=sl, qpw=qpw):
                            g.wait_ge(sem_t, qpw)
                            g.dma_start(out=wqk_q[:].rearrange("p (cb n) -> p cb n", n=128),
                                        in_=sl).then_inc(swq.sem, 16)
                        em("gpsimd", f)
                        wqv = swq.inc()
                        sl = d["wk"][:, (h+1)*128:(h+2)*128].rearrange(
                            "(cb p) n -> p cb n", p=128)
                        kpw = mil["kproj_mm"]
                        def f(g, sl=sl, kpw=kpw):
                            g.wait_ge(sem_t, kpw)
                            g.dma_start(out=wqk_k[:].rearrange("p (cb n) -> p cb n", n=128),
                                        in_=sl).then_inc(swk.sem, 16)
                        em("gpsimd", f)
                        PENDING_WQK[h+1] = (wqv, swk.inc())
                emit_p2(NCH-1)
                while RQ:
                    emit_stat_red(RQ.pop(0))

                mil["t_last"] = st.n
                mil["exp_last"] = exp_done[NCH-1][NG-1]
                mil["evac_last"] = evac_done[NCH-1]
                mil["recip_last"] = recip_done[NCH-1]
                mil["o_last"] = o_stores[NCH-1]
                return mil

            for h in range(NH):
                HEAD_MIL.append(plan_head(h))
            o_ready = HEAD_MIL[-1]["o_last"]

            vW2, sW2, tW2 = sv.n, ss.n, st.n
            def f(v, vW2=vW2, sW2=sW2):
                v.wait_ge(sem_v, vW2)
                v.wait_ge(sem_s, sW2)
                v.reduce_sum(fsum2[:, 0:1], fcr[:], axis=AX.X)
                v.reduce_sum(fsum2[:, 1:2], prr[:], axis=AX.X).then_inc(sem_v, 1)
            em("vector", f); fin1 = sv.inc()
            def f(t, fin1=fin1, tW2=tW2):
                t.wait_ge(sem_v, fin1)
                t.wait_ge(sem_t, tW2)
                t.matmul(psum[0:1, 0:1], fsum2[:, 0:1], ones_cf[:], start=True, stop=False)
                t.matmul(psum[0:1, 1:2], fsum2[:, 1:2], ones_cf[:], start=False,
                         stop=True).then_inc(sem_t, 1)
            em("tensor", f); fin2 = st.inc()
            NWF = float(H * FFH)
            def f(v, fin2=fin2):
                v.wait_ge(sem_t, fin2)
                v.tensor_scalar(am11[0:1, 0:2], psum[0:1, 0:2], 1.0/NWF, 1e-5,
                                ALU.mult, ALU.max).then_inc(sem_v, 1)
            em("vector", f); fin3 = sv.inc()
            def f(v, fin3=fin3):
                v.wait_ge(sem_v, fin3)
                v.tensor_scalar(am11[0:1, 2:3], am11[0:1, 0:1], 0.7, None, ALU.mult)
                v.tensor_scalar(am11[0:1, 3:4], am11[0:1, 0:1], -0.7, None, ALU.mult)
                v.tensor_scalar(am11[0:1, 4:5], am11[0:1, 0:1], -1.0, None, ALU.mult)
                v.tensor_scalar(am11[0:1, 5:6], am11[0:1, 1:2], 0.7, None, ALU.mult)
                v.tensor_scalar(am11[0:1, 6:7], am11[0:1, 1:2], -0.7, None, ALU.mult)
                v.tensor_scalar(am11[0:1, 7:8], am11[0:1, 1:2], -1.0, None,
                                ALU.mult).then_inc(sem_v, 1)
            em("vector", f); fin4 = sv.inc()
            def f(t, fin4=fin4):
                t.wait_ge(sem_v, fin4)
                for j in range(6):
                    ins = t.matmul(psum[:, 8+j:9+j], ones_rf[0:1, :], am11[0:1, 2+j:3+j],
                                   start=(j == 0), stop=(j == 5))
                ins.then_inc(sem_t, 1)
            em("tensor", f); fin5 = st.inc()
            def f(v, fin5=fin5):
                v.wait_ge(sem_t, fin5)
                v.tensor_copy(tcols[:], psum[:, 8:14]).then_inc(sem_v, 1)
            em("vector", f); tern_ready = sv.inc()

        wqk_stack.close()

        barrier()
        with ExitStack() as S3:
            E3 = S3.enter_context
            wfc_b = [E3(nc.sbuf_tensor(f"wfc_b{i}", [128, H], F32)) for i in range(2)]
            wpr_b = [E3(nc.sbuf_tensor(f"wpr_b{i}", [128, H], F32)) for i in range(2)]
            qprs = [E3(nc.sbuf_tensor("qprs0", [128, H], BF16))] * 2
            qfc = [E3(nc.sbuf_tensor(f"qfc{i}", [128, H], BF16)) for i in range(2)]
            gtb = E3(nc.sbuf_tensor("gtb", [128, H], BF16))
            relu_t = E3(nc.sbuf_tensor("relu_t", [128, 1024], F32))
            hT_t = [E3(nc.sbuf_tensor("hT_t0", [128, OWN], BF16))] * 2

            pr_tern_d, pr_store_d = [], []
            def emit_wpr_strip(fb):
                sl = d["wprT"][fb*128:(fb+1)*128, :]
                war = pr_store_d[fb-2] if fb >= 2 else None
                spc = swpr[fb % 2]
                def f(sy, sl=sl, war=war, fb=fb, spc=spc):
                    if war is not None:
                        sy.wait_ge(*war)
                    sy.dma_start(out=wpr_b[fb % 2][:], in_=sl).then_inc(spc.sem, 16)
                em("sync", f)
                ld = spc.inc()
                gwar = fc_tern[-1] if fc_tern else None
                gwar2 = pr_tern_d[-1] if pr_tern_d else None
                def f(v, fb=fb, ld=ld, spc=spc, gwar=gwar, gwar2=gwar2):
                    v.wait_ge(spc.sem, ld)
                    v.wait_ge(sem_v, tern_ready)
                    if gwar is not None:
                        v.wait_ge(sem_v, gwar)
                    if gwar2 is not None:
                        v.wait_ge(sem_v, gwar2)
                    v.tensor_scalar(gtb[:], wpr_b[fb % 2][:], tcols[:, 3:4], None,
                                    ALU.is_gt).then_inc(sem_v, 1)
                em("vector", f)
                ed = sv.inc()
                swar = pr_store_d[-1] if pr_store_d else None
                def f(v, fb=fb, ed=ed, swar=swar):
                    v.wait_ge(sem_v, ed)
                    if swar is not None:
                        v.wait_ge(*swar)
                    v.scalar_tensor_tensor(qprs[fb % 2][:], wpr_b[fb % 2][:],
                                           tcols[:, 4:5], gtb[:], ALU.is_lt,
                                           ALU.subtract).then_inc(sem_v, 1)
                em("vector", f)
                pr_tern_d.append(sv.inc())
                trn = pr_tern_d[-1]
                srsc = srst[fb % 2]
                def f(g, fb=fb, trn=trn, srsc=srsc):
                    g.wait_ge(sem_v, trn)
                    for q in range(4):
                        g.dma_start(out=d["qpr_scr"][fb][q],
                                    in_=qprs[fb % 2][:, q*512:(q+1)*512]).then_inc(srsc.sem, 16)
                em("gpsimd", f)
                for _ in range(4):
                    srsc.inc()
                pr_store_d.append((srsc.sem, srsc.n))

            fc_ld, fc_tern, fc_mm, fc_relu, fc_sq, fc_st = [], [], [], [], [], []
            for fb in range(NFB):
                sl = d["wfc3"][fb]
                war = fc_tern[fb-2] if fb >= 2 else None
                sfc = swfc[fb % 2]
                def f(sy, sl=sl, fb=fb, war=war, sfc=sfc):
                    if war is not None:
                        sy.wait_ge(sem_v, war)
                    sy.dma_start(out=wfc_b[fb % 2][:], in_=sl).then_inc(sfc.sem, 16)
                em("sync", f)
                fc_ld.append(sfc.inc())
                ld = fc_ld[fb]
                war = fc_mm[fb-2] if fb >= 2 else None
                gwar = pr_tern_d[-1] if pr_tern_d else None
                def f(v, fb=fb, ld=ld, war=war, sfc=sfc, gwar=gwar):
                    v.wait_ge(sfc.sem, ld)
                    v.wait_ge(sem_v, tern_ready)
                    if war is not None:
                        v.wait_ge(sem_t, war)
                    if gwar is not None:
                        v.wait_ge(sem_v, gwar)
                    v.tensor_scalar(gtb[:], wfc_b[fb % 2][:], tcols[:, 0:1], None,
                                    ALU.is_gt).then_inc(sem_v, 1)
                em("vector", f)
                ed = sv.inc()
                def f(v, fb=fb, ed=ed):
                    v.wait_ge(sem_v, ed)
                    v.scalar_tensor_tensor(qfc[fb % 2][:], wfc_b[fb % 2][:], tcols[:, 1:2],
                                           gtb[:], ALU.is_lt, ALU.subtract).then_inc(sem_v, 1)
                em("vector", f)
                fc_tern.append(sv.inc())
                emit_wpr_strip(fb)
                reg = (fb % 2) * 2048
                trn = fc_tern[fb]
                war = fc_relu[4*(fb-2)+3] if fb >= 2 else None
                def f(t, fb=fb, reg=reg, trn=trn, war=war):
                    t.wait_ge(sem_v, trn)
                    if war is not None:
                        t.wait_ge(sem_s, war)
                    for tc in range(4):
                        for cb in range(NCB):
                            ins = t.matmul(psum[:, reg + tc*512: reg + (tc+1)*512],
                                           qfc[fb % 2][:, cb*128:(cb+1)*128],
                                           xnT[:, cb*T + tc*512: cb*T + (tc+1)*512],
                                           start=(cb == 0), stop=(cb == NCB-1))
                        ins.then_inc(sem_t, 1)
                em("tensor", f)
                mmv = [st.inc() for _ in range(4)]
                fc_mm.append(mmv[3])
                for hh in range(4):
                    gi = 4*fb + hh
                    rwar = fc_sq[gi-2] if gi >= 2 else None
                    mmw = mmv[hh]
                    def f(s, reg=reg, hh=hh, rwar=rwar, mmw=mmw):
                        s.wait_ge(sem_t, mmw)
                        if rwar is not None:
                            s.wait_ge(sem_s, rwar)
                        s.activation(relu_t[:, (hh % 2)*512:(hh % 2)*512+512],
                                     psum[:, reg+hh*512:reg+(hh+1)*512],
                                     AF.Relu, scale=tcols[:, 2:3]).then_inc(sem_s, 1)
                    em("scalar", f)
                    fc_relu.append(ss.inc())
                    rv = fc_relu[-1]
                    war2 = (shT[(fb-1) % 2].sem, fc_st[fb-1]) if (hh == 0 and fb >= 1) else None
                    def f(s, fb=fb, hh=hh, rv=rv, war2=war2):
                        s.wait_ge(sem_s, rv)
                        if war2 is not None:
                            s.wait_ge(*war2)
                        s.activation(hT_t[fb % 2][:, hh*512:(hh+1)*512],
                                     relu_t[:, (hh % 2)*512:(hh % 2)*512+512],
                                     AF.Square).then_inc(sem_s, 1)
                    em("scalar", f)
                    fc_sq.append(ss.inc())
                shc = shT[fb % 2]
                def f(g, fb=fb, shc=shc):
                    g.wait_ge(sem_s, fc_sq[4*fb+3])
                    for tcx in range(NCH):
                        g.dma_start(out=d["hT_scr"][tcx][:, fb],
                                    in_=hT_t[fb % 2][:, tcx*256:(tcx+1)*256]).then_inc(shc.sem, 16)
                em("gpsimd", f)
                for _ in range(NCH):
                    shc.inc()
                fc_st.append(shc.n)
            hT_ready = (shT[0].n, shT[1].n)
        xnT_stack.close()

        barrier()
        wo_sl = [d["wo"][grp*512:(grp+1)*512, :].rearrange("(ob p) n -> p ob n", p=128)
                 for grp in range(4)]
        wo_h01 = E0(nc.sbuf_tensor("wo_h01", [128, 8 * H], BF16))
        bob = E0(nc.sbuf_tensor("bob", [1, H], BF16))
        with ExitStack() as S4:
            E4 = S4.enter_context
            qpr = [E4(nc.sbuf_tensor(f"qpr{i}", [128, NFB * 512], BF16)) for i in range(2)]
            hT_tc = [E4(nc.sbuf_tensor(f"hT_tc{i}", [128, NFB * 256], BF16)) for i in range(3)]
            x_e = [E4(nc.sbuf_tensor(f"x_e{i}", [128, 512], F32)) for i in range(2)]
            res_e = [E4(nc.sbuf_tensor(f"res_e{i}", [128, 512], F32)) for i in range(2)]

            res_stores, ev_all, mm_all = [], [], []
            qpr_ready_vals = (srst[0].n, srst[1].n)
            qpr_ld = []
            def emit_qpr_load(q):
                war = mm_all[16*(q-1) - 1] if q >= 2 else None
                spc = swpr[q % 2]
                def f(sy, q=q, war=war, spc=spc):
                    if q < 2:
                        sy.wait_ge(srst[0].sem, qpr_ready_vals[0])
                        sy.wait_ge(srst[1].sem, qpr_ready_vals[1])
                    if war is not None:
                        sy.wait_ge(sem_t, war)
                    for fb in range(NFB):
                        sy.dma_start(out=qpr[q % 2][:, fb*512:(fb+1)*512],
                                     in_=d["qpr_scr"][fb][q]).then_inc(spc.sem, 16)
                em("sync", f)
                for _ in range(NFB):
                    spc.inc()
                qpr_ld.append((spc.sem, spc.n))

            def emit_quarter(q):
                tc_ld = []
                for tc in range(NCH):
                    gi = q * NCH + tc
                    war = mm_all[-5] if len(mm_all) >= 5 else None
                    shtcc = shtc[gi % 3]
                    src = d["hT_scr"][tc]
                    def f(sy, src=src, tc=tc, gi=gi, q=q, war=war, shtcc=shtcc):
                        if tc < 3 and q == 0:
                            sy.wait_ge(shT[0].sem, hT_ready[0])
                            sy.wait_ge(shT[1].sem, hT_ready[1])
                        if war is not None:
                            sy.wait_ge(sem_t, war)
                        sy.dma_start(out=hT_tc[gi % 3][:].rearrange("p (fb n) -> p fb n", n=256),
                                     in_=src).then_inc(shtcc.sem, 16)
                    em("sync", f)
                    tc_ld.append((shtcc.sem, shtcc.inc()))
                    if tc == 0 and q + 1 < 4:
                        emit_qpr_load(q + 1)
                    for tb2 in range(2):
                        reg = (tc % 2) * 2048 + tb2 * 512
                        row = tc * 256 + tb2 * 128
                        tl_ = tc_ld[tc]
                        war = ev_all[-4] if len(ev_all) >= 4 else None
                        qld = qpr_ld[q]
                        def f(t, tc=tc, tb2=tb2, reg=reg, tl_=tl_, war=war, q=q, qld=qld):
                            t.wait_ge(*tl_)
                            t.wait_ge(*qld)
                            if war is not None:
                                t.wait_ge(sem_v, war)
                            for fb in range(NFB):
                                lhs = hT_tc[(q * NCH + tc) % 3][:, fb*256 + tb2*128: fb*256 + (tb2+1)*128]
                                ins = t.matmul(psum[:, reg:reg+512], lhs,
                                               qpr[q % 2][:, fb*512:(fb+1)*512],
                                               start=(fb == 0), stop=(fb == NFB-1))
                            ins.then_inc(sem_t, 1)
                        em("tensor", f)
                        mm_all.append(st.inc())
                        mmv = mm_all[-1]
                        war = ev_all[-2] if len(ev_all) >= 2 else None
                        sxec = sxe[len(ev_all) % 2]
                        srsc = srst[len(ev_all) % 2]
                        def f(sy, row=row, q=q, war=war, idx=len(ev_all), sxec=sxec):
                            if war is not None:
                                sy.wait_ge(sem_v, war)
                            sy.dma_start(out=x_e[idx % 2][:],
                                         in_=d["xp"][row:row+128, q*512:(q+1)*512]).then_inc(sxec.sem, 16)
                        em("sync", f)
                        xl = sxec.inc()
                        war2 = res_stores[-2] if len(res_stores) >= 2 else None
                        def f(v, reg=reg, xl=xl, mmv=mmv, war2=war2, idx=len(ev_all), sxec=sxec, srsc=srsc):
                            v.wait_ge(sem_t, mmv)
                            v.wait_ge(sxec.sem, xl)
                            if war2 is not None:
                                v.wait_ge(srsc.sem, war2)
                            v.scalar_tensor_tensor(res_e[idx % 2][:], psum[:, reg:reg+512],
                                                   tcols[:, 5:6], x_e[idx % 2][:],
                                                   ALU.mult, ALU.add).then_inc(sem_v, 1)
                        em("vector", f)
                        ev_all.append(sv.inc())
                        evv = ev_all[-1]
                        def f(g, row=row, q=q, evv=evv, idx=len(ev_all)-1, srsc=srsc):
                            g.wait_ge(sem_v, evv)
                            g.dma_start(out=d["res_scr"][row:row+128, q*512:(q+1)*512],
                                        in_=res_e[idx % 2][:]).then_inc(srsc.sem, 16)
                        em("gpsimd", f)
                        res_stores.append(srsc.inc())

            emit_qpr_load(0)
            for q in range(4):
                emit_quarter(q)
            res_ready = (srst[0].n, srst[1].n)
            def f(g):
                g.dma_start(out=wo_h01[:, 0:4*H].rearrange("p (ob n) -> p ob n", n=H),
                            in_=wo_sl[0]).then_inc(swo[0].sem, 16)
                g.dma_start(out=bob[:], in_=d["bor"]).then_inc(sbo.sem, 16)
                g.dma_start(out=wo_h01[:, 4*H:8*H].rearrange("p (ob n) -> p ob n", n=H),
                            in_=wo_sl[1]).then_inc(swo[1].sem, 16)
            em("gpsimd", f)
            swo[0].inc(); sbo.inc(); swo[1].inc()
        barrier()
        with ExitStack() as S5:
            E5 = S5.enter_context
            wo_h23 = E5(nc.sbuf_tensor("wo_h23", [128, 8 * H], BF16))
            oT_t = [E5(nc.sbuf_tensor(f"oT_t{i}", [128, NCB * 256], BF16)) for i in range(2)]
            res_f = [E5(nc.sbuf_tensor(f"res_f{i}", [128, H], F32)) for i in range(2)]
            out_f = [E5(nc.sbuf_tensor(f"out_f{i}", [128, H], F32)) for i in range(2)]

            def f(g):
                g.dma_start(out=wo_h23[:, 0:4*H].rearrange(
                    "p (ob n) -> p ob n", n=H), in_=wo_sl[2]).then_inc(swo[2].sem, 16)
                g.dma_start(out=wo_h23[:, 4*H:8*H].rearrange(
                    "p (ob n) -> p ob n", n=H), in_=wo_sl[3]).then_inc(swo[3].sem, 16)
            em("gpsimd", f)
            swo[2].inc(); swo[3].inc()

            f_mm, f_ev, out_stores = [], [], []
            for tc in range(NCH):
                src = d["o_scr"][:, tc*256:(tc+1)*256].rearrange("(ob p) n -> p ob n", p=128)
                war = f_mm[2*(tc-2)+1] if tc >= 2 else None
                sotc = sot[tc % 2]
                def f(sy, src=src, tc=tc, war=war, sotc=sotc):
                    if tc < 2:
                        sy.wait_ge(sost.sem, o_ready)
                    if war is not None:
                        sy.wait_ge(sem_t, war)
                    sy.dma_start(out=oT_t[tc % 2][:].rearrange("p (ob n) -> p ob n", n=256),
                                 in_=src).then_inc(sotc.sem, 16)
                em("sync", f)
                ol = (sotc.sem, sotc.inc())
                for tb2 in range(2):
                    reg = tb2 * 2048
                    row = tc * 256 + tb2 * 128
                    idx = 2*tc + tb2
                    war = f_ev[idx-2] if idx >= 2 else None
                    def f(t, tc=tc, tb2=tb2, reg=reg, ol=ol, war=war):
                        t.wait_ge(*ol)
                        if war is not None:
                            t.wait_ge(sem_v, war)
                        for ob in range(NCB):
                            if ob % 4 == 0:
                                t.wait_ge(swo[ob // 4].sem, 16)
                            if ob == NCB-1:
                                t.wait_ge(sbo.sem, 16)
                            lhs = oT_t[tc % 2][:, ob*256 + tb2*128: ob*256 + (tb2+1)*128]
                            wo_buf = wo_h01 if ob < 8 else wo_h23
                            for cc in range(4):
                                t.matmul(psum[:, reg + cc*512: reg + (cc+1)*512], lhs,
                                         wo_buf[:, (ob % 8)*H + cc*512: (ob % 8)*H + (cc+1)*512],
                                         start=(ob == 0), stop=False)
                        for cc in range(4):
                            ins = t.matmul(psum[:, reg + cc*512: reg + (cc+1)*512],
                                           ones_rb[0:1, :], bob[0:1, cc*512:(cc+1)*512],
                                           start=False, stop=True)
                        ins.then_inc(sem_t, 1)
                    em("tensor", f)
                    f_mm.append(st.inc())
                    war = f_ev[idx-2] if idx >= 2 else None
                    srfc = srf[idx % 2]
                    def f(sy, row=row, war=war, srfc=srfc, idx=idx):
                        if idx < 2:
                            sy.wait_ge(srst[0].sem, res_ready[0])
                            sy.wait_ge(srst[1].sem, res_ready[1])
                        if war is not None:
                            sy.wait_ge(sem_v, war)
                        sy.dma_start(out=res_f[(row//128) % 2][:],
                                     in_=d["res_scr"][row:row+128, :]).then_inc(srfc.sem, 16)
                    em("sync", f)
                    rl = (srfc.sem, srfc.inc())
                    mmv = f_mm[-1]
                    war2 = out_stores[idx-2] if idx >= 2 else None
                    sdoc = sdo[idx % 2]
                    def f(v, reg=reg, rl=rl, mmv=mmv, war2=war2, idx=idx, sdoc=sdoc):
                        v.wait_ge(sem_t, mmv)
                        v.wait_ge(*rl)
                        if war2 is not None:
                            v.wait_ge(sdoc.sem, war2)
                        v.tensor_tensor(out_f[idx % 2][:], psum[:, reg:reg+2048],
                                        res_f[idx % 2][:], ALU.add).then_inc(sem_v, 1)
                    em("vector", f)
                    f_ev.append(sv.inc())
                    evv = f_ev[-1]
                    def f(g, row=row, evv=evv, idx=idx, sdoc=sdoc):
                        g.wait_ge(sem_v, evv)
                        g.dma_start(out=d["out"][row:row+128, :],
                                    in_=out_f[idx % 2][:]).then_inc(sdoc.sem, 16)
                    em("gpsimd", f)
                    out_stores.append(sdoc.inc())

            fin = (sdo[0].n, sdo[1].n)
            def f(g, fin=fin):
                g.wait_ge(sdo[0].sem, fin[0])
                g.wait_ge(sdo[1].sem, fin[1])
            em("gpsimd", f)

        with nc.Block() as blk:
            @blk.sync
            def _(eng):
                for fn in streams["sync"]:
                    fn(eng)
            @blk.gpsimd
            def _(eng):
                for fn in streams["gpsimd"]:
                    fn(eng)
            @blk.vector
            def _(eng):
                for fn in streams["vector"]:
                    fn(eng)
            @blk.scalar
            def _(eng):
                for fn in streams["scalar"]:
                    fn(eng)
            @blk.tensor
            def _(eng):
                for fn in streams["tensor"]:
                    fn(eng)

    return nc


def kernel(**inputs):
    in_maps, gidx = _host_layout(inputs)
    nc = build_nc()
    res = run_bass_kernel_spmd(nc, in_maps, core_ids=list(range(8)))
    out = np.zeros((B, T, H), np.float32)
    for c in range(8):
        out[c // 2, gidx[c]] = np.asarray(res.results[c]["out"], np.float32)
    return out
